# version 14
# baseline (speedup 1.0000x reference)
import numpy as np

import concourse.bass as bass
import concourse.mybir as mybir
from concourse.bass_utils import run_bass_kernel_spmd

F32 = mybir.dt.float32
BF16 = mybir.dt.bfloat16
AF = mybir.ActivationFunctionType
ALU = mybir.AluOpType
AX = mybir.AxisListType

T = 2048
C = 1024
KC = 8
NCH = 8
CS = 256
E = 8
I_MOE = 1408
NCORE = 8

UNITS = []
for e in range(E):
    UNITS.append(("r", e, 0, 6))
    UNITS.append(("r", e, 6, 5))
for j, (ic0, nic) in enumerate([(0, 6), (6, 5), (11, 6), (17, 5)]):
    UNITS.append(("s", j, ic0, nic))

FFNW_OFFS = []
_off = 0
for (_k, _i, _ic0, _nic) in UNITS:
    gu_len = 8 * 2 * _nic * 128
    dn_len = _nic * 1024
    FFNW_OFFS.append((_off, gu_len, _off + gu_len, dn_len))
    _off += gu_len + dn_len
FFNW_TOT = _off

A_QKV = 0
A_WOUT = 24576
A_Q = 32768
A_K = 34816
A_V = 36864
A_ATT = 38912
A_EXPT = 40960
A_XSCR = 43008
A_RB = 45056
A_T = 46080
A_LCOL = 46592
A_ROWS = 47104
A_WET = 50416
A_SG = 52464
A_T2 = 54512
A_END = 55024
AF_GU = [0, 12288]
AF_DN = [24576, 30720]
AF_H = 36864
AF_SGB = [39936, 40448]
AF_UW = [40960, 41472]
AF_WB = 41984
AS_LOGP = 24576
AS_H1 = 28672
AS_E = 29696
AS_LAM = 30720

R_MEAN = 0
R_VAR = 256
R_RSTD = 512
R_MR = 768
R_RROW = 1024
R_MISC = 1536
R_NEGB = 1600
R_SMALL = 1602
R_STM = 1608
R_E8 = 1616
R_SC8 = 1624
R_GE = 1632
R_WTM = 1640
R_GWSUM = 1648
R_BL2N = 1650
ROWS_F32 = 1656


def build_nc(stop=None):
    nc = bass.Bass("TRN2", target_bir_lowering=False, debug=False)

    d_x = nc.dram_tensor("xt", [128, KC * T], F32, kind="ExternalInput")
    d_atnw = nc.dram_tensor("atnw", [128, 32768], F32, kind="ExternalInput")
    d_ffnw = nc.dram_tensor("ffnw", [128, FFNW_TOT], F32, kind="ExternalInput")
    d_gw = nc.dram_tensor("gw", [128, 64], F32, kind="ExternalInput")
    d_seg = nc.dram_tensor("seg", [128, 8], F32, kind="ExternalInput")
    d_bout = nc.dram_tensor("boutp", [128, 8], F32, kind="ExternalInput")
    d_sel = nc.dram_tensor("sel", [8, 1024], F32, kind="ExternalInput")
    d_idn = nc.dram_tensor("idn", [128, 128], F32, kind="ExternalInput")
    d_ones1 = nc.dram_tensor("ones1", [1, 128], F32, kind="ExternalInput")
    d_onesc = nc.dram_tensor("onesc", [128, 1], F32, kind="ExternalInput")
    d_onesf = nc.dram_tensor("onesf", [128, 128], F32, kind="ExternalInput")
    d_wl1t = nc.dram_tensor("wl1t", [1, 64], F32, kind="ExternalInput")
    d_bl1 = nc.dram_tensor("bl1c", [64, 1], F32, kind="ExternalInput")
    d_wl2t = nc.dram_tensor("wl2t", [64, 16], F32, kind="ExternalInput")
    d_bl2 = nc.dram_tensor("bl2c", [16, 1], F32, kind="ExternalInput")
    d_out = nc.dram_tensor("out", [128, KC * T], F32, kind="ExternalOutput")

    RES = nc.alloc_sbuf_tensor("res", [128, KC * T], F32)
    LNB = nc.alloc_sbuf_tensor("lnb", [128, KC * T], BF16)
    AR = nc.alloc_sbuf_tensor("arena", [128, A_END], BF16)
    IDN32 = nc.alloc_sbuf_tensor("idn32", [128, 128], F32)
    ONES32 = nc.alloc_sbuf_tensor("ones32", [1, 128], F32)
    ONESC32 = nc.alloc_sbuf_tensor("onesc32", [128, 1], F32)
    ONESB = nc.alloc_sbuf_tensor("onesb", [128, 1], BF16)
    ONESF = nc.alloc_sbuf_tensor("onesf_sb", [128, 128], BF16)
    ONESR = nc.alloc_sbuf_tensor("onesr", [1, 128], BF16)
    SELB = nc.alloc_sbuf_tensor("selb", [8, 1024], BF16)
    GW = nc.alloc_sbuf_tensor("gwsb", [128, 64], F32)
    SEGB = nc.alloc_sbuf_tensor("segb", [128, 8], BF16)
    BOUT = nc.alloc_sbuf_tensor("boutsb", [128, 8], F32)
    WL1T = nc.alloc_sbuf_tensor("wl1ts", [1, 64], F32)
    BL1 = nc.alloc_sbuf_tensor("bl1s", [64, 1], F32)
    WL2T = nc.alloc_sbuf_tensor("wl2ts", [64, 16], F32)
    BL2 = nc.alloc_sbuf_tensor("bl2s", [16, 1], F32)

    PB = [nc.alloc_psum_tensor(f"pb{i}", [128, 512], F32) for i in range(8)]

    def bfv(off, n, p0=0, p1=128):
        return AR[p0:p1, off:off + n]

    def f32v(off, nf, p0=0, p1=128):
        return AR[p0:p1, off:off + 2 * nf].bitcast(F32)

    def rcol(off, n=1, p=1):
        return AR[0:p, A_ROWS + 2 * off:A_ROWS + 2 * (off + n)].bitcast(F32)

    LCOL = f32v(A_LCOL, 256)
    WET = bfv(A_WET, 2048, 0, 8)
    SG = bfv(A_SG, 2048, 0, 1)
    WB = f32v(AF_WB, 2048)
    XSCR = bfv(A_XSCR, 2048)
    AT = f32v(A_T, 256)
    AT2 = f32v(A_T2, 256)
    NEGB = rcol(R_NEGB, 1, 128)
    LOGP = f32v(AS_LOGP, 2048, 0, 1)
    H1 = f32v(AS_H1, 512, 0, 64)
    EE = f32v(AS_E, 512, 0, 16)
    LAMR = f32v(AS_LAM, 512, 0, 16)

    ENGS = ("sync", "gpsimd", "tensor", "scalar", "vector")
    ops = {e: [] for e in ENGS}
    cnt = {s: 0 for s in ("sIN", "sX", "sX2", "sWQ", "sWK", "sWV", "sWO",
                          "sW", "sGU0", "sGU1", "sDN0", "sDN1", "sPO",
                          "sPE", "sSC", "sVE", "sOUT")}
    lastw = {e: {} for e in ENGS}
    sems = {}

    def emit(eng, fn, inc=None):
        if inc is None:
            ops[eng].append(fn)
            return None
        sem, k = inc
        cnt[sem] += k
        v = cnt[sem]

        def wrapped(Eng, fn=fn, sem=sem, k=k):
            fn(Eng).then_inc(sems[sem], k)
        ops[eng].append(wrapped)
        return (sem, v)

    def wait(eng, tok):
        if not tok:
            return
        sem, v = tok
        if v > cnt[sem]:
            raise RuntimeError(f"forward wait {eng} {sem} {v} > {cnt[sem]}")
        if lastw[eng].get(sem, 0) >= v:
            return
        lastw[eng][sem] = v
        ops[eng].append(lambda Eng, s=sem, v=v: Eng.wait_ge(sems[s], v))

    def mm(bank_ap, lhsT, rhs, start, stop, inc=False):
        fn = (lambda Eng, o=bank_ap, l=lhsT, r=rhs, st=start, sp=stop:
              Eng.matmul(o, l, r, start=st, stop=sp, skip_group_check=True))
        return emit("tensor", fn, inc=("sPE", 1) if inc else None)

    def tr(out_ap, in_ap, idn_ap, inc=True):
        fn = lambda Eng, o=out_ap, i=in_ap, d=idn_ap: Eng.transpose(o, i, d)
        return emit("tensor", fn, inc=("sPE", 1) if inc else None)

    def act(out_ap, in_ap, func, bias=0.0, scale=1.0, inc=True):
        fn = (lambda Eng, o=out_ap, i=in_ap, f=func, b=bias, s=scale:
              Eng.activation(o, i, f, bias=b, scale=s))
        return emit("scalar", fn, inc=("sSC", 1) if inc else None)

    def scopy(out_ap, in_ap, inc=True):
        fn = lambda Eng, o=out_ap, i=in_ap: Eng.copy(o, i)
        return emit("scalar", fn, inc=("sSC", 1) if inc else None)

    def sdrain():
        emit("scalar", lambda Eng: Eng.drain())

    def vtt(out_ap, a_ap, b_ap, op, inc=True):
        fn = (lambda Eng, o=out_ap, a=a_ap, b=b_ap, p=op:
              Eng.tensor_tensor(o, a, b, p))
        return emit("vector", fn, inc=("sVE", 1) if inc else None)

    def vts(out_ap, a_ap, scalar, op, inc=True):
        fn = (lambda Eng, o=out_ap, a=a_ap, s=scalar, p=op:
              Eng.tensor_single_scalar(o, a, s, p))
        return emit("vector", fn, inc=("sVE", 1) if inc else None)

    def vstt(out_ap, in0, scalar, in1, op0, op1, inc=True):
        fn = (lambda Eng, o=out_ap, a=in0, s=scalar, b=in1, p0=op0, p1=op1:
              Eng.scalar_tensor_tensor(o, a, s, b, p0, p1))
        return emit("vector", fn, inc=("sVE", 1) if inc else None)

    def vred(out_ap, in_ap, op, negate=False, inc=True):
        fn = (lambda Eng, o=out_ap, i=in_ap, p=op, n=negate:
              Eng.tensor_reduce(o, i, AX.X, p, negate=n))
        return emit("vector", fn, inc=("sVE", 1) if inc else None)

    def vrecip(out_ap, in_ap, inc=True):
        fn = lambda Eng, o=out_ap, i=in_ap: Eng.reciprocal(o, i)
        return emit("vector", fn, inc=("sVE", 1) if inc else None)

    def vdrain():
        emit("vector", lambda Eng: Eng.drain())

    def vcopy(out_ap, in_ap, inc=True):
        fn = lambda Eng, o=out_ap, i=in_ap: Eng.tensor_copy(o, i)
        return emit("vector", fn, inc=("sVE", 1) if inc else None)

    def pcopy(out_ap, in_ap, inc=True):
        fn = lambda Eng, o=out_ap, i=in_ap: Eng.tensor_copy(o, i)
        return emit("gpsimd", fn, inc=("sPO", 1) if inc else None)

    def dma(q, out_ap, in_ap, sem):
        fn = lambda Eng, o=out_ap, i=in_ap: Eng.dma_start(o, i)
        return emit(q, fn, inc=(sem, 16))

    bfree = [None] * 8

    def bank_wait(i, extra=()):
        wait("tensor", bfree[i])
        for t in extra:
            wait("tensor", t)

    resh = RES[:].rearrange("p (c t) -> p c t", c=KC)
    xh = d_x[:].rearrange("p (c t) -> p c t", c=KC)
    tok_xa = dma("sync", resh[:, :, 0:1024], xh[:, :, 0:1024], "sX")
    tok_xb = dma("sync", resh[:, :, 1024:2048], xh[:, :, 1024:2048], "sX2")

    def tok_x_for(ch):
        return tok_xa if ch < 4 else tok_xb
    tok_x = tok_xb
    tok_consts = None
    for (dst, src) in ((IDN32[:], d_idn[:]), (ONES32[:], d_ones1[:]),
                       (ONESC32[:], d_onesc[:]), (GW[:], d_gw[:]),
                       (BOUT[:], d_bout[:]), (WL1T[:], d_wl1t[:]),
                       (BL1[:], d_bl1[:]), (WL2T[:], d_wl2t[:]),
                       (BL2[:], d_bl2[:])):
        tok_consts = dma("sync", dst, src, "sIN")
    tok_wq = dma("gpsimd", bfv(A_QKV, 8192), d_atnw[:, 0:8192], "sWQ")
    tok_wk = dma("gpsimd", bfv(A_QKV + 8192, 8192), d_atnw[:, 8192:16384],
                 "sWK")
    tok_wv = dma("gpsimd", bfv(A_QKV + 16384, 8192), d_atnw[:, 16384:24576],
                 "sWV")
    dma("gpsimd", ONESB[:], d_onesc[:], "sW")
    dma("gpsimd", ONESR[:], d_ones1[:], "sW")
    dma("gpsimd", SELB[:], d_sel[:], "sW")
    dma("gpsimd", ONESF[:], d_onesf[:], "sW")
    tok_smalls = dma("gpsimd", SEGB[:], d_seg[:], "sW")
    tok_onesb = tok_onesr = tok_selb = tok_segb = tok_smalls

    def iota_fn(Eng, ap=LOGP):
        return Eng.iota(ap, [[1, T]], channel_multiplier=0,
                        allow_small_or_imprecise_dtypes=True)
    tok_iota = emit("gpsimd", iota_fn, inc=("sPO", 1))

    wait("vector", tok_iota)
    tok_ip1 = vts(LOGP, LOGP, 1.0, ALU.add)
    wait("scalar", tok_ip1)
    wait("scalar", tok_consts)
    tok_logp = act(LOGP, LOGP, AF.Ln)
    BL2N = rcol(R_BL2N, 1, 16)
    wait("vector", tok_consts)
    tok_bl2n = vts(BL2N, BL2[:], -1.0, ALU.mult)
    tok_relu = None
    tok_sig = None
    tok_lam_done = None
    for tt in range(4):
        wait("tensor", tok_logp)
        wait("tensor", tok_relu)
        t1 = mm(PB[0][0:64, :], WL1T[0:1, :],
                LOGP[0:1, tt * 512:(tt + 1) * 512], True, True, inc=True)
        wait("scalar", t1)
        tok_relu = act(H1, PB[0][0:64, :], AF.Relu, bias=BL1[:])
        wait("tensor", tok_relu)
        wait("tensor", tok_sig)
        t3 = mm(PB[1][0:16, :], WL2T[:], H1, True, True, inc=True)
        wait("scalar", t3)
        wait("scalar", tok_bl2n)
        tok_sig = act(EE, PB[1][0:16, :], AF.Exp, bias=BL2N, scale=-1.0)
        wait("vector", tok_sig)
        vts(EE, EE, 1.0, ALU.add, inc=False)
        vdrain()
        t6 = vrecip(LAMR, EE)
        for q in range(4):
            t0g = tt * 512 + q * 128
            n_i, kth = t0g // 256, (t0g // 128) % 2
            wait("tensor", t6)
            wait("tensor", tok_lam_done)
            t7 = tr(PB[2][0:128, 0:16], LAMR[0:16, q * 128:(q + 1) * 128],
                    IDN32[0:16, 0:16])
            wait("scalar", t7)
            co = n_i * 32 + kth * 16
            tok_lam_done = scopy(LCOL[:, co:co + 16], PB[2][0:128, 0:16])
    bfree[2] = tok_lam_done
    wait("gpsimd", tok_lam_done)
    tok_woutw = dma("gpsimd", bfv(A_WOUT, 8192), d_atnw[:, 24576:32768], "sWO")

    xscr_free = [None]
    rows_free = [None]
    b2_chain = [None]
    tok_chunk_res = [None] * NCH
    tok_qkv_done = [None] * NCH
    tok_st_done = [None] * NCH
    tok_avsum_done = [None] * NCH
    tok_wout_done = [None] * NCH
    tok_lnb_w = [None] * 16
    exp_tok4 = [[None, None], [None, None]]
    rtok2 = [None, None]
    sum_tok4 = [[None, None], [None, None]]
    avp_tok = [[None, None], [None, None]]
    evac_tok2 = [None, None]
    tok_rb_last = [None]
    tok_gating = dict(wet=None, sg=None)
    qk_bank = [0]

    def next_qk():
        b = qk_bank[0]
        qk_bank[0] = 1 - b
        return b

    ln_ctx = {}

    def ln_stats(p, res_tok):
        gating = p >= 8
        ch = p % 8
        t0 = ch * 256
        wait("gpsimd", tok_x_for(p % 8))
        wait("gpsimd", res_tok)
        if gating:
            wait("gpsimd", tok_qkv_done[ch])
        lnb3 = LNB[:].rearrange("p (c t) -> p c t", c=KC)[:, :, t0:t0 + 256]
        res3 = RES[:].rearrange("p (c t) -> p c t", c=KC)[:, :, t0:t0 + 256]
        tok_cast = pcopy(lnb3, res3)
        wait("scalar", tok_cast)
        wait("scalar", xscr_free[0])
        xs3 = XSCR[:].rearrange("p (c t) -> p c t", c=KC)
        tok_sq = act(xs3, lnb3, AF.Square)
        bs = next_qk()
        bank_wait(bs)
        wait("tensor", tok_cast)
        wait("tensor", tok_smalls)
        tok_sum = None
        for kc in range(KC):
            tok_sum = mm(PB[bs][:, 0:256], ONESF[:],
                         LNB[:, kc * T + t0:kc * T + t0 + 256],
                         kc == 0, kc == KC - 1, inc=(kc == KC - 1))
        wait("tensor", tok_sq)
        tok_sq2 = None
        for kc in range(KC):
            tok_sq2 = mm(PB[bs][:, 256:512], ONESF[:],
                         XSCR[:, kc * 256:(kc + 1) * 256],
                         kc == 0, kc == KC - 1, inc=(kc == KC - 1))
        xscr_free[0] = tok_sq2
        MEAN = rcol(R_MEAN, 256)
        VAR = rcol(R_VAR, 256)
        RSTD = rcol(R_RSTD, 256)
        MR = rcol(R_MR, 256)
        wait("vector", tok_sum)
        wait("vector", tok_sq2)
        wait("vector", rows_free[0])
        vts(MEAN, PB[bs][0:1, 0:256], 1.0 / C, ALU.mult, inc=False)
        vdrain()
        vtt(MR, MEAN, MEAN, ALU.mult, inc=False)
        vdrain()
        vstt(VAR, PB[bs][0:1, 256:512], 1.0 / C, MR, ALU.mult,
             ALU.subtract, inc=False)
        vdrain()
        tok_var = vts(VAR, VAR, 1e-5, ALU.add)
        bfree[bs] = tok_var
        wait("scalar", tok_var)
        act(VAR, VAR, AF.Ln, inc=False)
        sdrain()
        tok_rstd = act(RSTD, VAR, AF.Exp, scale=-0.5)
        wait("vector", tok_rstd)
        tok_mr = vtt(MR, MEAN, RSTD, ALU.mult)
        vdrain()
        ln_ctx[p] = (tok_var, tok_mr)

    def ln_apply(p):
        ch = p % 8
        t0 = ch * 256
        MR = rcol(R_MR, 256)
        RSTD = rcol(R_RSTD, 256)
        tok_var, tok_mr = ln_ctx[p]
        bb = next_qk()
        bank_wait(bb, (tok_var, tok_mr))
        mm(PB[bb][:, 0:256], ONES32[:], MR, True, True)
        tok_bc = mm(PB[bb][:, 256:512], ONES32[:], RSTD, True, True, inc=True)
        wait("vector", tok_bc)
        wait("vector", b2_chain[0])
        tok_ln = None
        ats = [AT, AT2]
        ttoks = [None] * KC
        lntoks = [None] * KC
        prev = None
        for kc in range(KC):
            sl = slice(kc * T + t0, kc * T + t0 + 256)
            if kc >= 2:
                wait("vector", lntoks[kc - 2])
            ttoks[kc] = vtt(ats[kc % 2], RES[:, sl], PB[bb][:, 256:512],
                            ALU.mult)
            if prev is not None:
                wait("vector", ttoks[prev])
                slp = slice(prev * T + t0, prev * T + t0 + 256)
                tok_ln = lntoks[prev] = vtt(LNB[:, slp], ats[prev % 2],
                                            PB[bb][:, 0:256], ALU.subtract)
            prev = kc
        wait("vector", ttoks[prev])
        slp = slice(prev * T + t0, prev * T + t0 + 256)
        tok_ln = lntoks[prev] = vtt(LNB[:, slp], ats[prev % 2],
                                    PB[bb][:, 0:256], ALU.subtract)
        b2_chain[0] = tok_ln
        bfree[bb] = tok_ln
        tok_lnb_w[p] = tok_ln
        if p < 8:
            rows_free[0] = tok_bc
        else:
            ln_ctx[p] = (tok_bc, tok_ln)

    gate_ctx = {}

    def ln_gate1(p):
        ch = p % 8
        t0 = ch * 256
        MEAN = rcol(R_MEAN, 256)
        RSTD = rcol(R_RSTD, 256)
        g1 = next_qk()
        bank_wait(g1)
        tok_lg = None
        for kc in range(KC):
            tok_lg = mm(PB[g1][0:8, 0:256], GW[:, kc * 8:(kc + 1) * 8],
                        RES[:, kc * T + t0:kc * T + t0 + 256],
                        kc == 0, kc == KC - 1, inc=(kc == KC - 1))
        g2 = next_qk()
        bank_wait(g2)
        mm(PB[g2][0:8, 0:256], ONES32[0:1, 0:8], MEAN, True, True)
        tok_m8 = mm(PB[g2][0:8, 256:512], ONES32[0:1, 0:8], RSTD, True, True,
                    inc=True)
        LG1 = rcol(R_VAR, 256, 8)
        LG2 = rcol(R_MR, 256, 8)
        GWS = rcol(R_GWSUM, 1, 8)
        wait("vector", tok_m8)
        wait("vector", tok_lg)
        vts(LG1, PB[g2][0:8, 0:256], GWS, ALU.mult, inc=False)
        vdrain()
        vstt(LG2, PB[g1][0:8, 0:256], 1.0, LG1, ALU.mult, ALU.subtract,
             inc=False)
        vdrain()
        tok_l3 = vtt(LG1, LG2, PB[g2][0:8, 256:512], ALU.mult)
        vdrain()
        bfree[g2] = tok_l3
        bfree[g1] = tok_l3
        gate_ctx[p] = dict(tok_l3=tok_l3, tok_wet=None)

    def ln_gate_j(p, j):
        ch = p % 8
        t0 = ch * 256
        gc = gate_ctx[p]
        g1 = next_qk()
        bank_wait(g1)
        LG1 = rcol(R_VAR, 256, 8)
        STM = rcol(R_STM, 8, 128)
        E8 = rcol(R_E8, 8, 128)
        SC8 = rcol(R_SC8, 8, 128)
        GE8 = rcol(R_GE, 8, 128)
        WTM = rcol(R_WTM, 8, 128)
        NM = rcol(R_SMALL, 1, 128)
        SM = rcol(R_SMALL + 1, 1, 128)
        RC = rcol(R_SMALL + 2, 1, 128)
        M1 = rcol(R_SMALL + 3, 1, 128)
        M2 = rcol(R_SMALL + 4, 1, 128)
        wait("tensor", gc["tok_l3"])
        if gc["tok_wet"]:
            wait("tensor", gc["tok_wet"])
        tok_tr = tr(PB[g1][0:128, 0:8], LG1[0:8, j * 128:(j + 1) * 128],
                    IDN32[0:8, 0:8])
        wait("scalar", tok_tr)
        tok_stm = scopy(STM, PB[g1][0:128, 0:8])
        wait("vector", tok_stm)
        tok_nm = vred(NM, STM, ALU.max, negate=True)
        wait("scalar", tok_nm)
        tok_e8 = act(E8, STM, AF.Exp, bias=NM)
        wait("vector", tok_e8)
        vred(SM, E8, ALU.add, inc=False)
        vdrain()
        vrecip(RC, SM, inc=False)
        vdrain()
        vts(SC8, E8, RC, ALU.mult, inc=False)
        vdrain()
        vred(M1, SC8, ALU.max, inc=False)
        vdrain()
        vts(GE8, SC8, M1, ALU.is_ge, inc=False)
        vdrain()
        vstt(GE8, GE8, -1e30, SC8, ALU.mult, ALU.add, inc=False)
        vdrain()
        vred(M2, GE8, ALU.max, inc=False)
        vdrain()
        vts(GE8, SC8, M2, ALU.is_ge, inc=False)
        vdrain()
        tok_w = vtt(WTM, SC8, GE8, ALU.mult)
        wait("tensor", tok_w)
        tok_trb = tr(PB[g1][0:8, 0:128], WTM, IDN32[:])
        wait("scalar", tok_trb)
        gc["tok_wet"] = scopy(WET[0:8, t0 + j * 128:t0 + j * 128 + 128],
                              PB[g1][0:8, 0:128])
        bfree[g1] = gc["tok_wet"]

    def ln_gate_sg(p):
        ch = p % 8
        t0 = ch * 256
        gc = gate_ctx[p]
        g1 = next_qk()
        bank_wait(g1)
        tok_bc, tok_ln = ln_ctx[p]
        tok_gating["wet"] = gc["tok_wet"]
        wait("tensor", gc["tok_wet"])
        wait("tensor", tok_ln)
        wait("tensor", tok_segb)
        tok_sg = None
        for kc in range(KC):
            tok_sg = mm(PB[g1][0:1, 0:256], SEGB[:, kc:kc + 1],
                        LNB[:, kc * T + t0:kc * T + t0 + 256],
                        kc == 0, kc == KC - 1, inc=(kc == KC - 1))
        SGR = rcol(R_VAR, 256)
        wait("scalar", tok_sg)
        tok_sge = act(SGR, PB[g1][0:1, 0:256], AF.Exp, scale=-1.0)
        bfree[g1] = tok_sge
        wait("vector", tok_sge)
        vts(SGR, SGR, 1.0, ALU.add, inc=False)
        vdrain()
        SGR2 = rcol(R_RSTD, 256)
        vrecip(SGR2, SGR, inc=False)
        vdrain()
        tok_sgf = vcopy(SG[0:1, t0:t0 + 256], SGR2)
        tok_gating["sg"] = tok_sgf
        rows_free[0] = tok_sgf

    def attn_qkv_bound(n):
        t0 = n * CS
        wait("tensor", tok_lnb_w[n])
        QMX = rcol(R_MISC, 16)
        tok_qk_evac = None
        tok_v_evac = None
        tok_bnd_mm = None
        for which, dst_off, scale, poff in ((0, A_Q, 0.125, 0),
                                            (1, A_K, 1.0, 4)):
            ev_toks = [None] * 8
            sq_toks = [None] * 8

            def emit_sq(oc, which=which, dst_off=dst_off, ev_toks=None,
                        sq_toks=None):
                wait("scalar", ev_toks[oc])
                wait("scalar", xscr_free[0])
                sq_toks[oc] = act(XSCR[:, oc * 256:(oc + 1) * 256],
                                  bfv(dst_off, 2048)[:, oc * 256:
                                                     (oc + 1) * 256],
                                  AF.Square)

            wait("tensor", tok_wq if which == 0 else tok_wk)
            for oc in range(8):
                b = next_qk()
                bank_wait(b)
                tok = None
                for kc in range(KC):
                    o = A_QKV + which * 8192 + kc * 1024 + oc * 128
                    tok = mm(PB[b][:, 0:256], AR[:, o:o + 128],
                             LNB[:, kc * T + t0:kc * T + t0 + 256],
                             kc == 0, kc == KC - 1, inc=(kc == KC - 1))
                wait("scalar", tok)
                if n > 0:
                    wait("scalar", tok_st_done[n - 1])
                dst = bfv(dst_off, 2048)[:, oc * 256:(oc + 1) * 256]
                if scale != 1.0:
                    ev_toks[oc] = act(dst, PB[b][:, 0:256], AF.Copy,
                                      scale=scale)
                else:
                    ev_toks[oc] = scopy(dst, PB[b][:, 0:256])
                tok_qk_evac = ev_toks[oc]
                bfree[b] = ev_toks[oc]
                if oc >= 1:
                    emit_sq(oc - 1, ev_toks=ev_toks, sq_toks=sq_toks)
            emit_sq(7, ev_toks=ev_toks, sq_toks=sq_toks)
            for tt4 in range(4):
                b = next_qk()
                bank_wait(b, (sq_toks[2 * tt4], sq_toks[2 * tt4 + 1]))
                tok_bnd_mm = mm(PB[b][:, 0:512], ONESF[:],
                                XSCR[:, tt4 * 512:(tt4 + 1) * 512],
                                True, True, inc=True)
                wait("vector", tok_bnd_mm)
                tok_r = vred(QMX[0:1, poff + tt4:poff + tt4 + 1],
                             PB[b][0:1, 0:512], ALU.max)
                bfree[b] = tok_r
            xscr_free[0] = tok_bnd_mm
        wait("tensor", tok_wv)
        for th in range(2):
            for dv2 in range(2):
                b = next_qk()
                bank_wait(b)
                tok = None
                for kc in range(KC):
                    o = A_QKV + 16384 + kc * 1024 + dv2 * 512
                    tok = mm(PB[b][:, 0:512],
                             LNB[:, kc * T + t0 + th * 128:
                                 kc * T + t0 + th * 128 + 128],
                             AR[:, o:o + 512],
                             kc == 0, kc == KC - 1, inc=(kc == KC - 1))
                wait("scalar", tok)
                if n > 0:
                    wait("scalar", tok_avsum_done[n - 1])
                tok_v_evac = scopy(
                    bfv(A_V, 2048)[:, th * 1024 + dv2 * 512:
                                   th * 1024 + dv2 * 512 + 512],
                    PB[b][:, 0:512])
                bfree[b] = tok_v_evac
        tok_qkv_done[n] = tok
        vdrain()
        vred(QMX[0:1, 8:9], QMX[0:1, 0:4], ALU.max, inc=False)
        vred(QMX[0:1, 9:10], QMX[0:1, 4:8], ALU.max, inc=False)
        vdrain()
        tok_p = vtt(QMX[0:1, 10:11], QMX[0:1, 8:9], QMX[0:1, 9:10], ALU.mult)
        wait("scalar", tok_p)
        act(QMX[0:1, 11:12], QMX[0:1, 10:11], AF.Ln, inc=False)
        sdrain()
        tok_b = act(QMX[0:1, 12:13], QMX[0:1, 11:12], AF.Exp, scale=0.5)
        wait("vector", tok_b)
        tok_nb = vts(QMX[0:1, 13:14], QMX[0:1, 12:13], -1.0, ALU.mult)
        b = next_qk()
        bank_wait(b, (tok_nb,))
        tok = mm(PB[b][:, 0:1], ONES32[:], QMX[0:1, 13:14], True, True,
                 inc=True)
        wait("scalar", tok)
        for pp in (0, 1):
            for hp in (0, 1):
                wait("scalar", exp_tok4[pp][hp])
        tok_negb = scopy(NEGB, PB[b][:, 0:1])
        sdrain()
        bfree[b] = tok_negb
        return tok_qk_evac, tok_v_evac, tok_negb

    QN = bfv(A_Q, 2048)
    KN = bfv(A_K, 2048)
    VN = bfv(A_V, 2048)
    ATT = bfv(A_ATT, 2048)

    def st_pair(n, c, tok_qk, tok_negb):
        pp = c % 2
        for hp in (0, 1):
            h = 2 * c + hp
            bi = 2 + pp * 2 + hp
            bank_wait(bi, (exp_tok4[pp][hp], tok_qk))
            lh = KN[hp * 64:hp * 64 + 64, c * 256:c * 256 + 128]
            lh2 = KN[hp * 64:hp * 64 + 64, c * 256 + 128:c * 256 + 256]
            rh = QN[hp * 64:hp * 64 + 64, c * 256:(c + 1) * 256]
            mm(PB[bi][:, 0:256], lh, rh, True, True)
            tok_st = mm(PB[bi][:, 256:512], lh2, rh, True, True, inc=True)
            tok_st_done[n] = tok_st
            wait("scalar", tok_st)
            wait("scalar", tok_negb)
            if avp_tok[pp][hp]:
                wait("scalar", avp_tok[pp][hp])
            ex = bfv(A_EXPT + pp * 1024 + hp * 512, 512)
            tok_exp = None
            for kth in range(2):
                lc = LCOL[:, n * 32 + kth * 16 + h:n * 32 + kth * 16 + h + 1]
                tok_exp = act(ex[:, kth * 256:(kth + 1) * 256],
                              PB[bi][:, kth * 256:(kth + 1) * 256],
                              AF.Exp, bias=NEGB, scale=lc)
            exp_tok4[pp][hp] = tok_exp

    def sums_av_pair(n, p, tok_v):
        pp = p % 2
        bs0 = next_qk()
        bank_wait(bs0)
        for hp in (0, 1):
            ex = bfv(A_EXPT + pp * 1024 + hp * 512, 512)
            wait("tensor", exp_tok4[pp][hp])
            co = hp * 256
            mm(PB[bs0][:, co:co + 256], ONESF[:], ex[:, 0:256], True, False)
            sum_tok4[pp][hp] = mm(PB[bs0][:, co:co + 256], ONESF[:],
                                  ex[:, 256:512], False, True, inc=True)
            bav = 6 + pp
            bank_wait(bav, (evac_tok2[pp], tok_v))
            mm(PB[bav][:, co:co + 256], VN[:, p * 128:p * 128 + 128],
               ex[:, 0:256], True, False)
            tok_av = mm(PB[bav][:, co:co + 256],
                        VN[:, 1024 + p * 128:1024 + p * 128 + 128],
                        ex[:, 256:512], False, True, inc=True)
            avp_tok[pp][hp] = tok_av
        tok_avsum_done[n] = tok_av
        RR = rcol(R_RROW, 512)
        RRB = AR[0:1, A_ROWS + 2 * R_RROW:A_ROWS + 2 * R_RROW + 512]
        wait("scalar", sum_tok4[pp][1])
        wait("scalar", tok_rb_last[0])
        tok_lnr = act(RR, PB[bs0][0:1, 0:512], AF.Ln)
        bfree[bs0] = tok_lnr
        wait("scalar", tok_lnr)
        rtok2[pp] = act(RRB, RR, AF.Exp, scale=-1.0)

    def rb_evac_pair(n, p):
        pp = p % 2
        RRB = AR[0:1, A_ROWS + 2 * R_RROW:A_ROWS + 2 * R_RROW + 512]
        brb = next_qk()
        bank_wait(brb, (rtok2[pp],))
        mm(PB[brb][0:64, 0:256], ONESR[0:1, 0:64], RRB[0:1, 0:256],
           True, True)
        tok_rb = mm(PB[brb][64:128, 0:256], ONESR[0:1, 0:64],
                    RRB[0:1, 256:512], True, True, inc=True)
        tok_rb_last[0] = tok_rb
        wait("scalar", tok_rb)
        wait("scalar", evac_tok2[pp])
        rbs = f32v(A_RB + pp * 512, 256)
        tok_rbc = scopy(rbs, PB[brb][:, 0:256])
        bfree[brb] = tok_rbc
        wait("vector", tok_rbc)
        wait("vector", avp_tok[pp][0])
        wait("vector", avp_tok[pp][1])
        if n > 0:
            wait("vector", tok_wout_done[n - 1])
        bav = 6 + pp
        vtt(ATT[0:64, p * 256:(p + 1) * 256], PB[bav][0:64, 0:256],
            rbs[0:64, :], ALU.mult, inc=False)
        tok_ev = vtt(ATT[64:128, p * 256:(p + 1) * 256],
                     PB[bav][64:128, 256:512], rbs[64:128, :], ALU.mult)
        evac_tok2[pp] = tok_ev
        return tok_ev

    def attn_heads(n, tok_negb, tok_qk, tok_v, filler=None):
        tok_ev = None
        for c in range(10):
            if 2 <= c <= 9:
                tok_ev = rb_evac_pair(n, c - 2)
            if c <= 7:
                st_pair(n, c, tok_qk, tok_negb)
            if 1 <= c <= 8:
                sums_av_pair(n, c - 1, tok_v)
            if filler is not None:
                next(filler, None)
        return tok_ev

    def attn_wout(n, tok_ev_last):
        t0 = n * CS
        wait("tensor", tok_ev_last)
        wait("tensor", tok_woutw)
        tok_evw = None
        for oc in range(8):
            b = next_qk()
            bank_wait(b)
            tok = None
            for kc in range(KC):
                o = A_WOUT + kc * 1024 + oc * 128
                tok = mm(PB[b][:, 0:256], AR[:, o:o + 128],
                         ATT[:, kc * 256:(kc + 1) * 256],
                         kc == 0, kc == KC - 1, inc=(kc == KC - 1))
            tok_wout_done[n] = tok
            wait("vector", tok)
            wait("vector", tok_xa if n < 4 else tok_xb)
            sl = slice(oc * T + t0, oc * T + t0 + 256)
            tok_evw = vstt(RES[:, sl], PB[b][:, 0:256], BOUT[:, oc:oc + 1],
                           RES[:, sl], ALU.add, ALU.add)
            bfree[b] = tok_evw
        tok_chunk_res[n] = tok_evw

    def emit_gwsum():
        b = next_qk()
        bank_wait(b)
        wait("tensor", tok_consts)
        tok = None
        for kc in range(KC):
            tok = mm(PB[b][0:8, 0:1], GW[:, kc * 8:(kc + 1) * 8], ONESC32[:],
                     kc == 0, kc == KC - 1, inc=(kc == KC - 1))
        wait("vector", tok)
        tok_g = vcopy(rcol(R_GWSUM, 1, 8), PB[b][0:8, 0:1])
        vdrain()
        bfree[b] = tok_g

    emit_gwsum()
    for p in range(4):
        ln_stats(p, tok_xa)
        ln_apply(p)

    def ln_full(p, res_tok):
        ln_stats(p, res_tok)
        ln_apply(p)
        if p >= 8:
            ln_gate1(p)
            ln_gate_j(p, 0)
            ln_gate_j(p, 1)
            ln_gate_sg(p)

    for n in range(NCH):
        tok_qk, tok_v, tok_negb = attn_qkv_bound(n)
        tok_ev_last = attn_heads(n, tok_negb, tok_qk, tok_v)
        attn_wout(n, tok_ev_last)
        if n + 4 <= 7:
            ln_full(n + 4, tok_xb)
        if n >= 1:
            ln_full(8 + n - 1, tok_chunk_res[n - 1])
    ln_stats(15, tok_chunk_res[7])
    ln_apply(15)
    ln_gate1(15)
    ln_gate_j(15, 0)
    ln_gate_j(15, 1)
    ln_gate_sg(15)

    for pp in (0, 1):
        for hp in (0, 1):
            wait("tensor", exp_tok4[pp][hp])
        wait("tensor", rtok2[pp])
        wait("tensor", evac_tok2[pp])

    if stop == "attn":
        dbg_outs = {
            "d_res": (RES[:], [128, KC * T], F32, tok_chunk_res[7]),
            "d_lnb": (LNB[:], [128, KC * T], BF16, tok_lnb_w[15]),
            "d_lcol": (LCOL, [128, 256], F32, tok_lnb_w[15]),
            "d_q7": (bfv(A_Q, 2048), [128, 2048], BF16, tok_lnb_w[15]),
            "d_k7": (bfv(A_K, 2048), [128, 2048], BF16, tok_lnb_w[15]),
            "d_v7": (bfv(A_V, 2048), [128, 2048], BF16, tok_lnb_w[15]),
            "d_att7": (bfv(A_ATT, 2048), [128, 2048], BF16, tok_lnb_w[15]),
            "d_wet": (WET, [8, 2048], BF16, tok_gating["wet"]),
            "d_sg": (SG, [1, 2048], BF16, tok_gating["sg"]),
            "d_negb": (NEGB, [128, 1], F32, tok_lnb_w[15]),
        }
        ndma = 0
        for name, (ap, shape, dt_, tok) in dbg_outs.items():
            dd = nc.dram_tensor(name, shape, dt_, kind="ExternalOutput")
            wait("sync", tok)
            wait("sync", tok_gating["sg"])
            dma("sync", dd[:], ap, "sOUT")
            ndma += 1
        wait("sync", ("sOUT", 16 * ndma))
        return _materialize(nc, ops, sems)

    dma_gu_tok = [None] * len(UNITS)
    dma_dn_tok = [None] * len(UNITS)

    def emit_unit_dma(u, pe_release_tok):
        slot = u % 2
        gu_off, gu_len, dn_off, dn_len = FFNW_OFFS[u]
        wait("gpsimd", pe_release_tok)
        dma_gu_tok[u] = dma("gpsimd", bfv(AF_GU[slot], gu_len),
                            d_ffnw[:, gu_off:gu_off + gu_len], f"sGU{slot}")
        dma_dn_tok[u] = dma("gpsimd", bfv(AF_DN[slot], dn_len),
                            d_ffnw[:, dn_off:dn_off + dn_len], f"sDN{slot}")

    wait("gpsimd", tok_qkv_done[7])
    wait("gpsimd", tok_wout_done[7])
    wait("gpsimd", tok_st_done[7])
    emit_unit_dma(0, None)
    emit_unit_dma(1, None)

    unit_release = [None] * len(UNITS)
    tok_wb = None
    tok_wb_prev_uw = None
    h_free = None
    sgb_read = [None, None]
    gu_bank_tok = [None, None, None, None]
    dn_bank_tok = [None, None, None, None]
    last_evac = None
    tok_wb_region_ok = (xscr_free[0], evac_tok2[0], evac_tok2[1], b2_chain[0])
    first_wb = [True]

    for u, (kind, idx, ic0, nic) in enumerate(UNITS):
        slot = u % 2
        gu_base = AF_GU[slot]
        dn_base = AF_DN[slot]
        build_wb = (kind == "r" and ic0 == 0) or (kind == "s" and idx == 0)
        if build_wb:
            for tt in range(4):
                b4 = 4 + tt % 4
                bank_wait(b4, (dn_bank_tok[tt % 4], tok_gating["wet"],
                               tok_gating["sg"], tok_selb, tok_onesr))
                if kind == "r":
                    tok = mm(PB[b4][:, 0:512],
                             SELB[0:8, idx * 128:idx * 128 + 128],
                             WET[0:8, tt * 512:(tt + 1) * 512], True, True,
                             inc=True)
                else:
                    tok = mm(PB[b4][:, 0:512], ONESR[:],
                             SG[0:1, tt * 512:(tt + 1) * 512], True, True,
                             inc=True)
                wait("scalar", tok)
                if first_wb[0]:
                    for tk in tok_wb_region_ok:
                        wait("scalar", tk)
                    first_wb[0] = False
                wait("scalar", tok_wb_prev_uw)
                tok_wb = scopy(WB[:, tt * 512:(tt + 1) * 512],
                               PB[b4][:, 0:512])
                dn_bank_tok[tt % 4] = tok_wb
        tok_uw_last = None
        for tt in range(4):
            wait("tensor", tok_lnb_w[8 + min(2 * tt + 1, 7)])
            htok_list = []
            silu_toks = [None] * nic
            uw_toks = [None] * nic

            def emit_h(icl):
                par = icl % 2
                wait("vector", silu_toks[icl])
                wait("vector", uw_toks[icl])
                wait("vector", h_free)
                tok_h = vtt(bfv(AF_H + icl * 512, 512), bfv(AF_SGB[par], 512),
                            bfv(AF_UW[par], 512), ALU.mult)
                sgb_read[par] = tok_h
                htok_list.append(tok_h)

            for icl in range(nic):
                par = icl % 2
                bg, bu = par, 2 + par
                bank_wait(bg, (gu_bank_tok[bg],))
                wait("tensor", dma_gu_tok[u])
                tok_g = None
                for kc in range(KC):
                    o = gu_base + kc * (2 * nic * 128) + icl * 128
                    tok_g = mm(PB[bg][:, 0:512], AR[:, o:o + 128],
                               LNB[:, kc * T + tt * 512:
                                   kc * T + (tt + 1) * 512],
                               kc == 0, kc == KC - 1, inc=(kc == KC - 1))
                bank_wait(bu, (gu_bank_tok[bu],))
                tok_u = None
                for kc in range(KC):
                    o = gu_base + kc * (2 * nic * 128) + (nic + icl) * 128
                    tok_u = mm(PB[bu][:, 0:512], AR[:, o:o + 128],
                               LNB[:, kc * T + tt * 512:
                                   kc * T + (tt + 1) * 512],
                               kc == 0, kc == KC - 1, inc=(kc == KC - 1))
                wait("scalar", tok_g)
                wait("scalar", sgb_read[par])
                sgb = bfv(AF_SGB[par], 512)
                tok_silu = act(sgb, PB[bg][:, 0:512], AF.Silu)
                silu_toks[icl] = tok_silu
                gu_bank_tok[bg] = tok_silu
                wait("vector", tok_u)
                wait("vector", tok_wb)
                if icl >= 2:
                    wait("vector", htok_list[icl - 2])
                uwb = bfv(AF_UW[par], 512)
                tok_uw = vtt(uwb, PB[bu][:, 0:512],
                             WB[:, tt * 512:(tt + 1) * 512], ALU.mult)
                gu_bank_tok[bu] = tok_uw
                uw_toks[icl] = tok_uw
                tok_uw_last = tok_uw
                if icl >= 1:
                    emit_h(icl - 1)
            emit_h(nic - 1)
            dtoks = [None] * 4
            for oh in range(2):
                for oc4 in range(4):
                    oc = oh * 4 + oc4
                    bd = 4 + oc4
                    bank_wait(bd, (dn_bank_tok[oc4], dma_dn_tok[u]))
                    tok_d = None
                    for icl in range(nic):
                        wait("tensor", htok_list[icl])
                        o = dn_base + icl * 1024 + oc * 128
                        tok_d = mm(PB[bd][:, 0:512], AR[:, o:o + 128],
                                   bfv(AF_H + icl * 512, 512),
                                   icl == 0, icl == nic - 1,
                                   inc=(icl == nic - 1))
                    dtoks[oc4] = tok_d
                for oc4 in range(4):
                    oc = oh * 4 + oc4
                    wait("vector", dtoks[oc4])
                    sl = slice(oc * T + tt * 512, oc * T + (tt + 1) * 512)
                    last_evac = vtt(RES[:, sl], PB[4 + oc4][:, 0:512],
                                    RES[:, sl], ALU.add)
                    dn_bank_tok[oc4] = last_evac
            h_free = dtoks[3]
        unit_release[u] = dtoks[3]
        tok_wb_prev_uw = tok_uw_last
        if u + 2 < len(UNITS):
            emit_unit_dma(u + 2, unit_release[u])

    wait("sync", last_evac)
    tok_out = dma("sync", d_out[:], RES[:], "sOUT")
    wait("sync", tok_out)
    return _materialize(nc, ops, sems)


def _materialize(nc, ops, sems):
    names = ("sIN", "sX", "sX2", "sWQ", "sWK", "sWV", "sWO", "sW", "sGU0",
             "sGU1", "sDN0", "sDN1", "sPO", "sPE", "sSC", "sVE", "sOUT")
    import contextlib
    with contextlib.ExitStack() as stack:
        for nm in names:
            sems[nm] = stack.enter_context(nc.semaphore(nm))
        block = stack.enter_context(nc.Block())

        @block.sync
        def _(eng):
            for fn in ops["sync"]:
                fn(eng)

        @block.gpsimd
        def _(eng):
            for fn in ops["gpsimd"]:
                fn(eng)

        @block.tensor
        def _(eng):
            for fn in ops["tensor"]:
                fn(eng)

        @block.scalar
        def _(eng):
            for fn in ops["scalar"]:
                fn(eng)

        @block.vector
        def _(eng):
            for fn in ops["vector"]:
                fn(eng)

    return nc


_NC_CACHE = {}


def _get_nc():
    if "nc" not in _NC_CACHE:
        _NC_CACHE["nc"] = build_nc()
    return _NC_CACHE["nc"]


def _chunked_fm(a):
    k, o = a.shape
    return np.ascontiguousarray(
        a.reshape(k // 128, 128, o).transpose(1, 0, 2).reshape(128, -1))


def prep_in_maps(inputs):
    g = {k: np.asarray(v, dtype=np.float32) for k, v in inputs.items()}
    x = g["L0_attn_x"]
    wqkv = g["L0_attn_Wqkv"]
    wout = g["L0_attn_Wout"]
    bout = g["L0_attn_bout"]
    wl1 = g["L0_attn_Wl1"]
    bl1 = g["L0_attn_bl1"]
    wl2 = g["L0_attn_Wl2"]
    bl2 = g["L0_attn_bl2"]
    gate_w = g["L0_ffn_gate_w"]
    egu = g["L0_ffn_experts_gate_up"]
    edn = g["L0_ffn_experts_down"]
    sgw = g["L0_ffn_shared_gate_w"]
    suw = g["L0_ffn_shared_up_w"]
    sdw = g["L0_ffn_shared_down_w"]
    segw = g["L0_ffn_shared_expert_gate_w"]

    wqkvT = wqkv.T
    atnw = np.concatenate([_chunked_fm(wqkvT[:, 0:1024]),
                           _chunked_fm(wqkvT[:, 1024:2048]),
                           _chunked_fm(wqkvT[:, 2048:3072]),
                           _chunked_fm(wout.T)], axis=1)

    ffnw = np.empty((128, FFNW_TOT), dtype=np.float32)
    for u, (kind, idx, ic0, nic) in enumerate(UNITS):
        gu_off, gu_len, dn_off, dn_len = FFNW_OFFS[u]
        if kind == "r":
            guT = egu[idx].T
            gcol = guT[:, ic0 * 128:(ic0 + nic) * 128]
            ucol = guT[:, I_MOE + ic0 * 128:I_MOE + (ic0 + nic) * 128]
            drow = edn[idx].T[ic0 * 128:(ic0 + nic) * 128, :]
        else:
            gcol = sgw.T[:, ic0 * 128:(ic0 + nic) * 128]
            ucol = suw.T[:, ic0 * 128:(ic0 + nic) * 128]
            drow = sdw.T[ic0 * 128:(ic0 + nic) * 128, :]
        blk = np.concatenate([gcol, ucol], axis=1)
        ffnw[:, gu_off:gu_off + gu_len] = _chunked_fm(blk)
        ffnw[:, dn_off:dn_off + dn_len] = np.ascontiguousarray(
            drow.reshape(nic, 128, 1024).transpose(1, 0, 2).reshape(128, -1))

    gw = _chunked_fm(gate_w.T)
    seg = np.ascontiguousarray(segw.reshape(KC, 128).T)
    boutp = np.ascontiguousarray(bout.reshape(KC, 128).T)
    sel = np.zeros((8, 1024), np.float32)
    for e in range(8):
        sel[e, e * 128:(e + 1) * 128] = 1.0
    shared = dict(
        atnw=atnw, ffnw=ffnw, gw=gw, seg=seg, boutp=boutp, sel=sel,
        idn=np.eye(128, dtype=np.float32),
        ones1=np.ones((1, 128), np.float32),
        onesc=np.ones((128, 1), np.float32),
        onesf=np.ones((128, 128), np.float32),
        wl1t=np.ascontiguousarray(wl1.T),
        bl1c=np.ascontiguousarray(bl1.reshape(64, 1)),
        wl2t=np.ascontiguousarray(wl2.T),
        bl2c=np.ascontiguousarray(bl2.reshape(16, 1)))

    in_maps = []
    for b in range(NCORE):
        m = dict(shared)
        m["xt"] = np.ascontiguousarray(
            x[b].T.reshape(KC, 128, T).transpose(1, 0, 2).reshape(128, -1))
        in_maps.append(m)
    return in_maps


def kernel(**inputs):
    in_maps = prep_in_maps(inputs)
    nc = _get_nc()
    res = run_bass_kernel_spmd(nc, in_maps, core_ids=list(range(NCORE)))
    outs = []
    for b in range(NCORE):
        o = np.asarray(res.results[b]["out"])
        y = o.reshape(128, KC, T).transpose(1, 0, 2).reshape(C, T).T
        outs.append(np.ascontiguousarray(y))
    return np.stack(outs, axis=0).astype(np.float32)


# version 15
# speedup vs baseline: 1.1101x; 1.1101x over previous
import numpy as np

import concourse.bass as bass
import concourse.mybir as mybir
from concourse.bass_utils import run_bass_kernel_spmd

F32 = mybir.dt.float32
BF16 = mybir.dt.bfloat16
AF = mybir.ActivationFunctionType
ALU = mybir.AluOpType
AX = mybir.AxisListType

T = 2048
C = 1024
KC = 8
NCH = 8
CS = 256
E = 8
I_MOE = 1408
NCORE = 8

UNITS = []
for e in range(E):
    UNITS.append(("r", e, 0, 6))
    UNITS.append(("r", e, 6, 5))
for j, (ic0, nic) in enumerate([(0, 6), (6, 5), (11, 6), (17, 5)]):
    UNITS.append(("s", j, ic0, nic))

FFNW_OFFS = []
_off = 0
for (_k, _i, _ic0, _nic) in UNITS:
    gu_len = 8 * 2 * _nic * 128
    dn_len = _nic * 1024
    FFNW_OFFS.append((_off, gu_len, _off + gu_len, dn_len))
    _off += gu_len + dn_len
FFNW_TOT = _off

A_QKV = 0
A_WOUT = 24576
A_Q = 32768
A_K = 34816
A_V = 36864
A_ATT = 38912
A_EXPT = 40960
A_XSCR = 43008
A_RB = 45056
A_T = 46080
A_LCOL = 46592
A_ROWS = 47104
A_WET = 50416
A_SG = 52464
A_T2 = 54512
A_END = 55024
AF_GU = [0, 12288]
AF_DN = [24576, 30720]
AF_H = 36864
AF_SGB = [39936, 40448]
AF_UW = [40960, 41472]
AF_WB = 41984
AS_LOGP = 24576
AS_H1 = 28672
AS_E = 29696
AS_LAM = 30720

R_MEAN = 0
R_VAR = 256
R_RSTD = 512
R_MR = 768
R_RROW = 1024
R_MISC = 1536
R_NEGB = 1600
R_SMALL = 1602
R_STM = 1608
R_E8 = 1616
R_SC8 = 1624
R_GE = 1632
R_WTM = 1640
R_GWSUM = 1648
R_BL2N = 1650
ROWS_F32 = 1656


def build_nc(stop=None):
    nc = bass.Bass("TRN2", target_bir_lowering=False, debug=False)

    d_x = nc.dram_tensor("xt", [128, KC * T], F32, kind="ExternalInput")
    d_atnw = nc.dram_tensor("atnw", [128, 32768], F32, kind="ExternalInput")
    d_ffnw = nc.dram_tensor("ffnw", [128, FFNW_TOT], F32, kind="ExternalInput")
    d_gw = nc.dram_tensor("gw", [128, 64], F32, kind="ExternalInput")
    d_seg = nc.dram_tensor("seg", [128, 8], F32, kind="ExternalInput")
    d_bout = nc.dram_tensor("boutp", [128, 8], F32, kind="ExternalInput")
    d_sel = nc.dram_tensor("sel", [8, 1024], F32, kind="ExternalInput")
    d_idn = nc.dram_tensor("idn", [128, 128], F32, kind="ExternalInput")
    d_ones1 = nc.dram_tensor("ones1", [1, 128], F32, kind="ExternalInput")
    d_onesc = nc.dram_tensor("onesc", [128, 1], F32, kind="ExternalInput")
    d_onesf = nc.dram_tensor("onesf", [128, 128], F32, kind="ExternalInput")
    d_wl1t = nc.dram_tensor("wl1t", [1, 64], F32, kind="ExternalInput")
    d_bl1 = nc.dram_tensor("bl1c", [64, 1], F32, kind="ExternalInput")
    d_wl2t = nc.dram_tensor("wl2t", [64, 16], F32, kind="ExternalInput")
    d_bl2 = nc.dram_tensor("bl2c", [16, 1], F32, kind="ExternalInput")
    d_out = nc.dram_tensor("out", [128, KC * T], F32, kind="ExternalOutput")

    RES = nc.alloc_sbuf_tensor("res", [128, KC * T], F32)
    LNB = nc.alloc_sbuf_tensor("lnb", [128, KC * T], BF16)
    AR = nc.alloc_sbuf_tensor("arena", [128, A_END], BF16)
    IDN32 = nc.alloc_sbuf_tensor("idn32", [128, 128], F32)
    ONES32 = nc.alloc_sbuf_tensor("ones32", [1, 128], F32)
    ONESC32 = nc.alloc_sbuf_tensor("onesc32", [128, 1], F32)
    ONESB = nc.alloc_sbuf_tensor("onesb", [128, 1], BF16)
    ONESF = nc.alloc_sbuf_tensor("onesf_sb", [128, 128], BF16)
    ONESR = nc.alloc_sbuf_tensor("onesr", [1, 128], BF16)
    SELB = nc.alloc_sbuf_tensor("selb", [8, 1024], BF16)
    GW = nc.alloc_sbuf_tensor("gwsb", [128, 64], F32)
    SEGB = nc.alloc_sbuf_tensor("segb", [128, 8], BF16)
    BOUT = nc.alloc_sbuf_tensor("boutsb", [128, 8], F32)
    WL1T = nc.alloc_sbuf_tensor("wl1ts", [1, 64], F32)
    BL1 = nc.alloc_sbuf_tensor("bl1s", [64, 1], F32)
    WL2T = nc.alloc_sbuf_tensor("wl2ts", [64, 16], F32)
    BL2 = nc.alloc_sbuf_tensor("bl2s", [16, 1], F32)

    PB = [nc.alloc_psum_tensor(f"pb{i}", [128, 512], F32) for i in range(8)]

    def bfv(off, n, p0=0, p1=128):
        return AR[p0:p1, off:off + n]

    def f32v(off, nf, p0=0, p1=128):
        return AR[p0:p1, off:off + 2 * nf].bitcast(F32)

    def rcol(off, n=1, p=1):
        return AR[0:p, A_ROWS + 2 * off:A_ROWS + 2 * (off + n)].bitcast(F32)

    LCOL = f32v(A_LCOL, 256)
    WET = bfv(A_WET, 2048, 0, 8)
    SG = bfv(A_SG, 2048, 0, 1)
    WB = f32v(AF_WB, 2048)
    XSCR = bfv(A_XSCR, 2048)
    AT = f32v(A_T, 256)
    AT2 = f32v(A_T2, 256)
    NEGB = rcol(R_NEGB, 1, 128)
    LOGP = f32v(AS_LOGP, 2048, 0, 1)
    H1 = f32v(AS_H1, 512, 0, 64)
    EE = f32v(AS_E, 512, 0, 16)
    LAMR = f32v(AS_LAM, 512, 0, 16)

    ENGS = ("sync", "gpsimd", "tensor", "scalar", "vector")
    ops = {e: [] for e in ENGS}
    cnt = {s: 0 for s in ("sIN", "sX", "sX2", "sWQ", "sWK", "sWV", "sWO",
                          "sW", "sGU0", "sGU1", "sDN0", "sDN1", "sPO",
                          "sPE", "sSC", "sVE", "sOUT")}
    lastw = {e: {} for e in ENGS}
    sems = {}

    def emit(eng, fn, inc=None):
        if inc is None:
            ops[eng].append(fn)
            return None
        sem, k = inc
        cnt[sem] += k
        v = cnt[sem]

        def wrapped(Eng, fn=fn, sem=sem, k=k):
            fn(Eng).then_inc(sems[sem], k)
        ops[eng].append(wrapped)
        return (sem, v)

    def wait(eng, tok):
        if not tok:
            return
        sem, v = tok
        if v > cnt[sem]:
            raise RuntimeError(f"forward wait {eng} {sem} {v} > {cnt[sem]}")
        if lastw[eng].get(sem, 0) >= v:
            return
        lastw[eng][sem] = v
        ops[eng].append(lambda Eng, s=sem, v=v: Eng.wait_ge(sems[s], v))

    def mm(bank_ap, lhsT, rhs, start, stop, inc=False):
        fn = (lambda Eng, o=bank_ap, l=lhsT, r=rhs, st=start, sp=stop:
              Eng.matmul(o, l, r, start=st, stop=sp, skip_group_check=True))
        return emit("tensor", fn, inc=("sPE", 1) if inc else None)

    def tr(out_ap, in_ap, idn_ap, inc=True):
        fn = lambda Eng, o=out_ap, i=in_ap, d=idn_ap: Eng.transpose(o, i, d)
        return emit("tensor", fn, inc=("sPE", 1) if inc else None)

    def act(out_ap, in_ap, func, bias=0.0, scale=1.0, inc=True):
        fn = (lambda Eng, o=out_ap, i=in_ap, f=func, b=bias, s=scale:
              Eng.activation(o, i, f, bias=b, scale=s))
        return emit("scalar", fn, inc=("sSC", 1) if inc else None)

    def scopy(out_ap, in_ap, inc=True):
        fn = lambda Eng, o=out_ap, i=in_ap: Eng.copy(o, i)
        return emit("scalar", fn, inc=("sSC", 1) if inc else None)

    def sdrain():
        emit("scalar", lambda Eng: Eng.drain())

    def vtt(out_ap, a_ap, b_ap, op, inc=True):
        fn = (lambda Eng, o=out_ap, a=a_ap, b=b_ap, p=op:
              Eng.tensor_tensor(o, a, b, p))
        return emit("vector", fn, inc=("sVE", 1) if inc else None)

    def vts(out_ap, a_ap, scalar, op, inc=True):
        fn = (lambda Eng, o=out_ap, a=a_ap, s=scalar, p=op:
              Eng.tensor_single_scalar(o, a, s, p))
        return emit("vector", fn, inc=("sVE", 1) if inc else None)

    def vstt(out_ap, in0, scalar, in1, op0, op1, inc=True):
        fn = (lambda Eng, o=out_ap, a=in0, s=scalar, b=in1, p0=op0, p1=op1:
              Eng.scalar_tensor_tensor(o, a, s, b, p0, p1))
        return emit("vector", fn, inc=("sVE", 1) if inc else None)

    def vred(out_ap, in_ap, op, negate=False, inc=True):
        fn = (lambda Eng, o=out_ap, i=in_ap, p=op, n=negate:
              Eng.tensor_reduce(o, i, AX.X, p, negate=n))
        return emit("vector", fn, inc=("sVE", 1) if inc else None)

    def vrecip(out_ap, in_ap, inc=True):
        fn = lambda Eng, o=out_ap, i=in_ap: Eng.reciprocal(o, i)
        return emit("vector", fn, inc=("sVE", 1) if inc else None)

    def vdrain():
        emit("vector", lambda Eng: Eng.drain())

    def vcopy(out_ap, in_ap, inc=True):
        fn = lambda Eng, o=out_ap, i=in_ap: Eng.tensor_copy(o, i)
        return emit("vector", fn, inc=("sVE", 1) if inc else None)

    def pcopy(out_ap, in_ap, inc=True):
        fn = lambda Eng, o=out_ap, i=in_ap: Eng.tensor_copy(o, i)
        return emit("gpsimd", fn, inc=("sPO", 1) if inc else None)

    def dma(q, out_ap, in_ap, sem):
        fn = lambda Eng, o=out_ap, i=in_ap: Eng.dma_start(o, i)
        return emit(q, fn, inc=(sem, 16))

    bfree = [None] * 8

    def bank_wait(i, extra=()):
        wait("tensor", bfree[i])
        for t in extra:
            wait("tensor", t)

    resh = RES[:].rearrange("p (c t) -> p c t", c=KC)
    xh = d_x[:].rearrange("p (c t) -> p c t", c=KC)
    tok_xa = dma("sync", resh[:, :, 0:1024], xh[:, :, 0:1024], "sX")
    tok_xb = dma("sync", resh[:, :, 1024:2048], xh[:, :, 1024:2048], "sX2")

    def tok_x_for(ch):
        return tok_xa if ch < 4 else tok_xb
    tok_x = tok_xb
    tok_consts = None
    for (dst, src) in ((IDN32[:], d_idn[:]), (ONES32[:], d_ones1[:]),
                       (ONESC32[:], d_onesc[:]), (GW[:], d_gw[:]),
                       (BOUT[:], d_bout[:]), (WL1T[:], d_wl1t[:]),
                       (BL1[:], d_bl1[:]), (WL2T[:], d_wl2t[:]),
                       (BL2[:], d_bl2[:])):
        tok_consts = dma("sync", dst, src, "sIN")
    tok_wq = dma("gpsimd", bfv(A_QKV, 8192), d_atnw[:, 0:8192], "sWQ")
    tok_wk = dma("gpsimd", bfv(A_QKV + 8192, 8192), d_atnw[:, 8192:16384],
                 "sWK")
    tok_wv = dma("gpsimd", bfv(A_QKV + 16384, 8192), d_atnw[:, 16384:24576],
                 "sWV")
    dma("gpsimd", ONESB[:], d_onesc[:], "sW")
    dma("gpsimd", ONESR[:], d_ones1[:], "sW")
    dma("gpsimd", SELB[:], d_sel[:], "sW")
    dma("gpsimd", ONESF[:], d_onesf[:], "sW")
    tok_smalls = dma("gpsimd", SEGB[:], d_seg[:], "sW")
    tok_onesb = tok_onesr = tok_selb = tok_segb = tok_smalls

    def iota_fn(Eng, ap=LOGP):
        return Eng.iota(ap, [[1, T]], channel_multiplier=0,
                        allow_small_or_imprecise_dtypes=True)
    tok_iota = emit("gpsimd", iota_fn, inc=("sPO", 1))

    wait("vector", tok_iota)
    tok_ip1 = vts(LOGP, LOGP, 1.0, ALU.add)
    wait("scalar", tok_ip1)
    wait("scalar", tok_consts)
    tok_logp = act(LOGP, LOGP, AF.Ln)
    BL2N = rcol(R_BL2N, 1, 16)
    wait("vector", tok_consts)
    tok_bl2n = vts(BL2N, BL2[:], -1.0, ALU.mult)
    tok_relu = None
    tok_sig = None
    tok_lam_done = None
    for tt in range(4):
        wait("tensor", tok_logp)
        wait("tensor", tok_relu)
        t1 = mm(PB[0][0:64, :], WL1T[0:1, :],
                LOGP[0:1, tt * 512:(tt + 1) * 512], True, True, inc=True)
        wait("scalar", t1)
        tok_relu = act(H1, PB[0][0:64, :], AF.Relu, bias=BL1[:])
        wait("tensor", tok_relu)
        wait("tensor", tok_sig)
        t3 = mm(PB[1][0:16, :], WL2T[:], H1, True, True, inc=True)
        wait("scalar", t3)
        wait("scalar", tok_bl2n)
        tok_sig = act(EE, PB[1][0:16, :], AF.Exp, bias=BL2N, scale=-1.0)
        wait("vector", tok_sig)
        vts(EE, EE, 1.0, ALU.add, inc=False)
        vdrain()
        t6 = vrecip(LAMR, EE)
        for q in range(4):
            t0g = tt * 512 + q * 128
            n_i, kth = t0g // 256, (t0g // 128) % 2
            wait("tensor", t6)
            wait("tensor", tok_lam_done)
            t7 = tr(PB[2][0:128, 0:16], LAMR[0:16, q * 128:(q + 1) * 128],
                    IDN32[0:16, 0:16])
            wait("scalar", t7)
            co = n_i * 32 + kth * 16
            tok_lam_done = scopy(LCOL[:, co:co + 16], PB[2][0:128, 0:16])
    bfree[2] = tok_lam_done
    wait("gpsimd", tok_lam_done)
    tok_woutw = dma("gpsimd", bfv(A_WOUT, 8192), d_atnw[:, 24576:32768], "sWO")

    xscr_free = [None]
    rows_free = [None]
    b2_chain = [None]
    tok_chunk_res = [None] * NCH
    tok_qkv_done = [None] * NCH
    tok_st_done = [None] * NCH
    tok_avsum_done = [None] * NCH
    tok_wout_done = [None] * NCH
    tok_lnb_w = [None] * 16
    exp_tok4 = [[None, None], [None, None]]
    rtok2 = [None, None]
    sum_tok4 = [[None, None], [None, None]]
    avp_tok = [[None, None], [None, None]]
    evac_tok2 = [None, None]
    tok_rb_last = [None]
    tok_gating = dict(wet=None, sg=None)
    qk_bank = [0]

    def next_qk():
        b = qk_bank[0]
        qk_bank[0] = 1 - b
        return b

    ln_ctx = {}

    def ln_stats(p, res_tok):
        gating = p >= 8
        ch = p % 8
        t0 = ch * 256
        wait("gpsimd", tok_x_for(p % 8))
        wait("gpsimd", res_tok)
        if gating:
            wait("gpsimd", tok_qkv_done[ch])
        lnb3 = LNB[:].rearrange("p (c t) -> p c t", c=KC)[:, :, t0:t0 + 256]
        res3 = RES[:].rearrange("p (c t) -> p c t", c=KC)[:, :, t0:t0 + 256]
        tok_cast = pcopy(lnb3, res3)
        wait("scalar", tok_cast)
        wait("scalar", xscr_free[0])
        xs3 = XSCR[:].rearrange("p (c t) -> p c t", c=KC)
        tok_sq = act(xs3, lnb3, AF.Square)
        bs = next_qk()
        bank_wait(bs)
        wait("tensor", tok_cast)
        wait("tensor", tok_smalls)
        tok_sum = None
        for kc in range(KC):
            tok_sum = mm(PB[bs][:, 0:256], ONESF[:],
                         LNB[:, kc * T + t0:kc * T + t0 + 256],
                         kc == 0, kc == KC - 1, inc=(kc == KC - 1))
        wait("tensor", tok_sq)
        tok_sq2 = None
        for kc in range(KC):
            tok_sq2 = mm(PB[bs][:, 256:512], ONESF[:],
                         XSCR[:, kc * 256:(kc + 1) * 256],
                         kc == 0, kc == KC - 1, inc=(kc == KC - 1))
        xscr_free[0] = tok_sq2
        MEAN = rcol(R_MEAN, 256)
        VAR = rcol(R_VAR, 256)
        RSTD = rcol(R_RSTD, 256)
        MR = rcol(R_MR, 256)
        wait("vector", tok_sum)
        wait("vector", tok_sq2)
        wait("vector", rows_free[0])
        vts(MEAN, PB[bs][0:1, 0:256], 1.0 / C, ALU.mult, inc=False)
        vdrain()
        vtt(MR, MEAN, MEAN, ALU.mult, inc=False)
        vdrain()
        vstt(VAR, PB[bs][0:1, 256:512], 1.0 / C, MR, ALU.mult,
             ALU.subtract, inc=False)
        vdrain()
        tok_var = vts(VAR, VAR, 1e-5, ALU.add)
        bfree[bs] = tok_var
        wait("scalar", tok_var)
        act(VAR, VAR, AF.Ln, inc=False)
        sdrain()
        tok_rstd = act(RSTD, VAR, AF.Exp, scale=-0.5)
        wait("vector", tok_rstd)
        tok_mr = vtt(MR, MEAN, RSTD, ALU.mult)
        vdrain()
        ln_ctx[p] = (tok_var, tok_mr)

    def ln_apply(p):
        ch = p % 8
        t0 = ch * 256
        MR = rcol(R_MR, 256)
        RSTD = rcol(R_RSTD, 256)
        tok_var, tok_mr = ln_ctx[p]
        bb = next_qk()
        bank_wait(bb, (tok_var, tok_mr))
        mm(PB[bb][:, 0:256], ONES32[:], MR, True, True)
        tok_bc = mm(PB[bb][:, 256:512], ONES32[:], RSTD, True, True, inc=True)
        wait("vector", tok_bc)
        wait("vector", b2_chain[0])
        tok_ln = None
        ats = [AT, AT2]
        ttoks = [None] * KC
        lntoks = [None] * KC
        prev = None
        for kc in range(KC):
            sl = slice(kc * T + t0, kc * T + t0 + 256)
            if kc >= 2:
                wait("vector", lntoks[kc - 2])
            ttoks[kc] = vtt(ats[kc % 2], RES[:, sl], PB[bb][:, 256:512],
                            ALU.mult)
            if prev is not None:
                wait("vector", ttoks[prev])
                slp = slice(prev * T + t0, prev * T + t0 + 256)
                tok_ln = lntoks[prev] = vtt(LNB[:, slp], ats[prev % 2],
                                            PB[bb][:, 0:256], ALU.subtract)
            prev = kc
        wait("vector", ttoks[prev])
        slp = slice(prev * T + t0, prev * T + t0 + 256)
        tok_ln = lntoks[prev] = vtt(LNB[:, slp], ats[prev % 2],
                                    PB[bb][:, 0:256], ALU.subtract)
        b2_chain[0] = tok_ln
        bfree[bb] = tok_ln
        tok_lnb_w[p] = tok_ln
        if p < 8:
            rows_free[0] = tok_bc
        else:
            ln_ctx[p] = (tok_bc, tok_ln)

    gate_ctx = {}

    def ln_gate1(p):
        ch = p % 8
        t0 = ch * 256
        MEAN = rcol(R_MEAN, 256)
        RSTD = rcol(R_RSTD, 256)
        g1 = next_qk()
        bank_wait(g1)
        tok_lg = None
        for kc in range(KC):
            tok_lg = mm(PB[g1][0:8, 0:256], GW[:, kc * 8:(kc + 1) * 8],
                        RES[:, kc * T + t0:kc * T + t0 + 256],
                        kc == 0, kc == KC - 1, inc=(kc == KC - 1))
        g2 = next_qk()
        bank_wait(g2)
        mm(PB[g2][0:8, 0:256], ONES32[0:1, 0:8], MEAN, True, True)
        tok_m8 = mm(PB[g2][0:8, 256:512], ONES32[0:1, 0:8], RSTD, True, True,
                    inc=True)
        LG1 = rcol(R_VAR, 256, 8)
        LG2 = rcol(R_MR, 256, 8)
        GWS = rcol(R_GWSUM, 1, 8)
        wait("vector", tok_m8)
        wait("vector", tok_lg)
        vts(LG1, PB[g2][0:8, 0:256], GWS, ALU.mult, inc=False)
        vdrain()
        vstt(LG2, PB[g1][0:8, 0:256], 1.0, LG1, ALU.mult, ALU.subtract,
             inc=False)
        vdrain()
        tok_l3 = vtt(LG1, LG2, PB[g2][0:8, 256:512], ALU.mult)
        vdrain()
        bfree[g2] = tok_l3
        bfree[g1] = tok_l3
        gate_ctx[p] = dict(tok_l3=tok_l3, tok_wet=None)

    def ln_gate_j(p, j):
        ch = p % 8
        t0 = ch * 256
        gc = gate_ctx[p]
        g1 = next_qk()
        bank_wait(g1)
        LG1 = rcol(R_VAR, 256, 8)
        STM = rcol(R_STM, 8, 128)
        E8 = rcol(R_E8, 8, 128)
        SC8 = rcol(R_SC8, 8, 128)
        GE8 = rcol(R_GE, 8, 128)
        WTM = rcol(R_WTM, 8, 128)
        NM = rcol(R_SMALL, 1, 128)
        SM = rcol(R_SMALL + 1, 1, 128)
        RC = rcol(R_SMALL + 2, 1, 128)
        M1 = rcol(R_SMALL + 3, 1, 128)
        M2 = rcol(R_SMALL + 4, 1, 128)
        wait("tensor", gc["tok_l3"])
        if gc["tok_wet"]:
            wait("tensor", gc["tok_wet"])
        tok_tr = tr(PB[g1][0:128, 0:8], LG1[0:8, j * 128:(j + 1) * 128],
                    IDN32[0:8, 0:8])
        wait("scalar", tok_tr)
        tok_stm = scopy(STM, PB[g1][0:128, 0:8])
        wait("vector", tok_stm)
        tok_nm = vred(NM, STM, ALU.max, negate=True)
        wait("scalar", tok_nm)
        tok_e8 = act(E8, STM, AF.Exp, bias=NM)
        wait("vector", tok_e8)
        vred(SM, E8, ALU.add, inc=False)
        vdrain()
        vrecip(RC, SM, inc=False)
        vdrain()
        vts(SC8, E8, RC, ALU.mult, inc=False)
        vdrain()
        vred(M1, SC8, ALU.max, inc=False)
        vdrain()
        vts(GE8, SC8, M1, ALU.is_ge, inc=False)
        vdrain()
        vstt(GE8, GE8, -1e30, SC8, ALU.mult, ALU.add, inc=False)
        vdrain()
        vred(M2, GE8, ALU.max, inc=False)
        vdrain()
        vts(GE8, SC8, M2, ALU.is_ge, inc=False)
        vdrain()
        tok_w = vtt(WTM, SC8, GE8, ALU.mult)
        wait("tensor", tok_w)
        tok_trb = tr(PB[g1][0:8, 0:128], WTM, IDN32[:])
        wait("scalar", tok_trb)
        gc["tok_wet"] = scopy(WET[0:8, t0 + j * 128:t0 + j * 128 + 128],
                              PB[g1][0:8, 0:128])
        bfree[g1] = gc["tok_wet"]

    def ln_gate_sg(p):
        ch = p % 8
        t0 = ch * 256
        gc = gate_ctx[p]
        g1 = next_qk()
        bank_wait(g1)
        tok_bc, tok_ln = ln_ctx[p]
        tok_gating["wet"] = gc["tok_wet"]
        wait("tensor", gc["tok_wet"])
        wait("tensor", tok_ln)
        wait("tensor", tok_segb)
        tok_sg = None
        for kc in range(KC):
            tok_sg = mm(PB[g1][0:1, 0:256], SEGB[:, kc:kc + 1],
                        LNB[:, kc * T + t0:kc * T + t0 + 256],
                        kc == 0, kc == KC - 1, inc=(kc == KC - 1))
        SGR = rcol(R_VAR, 256)
        wait("scalar", tok_sg)
        tok_sge = act(SGR, PB[g1][0:1, 0:256], AF.Exp, scale=-1.0)
        bfree[g1] = tok_sge
        wait("vector", tok_sge)
        vts(SGR, SGR, 1.0, ALU.add, inc=False)
        vdrain()
        SGR2 = rcol(R_RSTD, 256)
        vrecip(SGR2, SGR, inc=False)
        vdrain()
        tok_sgf = vcopy(SG[0:1, t0:t0 + 256], SGR2)
        tok_gating["sg"] = tok_sgf
        rows_free[0] = tok_sgf

    def attn_qkv_bound(n):
        t0 = n * CS
        wait("tensor", tok_lnb_w[n])
        QMX = rcol(R_MISC, 16)
        tok_qk_evac = None
        tok_v_evac = None
        tok_bnd_mm = None
        for which, dst_off, scale, poff in ((0, A_Q, 0.125, 0),
                                            (1, A_K, 1.0, 4)):
            ev_toks = [None] * 8
            sq_toks = [None] * 8

            def emit_sq(oc, which=which, dst_off=dst_off, ev_toks=None,
                        sq_toks=None):
                wait("scalar", ev_toks[oc])
                wait("scalar", xscr_free[0])
                sq_toks[oc] = act(XSCR[:, oc * 256:(oc + 1) * 256],
                                  bfv(dst_off, 2048)[:, oc * 256:
                                                     (oc + 1) * 256],
                                  AF.Square)

            wait("tensor", tok_wq if which == 0 else tok_wk)
            for oc in range(8):
                b = next_qk()
                bank_wait(b)
                tok = None
                for kc in range(KC):
                    o = A_QKV + which * 8192 + kc * 1024 + oc * 128
                    tok = mm(PB[b][:, 0:256], AR[:, o:o + 128],
                             LNB[:, kc * T + t0:kc * T + t0 + 256],
                             kc == 0, kc == KC - 1, inc=(kc == KC - 1))
                wait("scalar", tok)
                if n > 0:
                    wait("scalar", tok_st_done[n - 1])
                dst = bfv(dst_off, 2048)[:, oc * 256:(oc + 1) * 256]
                if scale != 1.0:
                    ev_toks[oc] = act(dst, PB[b][:, 0:256], AF.Copy,
                                      scale=scale)
                else:
                    ev_toks[oc] = scopy(dst, PB[b][:, 0:256])
                tok_qk_evac = ev_toks[oc]
                bfree[b] = ev_toks[oc]
                if oc >= 1:
                    emit_sq(oc - 1, ev_toks=ev_toks, sq_toks=sq_toks)
            emit_sq(7, ev_toks=ev_toks, sq_toks=sq_toks)
            for tt4 in range(4):
                b = next_qk()
                bank_wait(b, (sq_toks[2 * tt4], sq_toks[2 * tt4 + 1]))
                tok_bnd_mm = mm(PB[b][:, 0:512], ONESF[:],
                                XSCR[:, tt4 * 512:(tt4 + 1) * 512],
                                True, True, inc=True)
                wait("vector", tok_bnd_mm)
                tok_r = vred(QMX[0:1, poff + tt4:poff + tt4 + 1],
                             PB[b][0:1, 0:512], ALU.max)
                bfree[b] = tok_r
            xscr_free[0] = tok_bnd_mm
        wait("tensor", tok_wv)
        for th in range(2):
            for dv2 in range(2):
                b = next_qk()
                bank_wait(b)
                tok = None
                for kc in range(KC):
                    o = A_QKV + 16384 + kc * 1024 + dv2 * 512
                    tok = mm(PB[b][:, 0:512],
                             LNB[:, kc * T + t0 + th * 128:
                                 kc * T + t0 + th * 128 + 128],
                             AR[:, o:o + 512],
                             kc == 0, kc == KC - 1, inc=(kc == KC - 1))
                wait("scalar", tok)
                if n > 0:
                    wait("scalar", tok_avsum_done[n - 1])
                tok_v_evac = scopy(
                    bfv(A_V, 2048)[:, th * 1024 + dv2 * 512:
                                   th * 1024 + dv2 * 512 + 512],
                    PB[b][:, 0:512])
                bfree[b] = tok_v_evac
        tok_qkv_done[n] = tok
        vdrain()
        vred(QMX[0:1, 8:9], QMX[0:1, 0:4], ALU.max, inc=False)
        vred(QMX[0:1, 9:10], QMX[0:1, 4:8], ALU.max, inc=False)
        vdrain()
        tok_p = vtt(QMX[0:1, 10:11], QMX[0:1, 8:9], QMX[0:1, 9:10], ALU.mult)
        wait("scalar", tok_p)
        act(QMX[0:1, 11:12], QMX[0:1, 10:11], AF.Ln, inc=False)
        sdrain()
        tok_b = act(QMX[0:1, 12:13], QMX[0:1, 11:12], AF.Exp, scale=0.5)
        wait("vector", tok_b)
        tok_nb = vts(QMX[0:1, 13:14], QMX[0:1, 12:13], -1.0, ALU.mult)
        b = next_qk()
        bank_wait(b, (tok_nb,))
        tok = mm(PB[b][:, 0:1], ONES32[:], QMX[0:1, 13:14], True, True,
                 inc=True)
        wait("scalar", tok)
        for pp in (0, 1):
            for hp in (0, 1):
                wait("scalar", exp_tok4[pp][hp])
        tok_negb = scopy(NEGB, PB[b][:, 0:1])
        sdrain()
        bfree[b] = tok_negb
        return tok_qk_evac, tok_v_evac, tok_negb

    QN = bfv(A_Q, 2048)
    KN = bfv(A_K, 2048)
    VN = bfv(A_V, 2048)
    ATT = bfv(A_ATT, 2048)

    def st_pair(n, c, tok_qk, tok_negb):
        pp = c % 2
        for hp in (0, 1):
            h = 2 * c + hp
            bi = 2 + pp * 2 + hp
            bank_wait(bi, (exp_tok4[pp][hp], tok_qk))
            lh = KN[hp * 64:hp * 64 + 64, c * 256:c * 256 + 128]
            lh2 = KN[hp * 64:hp * 64 + 64, c * 256 + 128:c * 256 + 256]
            rh = QN[hp * 64:hp * 64 + 64, c * 256:(c + 1) * 256]
            mm(PB[bi][:, 0:256], lh, rh, True, True)
            tok_st = mm(PB[bi][:, 256:512], lh2, rh, True, True, inc=True)
            tok_st_done[n] = tok_st
            wait("scalar", tok_st)
            wait("scalar", tok_negb)
            if avp_tok[pp][hp]:
                wait("scalar", avp_tok[pp][hp])
            ex = bfv(A_EXPT + pp * 1024 + hp * 512, 512)
            tok_exp = None
            for kth in range(2):
                lc = LCOL[:, n * 32 + kth * 16 + h:n * 32 + kth * 16 + h + 1]
                tok_exp = act(ex[:, kth * 256:(kth + 1) * 256],
                              PB[bi][:, kth * 256:(kth + 1) * 256],
                              AF.Exp, bias=NEGB, scale=lc)
            exp_tok4[pp][hp] = tok_exp

    def sums_av_pair(n, p, tok_v):
        pp = p % 2
        bs0 = next_qk()
        bank_wait(bs0)
        for hp in (0, 1):
            ex = bfv(A_EXPT + pp * 1024 + hp * 512, 512)
            wait("tensor", exp_tok4[pp][hp])
            co = hp * 256
            mm(PB[bs0][:, co:co + 256], ONESF[:], ex[:, 0:256], True, False)
            sum_tok4[pp][hp] = mm(PB[bs0][:, co:co + 256], ONESF[:],
                                  ex[:, 256:512], False, True, inc=True)
            bav = 6 + pp
            bank_wait(bav, (evac_tok2[pp], tok_v))
            mm(PB[bav][:, co:co + 256], VN[:, p * 128:p * 128 + 128],
               ex[:, 0:256], True, False)
            tok_av = mm(PB[bav][:, co:co + 256],
                        VN[:, 1024 + p * 128:1024 + p * 128 + 128],
                        ex[:, 256:512], False, True, inc=True)
            avp_tok[pp][hp] = tok_av
        tok_avsum_done[n] = tok_av
        RR = rcol(R_RROW, 512)
        RRB = AR[0:1, A_ROWS + 2 * R_RROW:A_ROWS + 2 * R_RROW + 512]
        wait("scalar", sum_tok4[pp][1])
        wait("scalar", tok_rb_last[0])
        tok_lnr = act(RR, PB[bs0][0:1, 0:512], AF.Ln)
        bfree[bs0] = tok_lnr
        wait("scalar", tok_lnr)
        rtok2[pp] = act(RRB, RR, AF.Exp, scale=-1.0)

    def rb_evac_pair(n, p):
        pp = p % 2
        RRB = AR[0:1, A_ROWS + 2 * R_RROW:A_ROWS + 2 * R_RROW + 512]
        brb = next_qk()
        bank_wait(brb, (rtok2[pp],))
        mm(PB[brb][0:64, 0:256], ONESR[0:1, 0:64], RRB[0:1, 0:256],
           True, True)
        tok_rb = mm(PB[brb][64:128, 0:256], ONESR[0:1, 0:64],
                    RRB[0:1, 256:512], True, True, inc=True)
        tok_rb_last[0] = tok_rb
        wait("scalar", tok_rb)
        wait("scalar", evac_tok2[pp])
        rbs = f32v(A_RB + pp * 512, 256)
        tok_rbc = scopy(rbs, PB[brb][:, 0:256])
        bfree[brb] = tok_rbc
        wait("vector", tok_rbc)
        wait("vector", avp_tok[pp][0])
        wait("vector", avp_tok[pp][1])
        if n > 0:
            wait("vector", tok_wout_done[n - 1])
        bav = 6 + pp
        vtt(ATT[0:64, p * 256:(p + 1) * 256], PB[bav][0:64, 0:256],
            rbs[0:64, :], ALU.mult, inc=False)
        tok_ev = vtt(ATT[64:128, p * 256:(p + 1) * 256],
                     PB[bav][64:128, 256:512], rbs[64:128, :], ALU.mult)
        evac_tok2[pp] = tok_ev
        return tok_ev

    def attn_heads(n, tok_negb, tok_qk, tok_v, filler=None):
        tok_ev = None
        for c in range(10):
            if 2 <= c <= 9:
                tok_ev = rb_evac_pair(n, c - 2)
            if c <= 7:
                st_pair(n, c, tok_qk, tok_negb)
            if 1 <= c <= 8:
                sums_av_pair(n, c - 1, tok_v)
            if filler is not None:
                next(filler, None)
        return tok_ev

    def attn_wout(n, tok_ev_last):
        t0 = n * CS
        wait("tensor", tok_ev_last)
        wait("tensor", tok_woutw)
        tok_evw = None
        for oc in range(8):
            b = next_qk()
            bank_wait(b)
            tok = None
            for kc in range(KC):
                o = A_WOUT + kc * 1024 + oc * 128
                tok = mm(PB[b][:, 0:256], AR[:, o:o + 128],
                         ATT[:, kc * 256:(kc + 1) * 256],
                         kc == 0, kc == KC - 1, inc=(kc == KC - 1))
            tok_wout_done[n] = tok
            wait("vector", tok)
            wait("vector", tok_xa if n < 4 else tok_xb)
            sl = slice(oc * T + t0, oc * T + t0 + 256)
            tok_evw = vstt(RES[:, sl], PB[b][:, 0:256], BOUT[:, oc:oc + 1],
                           RES[:, sl], ALU.add, ALU.add)
            bfree[b] = tok_evw
        tok_chunk_res[n] = tok_evw

    def emit_gwsum():
        b = next_qk()
        bank_wait(b)
        wait("tensor", tok_consts)
        tok = None
        for kc in range(KC):
            tok = mm(PB[b][0:8, 0:1], GW[:, kc * 8:(kc + 1) * 8], ONESC32[:],
                     kc == 0, kc == KC - 1, inc=(kc == KC - 1))
        wait("vector", tok)
        tok_g = vcopy(rcol(R_GWSUM, 1, 8), PB[b][0:8, 0:1])
        vdrain()
        bfree[b] = tok_g

    emit_gwsum()
    for p in range(4):
        ln_stats(p, tok_xa)
        ln_apply(p)

    def ln_full(p, res_tok):
        ln_stats(p, res_tok)
        ln_apply(p)
        if p >= 8:
            ln_gate1(p)
            ln_gate_j(p, 0)
            ln_gate_j(p, 1)
            ln_gate_sg(p)

    def stats2_filler(n):
        for c in range(10):
            if c == 3 and n >= 1:
                ln_stats(8 + n - 1, tok_chunk_res[n - 1])
            yield

    for n in range(NCH):
        tok_qk, tok_v, tok_negb = attn_qkv_bound(n)
        if n >= 1 and n + 3 <= 7:
            ln_apply(n + 3)
        tok_ev_last = attn_heads(n, tok_negb, tok_qk, tok_v,
                                 filler=stats2_filler(n))
        if n >= 1:
            p = 8 + n - 1
            ln_apply(p)
            ln_gate1(p)
            ln_gate_j(p, 0)
            ln_gate_j(p, 1)
            ln_gate_sg(p)
        if n + 4 <= 7:
            ln_stats(n + 4, tok_xb)
        attn_wout(n, tok_ev_last)
    ln_stats(15, tok_chunk_res[7])
    ln_apply(15)
    ln_gate1(15)
    ln_gate_j(15, 0)
    ln_gate_j(15, 1)
    ln_gate_sg(15)

    for pp in (0, 1):
        for hp in (0, 1):
            wait("tensor", exp_tok4[pp][hp])
        wait("tensor", rtok2[pp])
        wait("tensor", evac_tok2[pp])

    if stop == "attn":
        dbg_outs = {
            "d_res": (RES[:], [128, KC * T], F32, tok_chunk_res[7]),
            "d_lnb": (LNB[:], [128, KC * T], BF16, tok_lnb_w[15]),
            "d_lcol": (LCOL, [128, 256], F32, tok_lnb_w[15]),
            "d_q7": (bfv(A_Q, 2048), [128, 2048], BF16, tok_lnb_w[15]),
            "d_k7": (bfv(A_K, 2048), [128, 2048], BF16, tok_lnb_w[15]),
            "d_v7": (bfv(A_V, 2048), [128, 2048], BF16, tok_lnb_w[15]),
            "d_att7": (bfv(A_ATT, 2048), [128, 2048], BF16, tok_lnb_w[15]),
            "d_wet": (WET, [8, 2048], BF16, tok_gating["wet"]),
            "d_sg": (SG, [1, 2048], BF16, tok_gating["sg"]),
            "d_negb": (NEGB, [128, 1], F32, tok_lnb_w[15]),
        }
        ndma = 0
        for name, (ap, shape, dt_, tok) in dbg_outs.items():
            dd = nc.dram_tensor(name, shape, dt_, kind="ExternalOutput")
            wait("sync", tok)
            wait("sync", tok_gating["sg"])
            dma("sync", dd[:], ap, "sOUT")
            ndma += 1
        wait("sync", ("sOUT", 16 * ndma))
        return _materialize(nc, ops, sems)

    dma_gu_tok = [None] * len(UNITS)
    dma_dn_tok = [None] * len(UNITS)

    def emit_unit_dma(u, pe_release_tok):
        slot = u % 2
        gu_off, gu_len, dn_off, dn_len = FFNW_OFFS[u]
        wait("gpsimd", pe_release_tok)
        dma_gu_tok[u] = dma("gpsimd", bfv(AF_GU[slot], gu_len),
                            d_ffnw[:, gu_off:gu_off + gu_len], f"sGU{slot}")
        dma_dn_tok[u] = dma("gpsimd", bfv(AF_DN[slot], dn_len),
                            d_ffnw[:, dn_off:dn_off + dn_len], f"sDN{slot}")

    wait("gpsimd", tok_qkv_done[7])
    wait("gpsimd", tok_wout_done[7])
    wait("gpsimd", tok_st_done[7])
    emit_unit_dma(0, None)
    emit_unit_dma(1, None)

    unit_release = [None] * len(UNITS)
    tok_wb = None
    tok_wb_prev_uw = None
    h_free = None
    sgb_read = [None, None]
    gu_bank_tok = [None, None, None, None]
    dn_bank_tok = [None, None, None, None]
    last_evac = None
    tok_wb_region_ok = (xscr_free[0], evac_tok2[0], evac_tok2[1], b2_chain[0])
    first_wb = [True]

    for u, (kind, idx, ic0, nic) in enumerate(UNITS):
        slot = u % 2
        gu_base = AF_GU[slot]
        dn_base = AF_DN[slot]
        build_wb = (kind == "r" and ic0 == 0) or (kind == "s" and idx == 0)
        if build_wb:
            for tt in range(4):
                b4 = 4 + tt % 4
                bank_wait(b4, (dn_bank_tok[tt % 4], tok_gating["wet"],
                               tok_gating["sg"], tok_selb, tok_onesr))
                if kind == "r":
                    tok = mm(PB[b4][:, 0:512],
                             SELB[0:8, idx * 128:idx * 128 + 128],
                             WET[0:8, tt * 512:(tt + 1) * 512], True, True,
                             inc=True)
                else:
                    tok = mm(PB[b4][:, 0:512], ONESR[:],
                             SG[0:1, tt * 512:(tt + 1) * 512], True, True,
                             inc=True)
                wait("scalar", tok)
                if first_wb[0]:
                    for tk in tok_wb_region_ok:
                        wait("scalar", tk)
                    first_wb[0] = False
                wait("scalar", tok_wb_prev_uw)
                tok_wb = scopy(WB[:, tt * 512:(tt + 1) * 512],
                               PB[b4][:, 0:512])
                dn_bank_tok[tt % 4] = tok_wb
        tok_uw_last = None
        for tt in range(4):
            wait("tensor", tok_lnb_w[8 + min(2 * tt + 1, 7)])
            htok_list = []
            silu_toks = [None] * nic
            uw_toks = [None] * nic

            def emit_h(icl):
                par = icl % 2
                wait("vector", silu_toks[icl])
                wait("vector", uw_toks[icl])
                wait("vector", h_free)
                tok_h = vtt(bfv(AF_H + icl * 512, 512), bfv(AF_SGB[par], 512),
                            bfv(AF_UW[par], 512), ALU.mult)
                sgb_read[par] = tok_h
                htok_list.append(tok_h)

            for icl in range(nic):
                par = icl % 2
                bg, bu = par, 2 + par
                bank_wait(bg, (gu_bank_tok[bg],))
                wait("tensor", dma_gu_tok[u])
                tok_g = None
                for kc in range(KC):
                    o = gu_base + kc * (2 * nic * 128) + icl * 128
                    tok_g = mm(PB[bg][:, 0:512], AR[:, o:o + 128],
                               LNB[:, kc * T + tt * 512:
                                   kc * T + (tt + 1) * 512],
                               kc == 0, kc == KC - 1, inc=(kc == KC - 1))
                bank_wait(bu, (gu_bank_tok[bu],))
                tok_u = None
                for kc in range(KC):
                    o = gu_base + kc * (2 * nic * 128) + (nic + icl) * 128
                    tok_u = mm(PB[bu][:, 0:512], AR[:, o:o + 128],
                               LNB[:, kc * T + tt * 512:
                                   kc * T + (tt + 1) * 512],
                               kc == 0, kc == KC - 1, inc=(kc == KC - 1))
                wait("scalar", tok_g)
                wait("scalar", sgb_read[par])
                sgb = bfv(AF_SGB[par], 512)
                tok_silu = act(sgb, PB[bg][:, 0:512], AF.Silu)
                silu_toks[icl] = tok_silu
                gu_bank_tok[bg] = tok_silu
                wait("vector", tok_u)
                wait("vector", tok_wb)
                if icl >= 2:
                    wait("vector", htok_list[icl - 2])
                uwb = bfv(AF_UW[par], 512)
                tok_uw = vtt(uwb, PB[bu][:, 0:512],
                             WB[:, tt * 512:(tt + 1) * 512], ALU.mult)
                gu_bank_tok[bu] = tok_uw
                uw_toks[icl] = tok_uw
                tok_uw_last = tok_uw
                if icl >= 1:
                    emit_h(icl - 1)
            emit_h(nic - 1)
            dtoks = [None] * 4
            for oh in range(2):
                for oc4 in range(4):
                    oc = oh * 4 + oc4
                    bd = 4 + oc4
                    bank_wait(bd, (dn_bank_tok[oc4], dma_dn_tok[u]))
                    tok_d = None
                    for icl in range(nic):
                        wait("tensor", htok_list[icl])
                        o = dn_base + icl * 1024 + oc * 128
                        tok_d = mm(PB[bd][:, 0:512], AR[:, o:o + 128],
                                   bfv(AF_H + icl * 512, 512),
                                   icl == 0, icl == nic - 1,
                                   inc=(icl == nic - 1))
                    dtoks[oc4] = tok_d
                for oc4 in range(4):
                    oc = oh * 4 + oc4
                    wait("vector", dtoks[oc4])
                    sl = slice(oc * T + tt * 512, oc * T + (tt + 1) * 512)
                    last_evac = vtt(RES[:, sl], PB[4 + oc4][:, 0:512],
                                    RES[:, sl], ALU.add)
                    dn_bank_tok[oc4] = last_evac
            h_free = dtoks[3]
        unit_release[u] = dtoks[3]
        tok_wb_prev_uw = tok_uw_last
        if u + 2 < len(UNITS):
            emit_unit_dma(u + 2, unit_release[u])

    wait("sync", last_evac)
    tok_out = dma("sync", d_out[:], RES[:], "sOUT")
    wait("sync", tok_out)
    return _materialize(nc, ops, sems)


def _materialize(nc, ops, sems):
    names = ("sIN", "sX", "sX2", "sWQ", "sWK", "sWV", "sWO", "sW", "sGU0",
             "sGU1", "sDN0", "sDN1", "sPO", "sPE", "sSC", "sVE", "sOUT")
    import contextlib
    with contextlib.ExitStack() as stack:
        for nm in names:
            sems[nm] = stack.enter_context(nc.semaphore(nm))
        block = stack.enter_context(nc.Block())

        @block.sync
        def _(eng):
            for fn in ops["sync"]:
                fn(eng)

        @block.gpsimd
        def _(eng):
            for fn in ops["gpsimd"]:
                fn(eng)

        @block.tensor
        def _(eng):
            for fn in ops["tensor"]:
                fn(eng)

        @block.scalar
        def _(eng):
            for fn in ops["scalar"]:
                fn(eng)

        @block.vector
        def _(eng):
            for fn in ops["vector"]:
                fn(eng)

    return nc


_NC_CACHE = {}


def _get_nc():
    if "nc" not in _NC_CACHE:
        _NC_CACHE["nc"] = build_nc()
    return _NC_CACHE["nc"]


def _chunked_fm(a):
    k, o = a.shape
    return np.ascontiguousarray(
        a.reshape(k // 128, 128, o).transpose(1, 0, 2).reshape(128, -1))


def prep_in_maps(inputs):
    g = {k: np.asarray(v, dtype=np.float32) for k, v in inputs.items()}
    x = g["L0_attn_x"]
    wqkv = g["L0_attn_Wqkv"]
    wout = g["L0_attn_Wout"]
    bout = g["L0_attn_bout"]
    wl1 = g["L0_attn_Wl1"]
    bl1 = g["L0_attn_bl1"]
    wl2 = g["L0_attn_Wl2"]
    bl2 = g["L0_attn_bl2"]
    gate_w = g["L0_ffn_gate_w"]
    egu = g["L0_ffn_experts_gate_up"]
    edn = g["L0_ffn_experts_down"]
    sgw = g["L0_ffn_shared_gate_w"]
    suw = g["L0_ffn_shared_up_w"]
    sdw = g["L0_ffn_shared_down_w"]
    segw = g["L0_ffn_shared_expert_gate_w"]

    wqkvT = wqkv.T
    atnw = np.concatenate([_chunked_fm(wqkvT[:, 0:1024]),
                           _chunked_fm(wqkvT[:, 1024:2048]),
                           _chunked_fm(wqkvT[:, 2048:3072]),
                           _chunked_fm(wout.T)], axis=1)

    ffnw = np.empty((128, FFNW_TOT), dtype=np.float32)
    for u, (kind, idx, ic0, nic) in enumerate(UNITS):
        gu_off, gu_len, dn_off, dn_len = FFNW_OFFS[u]
        if kind == "r":
            guT = egu[idx].T
            gcol = guT[:, ic0 * 128:(ic0 + nic) * 128]
            ucol = guT[:, I_MOE + ic0 * 128:I_MOE + (ic0 + nic) * 128]
            drow = edn[idx].T[ic0 * 128:(ic0 + nic) * 128, :]
        else:
            gcol = sgw.T[:, ic0 * 128:(ic0 + nic) * 128]
            ucol = suw.T[:, ic0 * 128:(ic0 + nic) * 128]
            drow = sdw.T[ic0 * 128:(ic0 + nic) * 128, :]
        blk = np.concatenate([gcol, ucol], axis=1)
        ffnw[:, gu_off:gu_off + gu_len] = _chunked_fm(blk)
        ffnw[:, dn_off:dn_off + dn_len] = np.ascontiguousarray(
            drow.reshape(nic, 128, 1024).transpose(1, 0, 2).reshape(128, -1))

    gw = _chunked_fm(gate_w.T)
    seg = np.ascontiguousarray(segw.reshape(KC, 128).T)
    boutp = np.ascontiguousarray(bout.reshape(KC, 128).T)
    sel = np.zeros((8, 1024), np.float32)
    for e in range(8):
        sel[e, e * 128:(e + 1) * 128] = 1.0
    shared = dict(
        atnw=atnw, ffnw=ffnw, gw=gw, seg=seg, boutp=boutp, sel=sel,
        idn=np.eye(128, dtype=np.float32),
        ones1=np.ones((1, 128), np.float32),
        onesc=np.ones((128, 1), np.float32),
        onesf=np.ones((128, 128), np.float32),
        wl1t=np.ascontiguousarray(wl1.T),
        bl1c=np.ascontiguousarray(bl1.reshape(64, 1)),
        wl2t=np.ascontiguousarray(wl2.T),
        bl2c=np.ascontiguousarray(bl2.reshape(16, 1)))

    in_maps = []
    for b in range(NCORE):
        m = dict(shared)
        m["xt"] = np.ascontiguousarray(
            x[b].T.reshape(KC, 128, T).transpose(1, 0, 2).reshape(128, -1))
        in_maps.append(m)
    return in_maps


def kernel(**inputs):
    in_maps = prep_in_maps(inputs)
    nc = _get_nc()
    res = run_bass_kernel_spmd(nc, in_maps, core_ids=list(range(NCORE)))
    outs = []
    for b in range(NCORE):
        o = np.asarray(res.results[b]["out"])
        y = o.reshape(128, KC, T).transpose(1, 0, 2).reshape(C, T).T
        outs.append(np.ascontiguousarray(y))
    return np.stack(outs, axis=0).astype(np.float32)


# version 16
# speedup vs baseline: 1.1103x; 1.0002x over previous
import numpy as np

import concourse.bass as bass
import concourse.mybir as mybir
from concourse.bass_utils import run_bass_kernel_spmd

F32 = mybir.dt.float32
BF16 = mybir.dt.bfloat16
AF = mybir.ActivationFunctionType
ALU = mybir.AluOpType
AX = mybir.AxisListType

T = 2048
C = 1024
KC = 8
NCH = 8
CS = 256
E = 8
I_MOE = 1408
NCORE = 8

UNITS = []
for e in range(E):
    UNITS.append(("r", e, 0, 6))
    UNITS.append(("r", e, 6, 5))
for j, (ic0, nic) in enumerate([(0, 6), (6, 5), (11, 6), (17, 5)]):
    UNITS.append(("s", j, ic0, nic))

FFNW_OFFS = []
_off = 0
for (_k, _i, _ic0, _nic) in UNITS:
    gu_len = 8 * 2 * _nic * 128
    dn_len = _nic * 1024
    FFNW_OFFS.append((_off, gu_len, _off + gu_len, dn_len))
    _off += gu_len + dn_len
FFNW_TOT = _off

A_QKV = 0
A_WOUT = 24576
A_Q = 32768
A_K = 34816
A_V = 36864
A_ATT = 38912
A_EXPT = 40960
A_XSCR = 43008
A_RB = 45056
A_T = 46080
A_LCOL = 46592
A_ROWS = 47104
A_WET = 50416
A_SG = 52464
A_T2 = 54512
A_END = 55024
AF_GU = [0, 12288]
AF_DN = [24576, 30720]
AF_H = 36864
AF_SGB = [39936, 40448]
AF_UW = [40960, 41472]
AF_WB = 41984
AS_LOGP = 24576
AS_H1 = 28672
AS_E = 29696
AS_LAM = 30720

R_MEAN = 0
R_VAR = 256
R_RSTD = 512
R_MR = 768
R_RROW = 1024
R_MISC = 1536
R_NEGB = 1600
R_SMALL = 1602
R_STM = 1608
R_E8 = 1616
R_SC8 = 1624
R_GE = 1632
R_WTM = 1640
R_GWSUM = 1648
R_BL2N = 1650
ROWS_F32 = 1656


def build_nc(stop=None):
    nc = bass.Bass("TRN2", target_bir_lowering=False, debug=False)

    d_x = nc.dram_tensor("xt", [128, KC * T], F32, kind="ExternalInput")
    d_atnw = nc.dram_tensor("atnw", [128, 32768], F32, kind="ExternalInput")
    d_ffnw = nc.dram_tensor("ffnw", [128, FFNW_TOT], F32, kind="ExternalInput")
    d_gw = nc.dram_tensor("gw", [128, 64], F32, kind="ExternalInput")
    d_seg = nc.dram_tensor("seg", [128, 8], F32, kind="ExternalInput")
    d_bout = nc.dram_tensor("boutp", [128, 8], F32, kind="ExternalInput")
    d_sel = nc.dram_tensor("sel", [8, 1024], F32, kind="ExternalInput")
    d_idn = nc.dram_tensor("idn", [128, 128], F32, kind="ExternalInput")
    d_ones1 = nc.dram_tensor("ones1", [1, 128], F32, kind="ExternalInput")
    d_onesc = nc.dram_tensor("onesc", [128, 1], F32, kind="ExternalInput")
    d_onesf = nc.dram_tensor("onesf", [128, 128], F32, kind="ExternalInput")
    d_wl1t = nc.dram_tensor("wl1t", [1, 64], F32, kind="ExternalInput")
    d_bl1 = nc.dram_tensor("bl1c", [64, 1], F32, kind="ExternalInput")
    d_wl2t = nc.dram_tensor("wl2t", [64, 16], F32, kind="ExternalInput")
    d_bl2 = nc.dram_tensor("bl2c", [16, 1], F32, kind="ExternalInput")
    d_out = nc.dram_tensor("out", [128, KC * T], F32, kind="ExternalOutput")

    RES = nc.alloc_sbuf_tensor("res", [128, KC * T], F32)
    LNB = nc.alloc_sbuf_tensor("lnb", [128, KC * T], BF16)
    AR = nc.alloc_sbuf_tensor("arena", [128, A_END], BF16)
    IDN32 = nc.alloc_sbuf_tensor("idn32", [128, 128], F32)
    ONES32 = nc.alloc_sbuf_tensor("ones32", [1, 128], F32)
    ONESC32 = nc.alloc_sbuf_tensor("onesc32", [128, 1], F32)
    ONESB = nc.alloc_sbuf_tensor("onesb", [128, 1], BF16)
    ONESF = nc.alloc_sbuf_tensor("onesf_sb", [128, 128], BF16)
    ONESR = nc.alloc_sbuf_tensor("onesr", [1, 128], BF16)
    SELB = nc.alloc_sbuf_tensor("selb", [8, 1024], BF16)
    GW = nc.alloc_sbuf_tensor("gwsb", [128, 64], F32)
    SEGB = nc.alloc_sbuf_tensor("segb", [128, 8], BF16)
    BOUT = nc.alloc_sbuf_tensor("boutsb", [128, 8], F32)
    WL1T = nc.alloc_sbuf_tensor("wl1ts", [1, 64], F32)
    BL1 = nc.alloc_sbuf_tensor("bl1s", [64, 1], F32)
    WL2T = nc.alloc_sbuf_tensor("wl2ts", [64, 16], F32)
    BL2 = nc.alloc_sbuf_tensor("bl2s", [16, 1], F32)

    PB = [nc.alloc_psum_tensor(f"pb{i}", [128, 512], F32) for i in range(8)]

    def bfv(off, n, p0=0, p1=128):
        return AR[p0:p1, off:off + n]

    def f32v(off, nf, p0=0, p1=128):
        return AR[p0:p1, off:off + 2 * nf].bitcast(F32)

    def rcol(off, n=1, p=1):
        return AR[0:p, A_ROWS + 2 * off:A_ROWS + 2 * (off + n)].bitcast(F32)

    LCOL = f32v(A_LCOL, 256)
    WET = bfv(A_WET, 2048, 0, 8)
    SG = bfv(A_SG, 2048, 0, 1)
    WB = f32v(AF_WB, 2048)
    XSCR = bfv(A_XSCR, 2048)
    AT = f32v(A_T, 256)
    AT2 = f32v(A_T2, 256)
    NEGB = rcol(R_NEGB, 1, 128)
    LOGP = f32v(AS_LOGP, 2048, 0, 1)
    H1 = f32v(AS_H1, 512, 0, 64)
    EE = f32v(AS_E, 512, 0, 16)
    LAMR = f32v(AS_LAM, 512, 0, 16)

    ENGS = ("sync", "gpsimd", "tensor", "scalar", "vector")
    ops = {e: [] for e in ENGS}
    cnt = {s: 0 for s in ("sIN", "sX", "sX2", "sWQ", "sWK", "sWV", "sWO",
                          "sW", "sGU0", "sGU1", "sDN0", "sDN1", "sPO",
                          "sPE", "sSC", "sVE", "sOUT")}
    lastw = {e: {} for e in ENGS}
    sems = {}

    def emit(eng, fn, inc=None):
        if inc is None:
            ops[eng].append(fn)
            return None
        sem, k = inc
        cnt[sem] += k
        v = cnt[sem]

        def wrapped(Eng, fn=fn, sem=sem, k=k):
            fn(Eng).then_inc(sems[sem], k)
        ops[eng].append(wrapped)
        return (sem, v)

    def wait(eng, tok):
        if not tok:
            return
        sem, v = tok
        if v > cnt[sem]:
            raise RuntimeError(f"forward wait {eng} {sem} {v} > {cnt[sem]}")
        if lastw[eng].get(sem, 0) >= v:
            return
        lastw[eng][sem] = v
        ops[eng].append(lambda Eng, s=sem, v=v: Eng.wait_ge(sems[s], v))

    def mm(bank_ap, lhsT, rhs, start, stop, inc=False):
        fn = (lambda Eng, o=bank_ap, l=lhsT, r=rhs, st=start, sp=stop:
              Eng.matmul(o, l, r, start=st, stop=sp, skip_group_check=True))
        return emit("tensor", fn, inc=("sPE", 1) if inc else None)

    def tr(out_ap, in_ap, idn_ap, inc=True):
        fn = lambda Eng, o=out_ap, i=in_ap, d=idn_ap: Eng.transpose(o, i, d)
        return emit("tensor", fn, inc=("sPE", 1) if inc else None)

    def act(out_ap, in_ap, func, bias=0.0, scale=1.0, inc=True):
        fn = (lambda Eng, o=out_ap, i=in_ap, f=func, b=bias, s=scale:
              Eng.activation(o, i, f, bias=b, scale=s))
        return emit("scalar", fn, inc=("sSC", 1) if inc else None)

    def scopy(out_ap, in_ap, inc=True):
        fn = lambda Eng, o=out_ap, i=in_ap: Eng.copy(o, i)
        return emit("scalar", fn, inc=("sSC", 1) if inc else None)

    def sdrain():
        emit("scalar", lambda Eng: Eng.drain())

    def vtt(out_ap, a_ap, b_ap, op, inc=True):
        fn = (lambda Eng, o=out_ap, a=a_ap, b=b_ap, p=op:
              Eng.tensor_tensor(o, a, b, p))
        return emit("vector", fn, inc=("sVE", 1) if inc else None)

    def vts(out_ap, a_ap, scalar, op, inc=True):
        fn = (lambda Eng, o=out_ap, a=a_ap, s=scalar, p=op:
              Eng.tensor_single_scalar(o, a, s, p))
        return emit("vector", fn, inc=("sVE", 1) if inc else None)

    def vstt(out_ap, in0, scalar, in1, op0, op1, inc=True):
        fn = (lambda Eng, o=out_ap, a=in0, s=scalar, b=in1, p0=op0, p1=op1:
              Eng.scalar_tensor_tensor(o, a, s, b, p0, p1))
        return emit("vector", fn, inc=("sVE", 1) if inc else None)

    def vred(out_ap, in_ap, op, negate=False, inc=True):
        fn = (lambda Eng, o=out_ap, i=in_ap, p=op, n=negate:
              Eng.tensor_reduce(o, i, AX.X, p, negate=n))
        return emit("vector", fn, inc=("sVE", 1) if inc else None)

    def vrecip(out_ap, in_ap, inc=True):
        fn = lambda Eng, o=out_ap, i=in_ap: Eng.reciprocal(o, i)
        return emit("vector", fn, inc=("sVE", 1) if inc else None)

    def vdrain():
        emit("vector", lambda Eng: Eng.drain())

    def vcopy(out_ap, in_ap, inc=True):
        fn = lambda Eng, o=out_ap, i=in_ap: Eng.tensor_copy(o, i)
        return emit("vector", fn, inc=("sVE", 1) if inc else None)

    def pcopy(out_ap, in_ap, inc=True):
        fn = lambda Eng, o=out_ap, i=in_ap: Eng.tensor_copy(o, i)
        return emit("gpsimd", fn, inc=("sPO", 1) if inc else None)

    def dma(q, out_ap, in_ap, sem):
        fn = lambda Eng, o=out_ap, i=in_ap: Eng.dma_start(o, i)
        return emit(q, fn, inc=(sem, 16))

    bfree = [None] * 8

    def bank_wait(i, extra=()):
        wait("tensor", bfree[i])
        for t in extra:
            wait("tensor", t)

    resh = RES[:].rearrange("p (c t) -> p c t", c=KC)
    xh = d_x[:].rearrange("p (c t) -> p c t", c=KC)
    tok_xa = dma("sync", resh[:, :, 0:1024], xh[:, :, 0:1024], "sX")
    tok_xb = dma("sync", resh[:, :, 1024:2048], xh[:, :, 1024:2048], "sX2")

    def tok_x_for(ch):
        return tok_xa if ch < 4 else tok_xb
    tok_x = tok_xb
    tok_consts = None
    for (dst, src) in ((IDN32[:], d_idn[:]), (ONES32[:], d_ones1[:]),
                       (ONESC32[:], d_onesc[:]), (GW[:], d_gw[:]),
                       (BOUT[:], d_bout[:]), (WL1T[:], d_wl1t[:]),
                       (BL1[:], d_bl1[:]), (WL2T[:], d_wl2t[:]),
                       (BL2[:], d_bl2[:])):
        tok_consts = dma("sync", dst, src, "sIN")
    tok_wq = dma("gpsimd", bfv(A_QKV, 8192), d_atnw[:, 0:8192], "sWQ")
    tok_wk = dma("gpsimd", bfv(A_QKV + 8192, 8192), d_atnw[:, 8192:16384],
                 "sWK")
    tok_wv = dma("gpsimd", bfv(A_QKV + 16384, 8192), d_atnw[:, 16384:24576],
                 "sWV")
    dma("gpsimd", ONESB[:], d_onesc[:], "sW")
    dma("gpsimd", ONESR[:], d_ones1[:], "sW")
    dma("gpsimd", SELB[:], d_sel[:], "sW")
    dma("gpsimd", ONESF[:], d_onesf[:], "sW")
    tok_smalls = dma("gpsimd", SEGB[:], d_seg[:], "sW")
    tok_onesb = tok_onesr = tok_selb = tok_segb = tok_smalls

    def iota_fn(Eng, ap=LOGP):
        return Eng.iota(ap, [[1, T]], channel_multiplier=0,
                        allow_small_or_imprecise_dtypes=True)
    tok_iota = emit("gpsimd", iota_fn, inc=("sPO", 1))

    wait("vector", tok_iota)
    tok_ip1 = vts(LOGP, LOGP, 1.0, ALU.add)
    wait("scalar", tok_ip1)
    wait("scalar", tok_consts)
    tok_logp = act(LOGP, LOGP, AF.Ln)
    BL2N = rcol(R_BL2N, 1, 16)
    wait("vector", tok_consts)
    tok_bl2n = vts(BL2N, BL2[:], -1.0, ALU.mult)
    tok_relu = None
    tok_sig = None
    tok_lam_done = None
    for tt in range(4):
        wait("tensor", tok_logp)
        wait("tensor", tok_relu)
        t1 = mm(PB[0][0:64, :], WL1T[0:1, :],
                LOGP[0:1, tt * 512:(tt + 1) * 512], True, True, inc=True)
        wait("scalar", t1)
        tok_relu = act(H1, PB[0][0:64, :], AF.Relu, bias=BL1[:])
        wait("tensor", tok_relu)
        wait("tensor", tok_sig)
        t3 = mm(PB[1][0:16, :], WL2T[:], H1, True, True, inc=True)
        wait("scalar", t3)
        wait("scalar", tok_bl2n)
        tok_sig = act(EE, PB[1][0:16, :], AF.Exp, bias=BL2N, scale=-1.0)
        wait("vector", tok_sig)
        vts(EE, EE, 1.0, ALU.add, inc=False)
        vdrain()
        t6 = vrecip(LAMR, EE)
        for q in range(4):
            t0g = tt * 512 + q * 128
            n_i, kth = t0g // 256, (t0g // 128) % 2
            wait("tensor", t6)
            wait("tensor", tok_lam_done)
            t7 = tr(PB[2][0:128, 0:16], LAMR[0:16, q * 128:(q + 1) * 128],
                    IDN32[0:16, 0:16])
            wait("scalar", t7)
            co = n_i * 32 + kth * 16
            tok_lam_done = scopy(LCOL[:, co:co + 16], PB[2][0:128, 0:16])
    bfree[2] = tok_lam_done
    wait("gpsimd", tok_lam_done)
    tok_woutw = dma("gpsimd", bfv(A_WOUT, 8192), d_atnw[:, 24576:32768], "sWO")

    xscr_free = [None]
    rows_free = [None]
    b2_chain = [None]
    tok_chunk_res = [None] * NCH
    tok_qkv_done = [None] * NCH
    tok_st_done = [None] * NCH
    tok_avsum_done = [None] * NCH
    tok_wout_done = [None] * NCH
    tok_lnb_w = [None] * 16
    exp_tok4 = [[None, None], [None, None]]
    rtok2 = [None, None]
    sum_tok4 = [[None, None], [None, None]]
    avp_tok = [[None, None], [None, None]]
    evac_tok2 = [None, None]
    tok_rb_last = [None]
    tok_gating = dict(wet=None, sg=None)
    qk_bank = [0]

    def next_qk():
        b = qk_bank[0]
        qk_bank[0] = 1 - b
        return b

    ln_ctx = {}

    def ln_stats(p, res_tok):
        gating = p >= 8
        ch = p % 8
        t0 = ch * 256
        wait("gpsimd", tok_x_for(p % 8))
        wait("gpsimd", res_tok)
        if gating:
            wait("gpsimd", tok_qkv_done[ch])
        lnb3 = LNB[:].rearrange("p (c t) -> p c t", c=KC)[:, :, t0:t0 + 256]
        res3 = RES[:].rearrange("p (c t) -> p c t", c=KC)[:, :, t0:t0 + 256]
        tok_cast = pcopy(lnb3, res3)
        wait("scalar", tok_cast)
        wait("scalar", xscr_free[0])
        xs3 = XSCR[:].rearrange("p (c t) -> p c t", c=KC)
        tok_sq = act(xs3, lnb3, AF.Square)
        bs = next_qk()
        bank_wait(bs)
        wait("tensor", tok_cast)
        wait("tensor", tok_smalls)
        tok_sum = None
        for kc in range(KC):
            tok_sum = mm(PB[bs][:, 0:256], ONESF[:],
                         LNB[:, kc * T + t0:kc * T + t0 + 256],
                         kc == 0, kc == KC - 1, inc=(kc == KC - 1))
        wait("tensor", tok_sq)
        tok_sq2 = None
        for kc in range(KC):
            tok_sq2 = mm(PB[bs][:, 256:512], ONESF[:],
                         XSCR[:, kc * 256:(kc + 1) * 256],
                         kc == 0, kc == KC - 1, inc=(kc == KC - 1))
        xscr_free[0] = tok_sq2
        MEAN = rcol(R_MEAN, 256)
        VAR = rcol(R_VAR, 256)
        RSTD = rcol(R_RSTD, 256)
        MR = rcol(R_MR, 256)
        wait("vector", tok_sum)
        wait("vector", tok_sq2)
        wait("vector", rows_free[0])
        vts(MEAN, PB[bs][0:1, 0:256], 1.0 / C, ALU.mult, inc=False)
        vdrain()
        vtt(MR, MEAN, MEAN, ALU.mult, inc=False)
        vdrain()
        vstt(VAR, PB[bs][0:1, 256:512], 1.0 / C, MR, ALU.mult,
             ALU.subtract, inc=False)
        vdrain()
        tok_var = vts(VAR, VAR, 1e-5, ALU.add)
        bfree[bs] = tok_var
        wait("scalar", tok_var)
        act(VAR, VAR, AF.Ln, inc=False)
        sdrain()
        tok_rstd = act(RSTD, VAR, AF.Exp, scale=-0.5)
        wait("vector", tok_rstd)
        tok_mr = vtt(MR, MEAN, RSTD, ALU.mult)
        vdrain()
        ln_ctx[p] = (tok_var, tok_mr)

    def ln_apply(p):
        ch = p % 8
        t0 = ch * 256
        MR = rcol(R_MR, 256)
        RSTD = rcol(R_RSTD, 256)
        tok_var, tok_mr = ln_ctx[p]
        bb = next_qk()
        bank_wait(bb, (tok_var, tok_mr))
        mm(PB[bb][:, 0:256], ONES32[:], MR, True, True)
        tok_bc = mm(PB[bb][:, 256:512], ONES32[:], RSTD, True, True, inc=True)
        wait("vector", tok_bc)
        wait("vector", b2_chain[0])
        tok_ln = None
        ats = [AT, AT2]
        ttoks = [None] * KC
        lntoks = [None] * KC
        prev = None
        for kc in range(KC):
            sl = slice(kc * T + t0, kc * T + t0 + 256)
            if kc >= 2:
                wait("vector", lntoks[kc - 2])
            ttoks[kc] = vtt(ats[kc % 2], RES[:, sl], PB[bb][:, 256:512],
                            ALU.mult)
            if prev is not None:
                wait("vector", ttoks[prev])
                slp = slice(prev * T + t0, prev * T + t0 + 256)
                tok_ln = lntoks[prev] = vtt(LNB[:, slp], ats[prev % 2],
                                            PB[bb][:, 0:256], ALU.subtract)
            prev = kc
        wait("vector", ttoks[prev])
        slp = slice(prev * T + t0, prev * T + t0 + 256)
        tok_ln = lntoks[prev] = vtt(LNB[:, slp], ats[prev % 2],
                                    PB[bb][:, 0:256], ALU.subtract)
        b2_chain[0] = tok_ln
        bfree[bb] = tok_ln
        tok_lnb_w[p] = tok_ln
        if p < 8:
            rows_free[0] = tok_bc
        else:
            ln_ctx[p] = (tok_bc, tok_ln)

    gate_ctx = {}

    def ln_gate1(p):
        ch = p % 8
        t0 = ch * 256
        MEAN = rcol(R_MEAN, 256)
        RSTD = rcol(R_RSTD, 256)
        g1 = next_qk()
        bank_wait(g1)
        tok_lg = None
        for kc in range(KC):
            tok_lg = mm(PB[g1][0:8, 0:256], GW[:, kc * 8:(kc + 1) * 8],
                        RES[:, kc * T + t0:kc * T + t0 + 256],
                        kc == 0, kc == KC - 1, inc=(kc == KC - 1))
        g2 = next_qk()
        bank_wait(g2)
        mm(PB[g2][0:8, 0:256], ONES32[0:1, 0:8], MEAN, True, True)
        tok_m8 = mm(PB[g2][0:8, 256:512], ONES32[0:1, 0:8], RSTD, True, True,
                    inc=True)
        LG1 = rcol(R_VAR, 256, 8)
        LG2 = rcol(R_MR, 256, 8)
        GWS = rcol(R_GWSUM, 1, 8)
        wait("vector", tok_m8)
        wait("vector", tok_lg)
        vts(LG1, PB[g2][0:8, 0:256], GWS, ALU.mult, inc=False)
        vdrain()
        vstt(LG2, PB[g1][0:8, 0:256], 1.0, LG1, ALU.mult, ALU.subtract,
             inc=False)
        vdrain()
        tok_l3 = vtt(LG1, LG2, PB[g2][0:8, 256:512], ALU.mult)
        vdrain()
        bfree[g2] = tok_l3
        bfree[g1] = tok_l3
        gate_ctx[p] = dict(tok_l3=tok_l3, tok_wet=None)

    def ln_gate_j(p, j):
        ch = p % 8
        t0 = ch * 256
        gc = gate_ctx[p]
        g1 = next_qk()
        bank_wait(g1)
        LG1 = rcol(R_VAR, 256, 8)
        STM = rcol(R_STM, 8, 128)
        E8 = rcol(R_E8, 8, 128)
        SC8 = rcol(R_SC8, 8, 128)
        GE8 = rcol(R_GE, 8, 128)
        WTM = rcol(R_WTM, 8, 128)
        NM = rcol(R_SMALL, 1, 128)
        SM = rcol(R_SMALL + 1, 1, 128)
        RC = rcol(R_SMALL + 2, 1, 128)
        M1 = rcol(R_SMALL + 3, 1, 128)
        M2 = rcol(R_SMALL + 4, 1, 128)
        wait("tensor", gc["tok_l3"])
        if gc["tok_wet"]:
            wait("tensor", gc["tok_wet"])
        tok_tr = tr(PB[g1][0:128, 0:8], LG1[0:8, j * 128:(j + 1) * 128],
                    IDN32[0:8, 0:8])
        wait("scalar", tok_tr)
        tok_stm = scopy(STM, PB[g1][0:128, 0:8])
        wait("vector", tok_stm)
        tok_nm = vred(NM, STM, ALU.max, negate=True)
        wait("scalar", tok_nm)
        tok_e8 = act(E8, STM, AF.Exp, bias=NM)
        wait("vector", tok_e8)
        vred(SM, E8, ALU.add, inc=False)
        vdrain()
        vrecip(RC, SM, inc=False)
        vdrain()
        vts(SC8, E8, RC, ALU.mult, inc=False)
        vdrain()
        vred(M1, SC8, ALU.max, inc=False)
        vdrain()
        vts(GE8, SC8, M1, ALU.is_ge, inc=False)
        vdrain()
        vstt(GE8, GE8, -1e30, SC8, ALU.mult, ALU.add, inc=False)
        vdrain()
        vred(M2, GE8, ALU.max, inc=False)
        vdrain()
        vts(GE8, SC8, M2, ALU.is_ge, inc=False)
        vdrain()
        tok_w = vtt(WTM, SC8, GE8, ALU.mult)
        wait("tensor", tok_w)
        tok_trb = tr(PB[g1][0:8, 0:128], WTM, IDN32[:])
        wait("scalar", tok_trb)
        gc["tok_wet"] = scopy(WET[0:8, t0 + j * 128:t0 + j * 128 + 128],
                              PB[g1][0:8, 0:128])
        bfree[g1] = gc["tok_wet"]

    def ln_gate_sg(p):
        ch = p % 8
        t0 = ch * 256
        gc = gate_ctx[p]
        g1 = next_qk()
        bank_wait(g1)
        tok_bc, tok_ln = ln_ctx[p]
        tok_gating["wet"] = gc["tok_wet"]
        wait("tensor", gc["tok_wet"])
        wait("tensor", tok_ln)
        wait("tensor", tok_segb)
        tok_sg = None
        for kc in range(KC):
            tok_sg = mm(PB[g1][0:1, 0:256], SEGB[:, kc:kc + 1],
                        LNB[:, kc * T + t0:kc * T + t0 + 256],
                        kc == 0, kc == KC - 1, inc=(kc == KC - 1))
        SGR = rcol(R_VAR, 256)
        wait("scalar", tok_sg)
        tok_sge = act(SGR, PB[g1][0:1, 0:256], AF.Exp, scale=-1.0)
        bfree[g1] = tok_sge
        wait("vector", tok_sge)
        vts(SGR, SGR, 1.0, ALU.add, inc=False)
        vdrain()
        SGR2 = rcol(R_RSTD, 256)
        vrecip(SGR2, SGR, inc=False)
        vdrain()
        tok_sgf = vcopy(SG[0:1, t0:t0 + 256], SGR2)
        tok_gating["sg"] = tok_sgf
        rows_free[0] = tok_sgf

    def attn_qkv_bound(n):
        t0 = n * CS
        wait("tensor", tok_lnb_w[n])
        QMX = rcol(R_MISC, 16)
        tok_qk_evac = None
        tok_v_evac = None
        tok_bnd_mm = None
        for which, dst_off, scale, poff in ((0, A_Q, 0.125, 0),
                                            (1, A_K, 1.0, 4)):
            ev_toks = [None] * 8
            sq_toks = [None] * 8

            def emit_sq(oc, which=which, dst_off=dst_off, ev_toks=None,
                        sq_toks=None):
                wait("scalar", ev_toks[oc])
                wait("scalar", xscr_free[0])
                sq_toks[oc] = act(XSCR[:, oc * 256:(oc + 1) * 256],
                                  bfv(dst_off, 2048)[:, oc * 256:
                                                     (oc + 1) * 256],
                                  AF.Square)

            wait("tensor", tok_wq if which == 0 else tok_wk)
            for oc in range(8):
                b = next_qk()
                bank_wait(b)
                tok = None
                for kc in range(KC):
                    o = A_QKV + which * 8192 + kc * 1024 + oc * 128
                    tok = mm(PB[b][:, 0:256], AR[:, o:o + 128],
                             LNB[:, kc * T + t0:kc * T + t0 + 256],
                             kc == 0, kc == KC - 1, inc=(kc == KC - 1))
                wait("scalar", tok)
                if n > 0:
                    wait("scalar", tok_st_done[n - 1])
                dst = bfv(dst_off, 2048)[:, oc * 256:(oc + 1) * 256]
                if scale != 1.0:
                    ev_toks[oc] = act(dst, PB[b][:, 0:256], AF.Copy,
                                      scale=scale)
                else:
                    ev_toks[oc] = scopy(dst, PB[b][:, 0:256])
                tok_qk_evac = ev_toks[oc]
                bfree[b] = ev_toks[oc]
                if oc >= 1:
                    emit_sq(oc - 1, ev_toks=ev_toks, sq_toks=sq_toks)
            emit_sq(7, ev_toks=ev_toks, sq_toks=sq_toks)
            for tt4 in range(4):
                b = next_qk()
                bank_wait(b, (sq_toks[2 * tt4], sq_toks[2 * tt4 + 1]))
                tok_bnd_mm = mm(PB[b][:, 0:512], ONESF[:],
                                XSCR[:, tt4 * 512:(tt4 + 1) * 512],
                                True, True, inc=True)
                wait("vector", tok_bnd_mm)
                tok_r = vred(QMX[0:1, poff + tt4:poff + tt4 + 1],
                             PB[b][0:1, 0:512], ALU.max)
                bfree[b] = tok_r
            xscr_free[0] = tok_bnd_mm
        wait("tensor", tok_wv)
        for th in range(2):
            for dv2 in range(2):
                b = next_qk()
                bank_wait(b)
                tok = None
                for kc in range(KC):
                    o = A_QKV + 16384 + kc * 1024 + dv2 * 512
                    tok = mm(PB[b][:, 0:512],
                             LNB[:, kc * T + t0 + th * 128:
                                 kc * T + t0 + th * 128 + 128],
                             AR[:, o:o + 512],
                             kc == 0, kc == KC - 1, inc=(kc == KC - 1))
                wait("scalar", tok)
                if n > 0:
                    wait("scalar", tok_avsum_done[n - 1])
                tok_v_evac = scopy(
                    bfv(A_V, 2048)[:, th * 1024 + dv2 * 512:
                                   th * 1024 + dv2 * 512 + 512],
                    PB[b][:, 0:512])
                bfree[b] = tok_v_evac
        tok_qkv_done[n] = tok
        vdrain()
        vred(QMX[0:1, 8:9], QMX[0:1, 0:4], ALU.max, inc=False)
        vred(QMX[0:1, 9:10], QMX[0:1, 4:8], ALU.max, inc=False)
        vdrain()
        tok_p = vtt(QMX[0:1, 10:11], QMX[0:1, 8:9], QMX[0:1, 9:10], ALU.mult)
        wait("scalar", tok_p)
        act(QMX[0:1, 11:12], QMX[0:1, 10:11], AF.Ln, inc=False)
        sdrain()
        tok_b = act(QMX[0:1, 12:13], QMX[0:1, 11:12], AF.Exp, scale=0.5)
        wait("vector", tok_b)
        tok_nb = vts(QMX[0:1, 13:14], QMX[0:1, 12:13], -1.0, ALU.mult)
        b = next_qk()
        bank_wait(b, (tok_nb,))
        tok = mm(PB[b][:, 0:1], ONES32[:], QMX[0:1, 13:14], True, True,
                 inc=True)
        wait("scalar", tok)
        for pp in (0, 1):
            for hp in (0, 1):
                wait("scalar", exp_tok4[pp][hp])
        tok_negb = scopy(NEGB, PB[b][:, 0:1])
        sdrain()
        bfree[b] = tok_negb
        return tok_qk_evac, tok_v_evac, tok_negb

    QN = bfv(A_Q, 2048)
    KN = bfv(A_K, 2048)
    VN = bfv(A_V, 2048)
    ATT = bfv(A_ATT, 2048)

    def st_pair(n, c, tok_qk, tok_negb):
        pp = c % 2
        for hp in (0, 1):
            h = 2 * c + hp
            bi = 2 + pp * 2 + hp
            bank_wait(bi, (exp_tok4[pp][hp], tok_qk))
            lh = KN[hp * 64:hp * 64 + 64, c * 256:c * 256 + 128]
            lh2 = KN[hp * 64:hp * 64 + 64, c * 256 + 128:c * 256 + 256]
            rh = QN[hp * 64:hp * 64 + 64, c * 256:(c + 1) * 256]
            mm(PB[bi][:, 0:256], lh, rh, True, True)
            tok_st = mm(PB[bi][:, 256:512], lh2, rh, True, True, inc=True)
            tok_st_done[n] = tok_st
            wait("scalar", tok_st)
            wait("scalar", tok_negb)
            if avp_tok[pp][hp]:
                wait("scalar", avp_tok[pp][hp])
            ex = bfv(A_EXPT + pp * 1024 + hp * 512, 512)
            tok_exp = None
            for kth in range(2):
                lc = LCOL[:, n * 32 + kth * 16 + h:n * 32 + kth * 16 + h + 1]
                tok_exp = act(ex[:, kth * 256:(kth + 1) * 256],
                              PB[bi][:, kth * 256:(kth + 1) * 256],
                              AF.Exp, bias=NEGB, scale=lc)
            exp_tok4[pp][hp] = tok_exp

    def sums_av_pair(n, p, tok_v):
        pp = p % 2
        bs0 = next_qk()
        bank_wait(bs0)
        for hp in (0, 1):
            ex = bfv(A_EXPT + pp * 1024 + hp * 512, 512)
            wait("tensor", exp_tok4[pp][hp])
            co = hp * 256
            mm(PB[bs0][:, co:co + 256], ONESF[:], ex[:, 0:256], True, False)
            sum_tok4[pp][hp] = mm(PB[bs0][:, co:co + 256], ONESF[:],
                                  ex[:, 256:512], False, True, inc=True)
            bav = 6 + pp
            bank_wait(bav, (evac_tok2[pp], tok_v))
            mm(PB[bav][:, co:co + 256], VN[:, p * 128:p * 128 + 128],
               ex[:, 0:256], True, False)
            tok_av = mm(PB[bav][:, co:co + 256],
                        VN[:, 1024 + p * 128:1024 + p * 128 + 128],
                        ex[:, 256:512], False, True, inc=True)
            avp_tok[pp][hp] = tok_av
        tok_avsum_done[n] = tok_av
        RR = rcol(R_RROW, 512)
        RRB = AR[0:1, A_ROWS + 2 * R_RROW:A_ROWS + 2 * R_RROW + 512]
        wait("scalar", sum_tok4[pp][1])
        wait("scalar", tok_rb_last[0])
        tok_lnr = act(RR, PB[bs0][0:1, 0:512], AF.Ln)
        bfree[bs0] = tok_lnr
        wait("scalar", tok_lnr)
        rtok2[pp] = act(RRB, RR, AF.Exp, scale=-1.0)

    def rb_evac_pair(n, p):
        pp = p % 2
        RRB = AR[0:1, A_ROWS + 2 * R_RROW:A_ROWS + 2 * R_RROW + 512]
        brb = next_qk()
        bank_wait(brb, (rtok2[pp],))
        mm(PB[brb][0:64, 0:256], ONESR[0:1, 0:64], RRB[0:1, 0:256],
           True, True)
        tok_rb = mm(PB[brb][64:128, 0:256], ONESR[0:1, 0:64],
                    RRB[0:1, 256:512], True, True, inc=True)
        tok_rb_last[0] = tok_rb
        wait("scalar", tok_rb)
        wait("scalar", evac_tok2[pp])
        rbs = f32v(A_RB + pp * 512, 256)
        tok_rbc = scopy(rbs, PB[brb][:, 0:256])
        bfree[brb] = tok_rbc
        wait("vector", tok_rbc)
        wait("vector", avp_tok[pp][0])
        wait("vector", avp_tok[pp][1])
        if n > 0:
            wait("vector", tok_wout_done[n - 1])
        bav = 6 + pp
        vtt(ATT[0:64, p * 256:(p + 1) * 256], PB[bav][0:64, 0:256],
            rbs[0:64, :], ALU.mult, inc=False)
        tok_ev = vtt(ATT[64:128, p * 256:(p + 1) * 256],
                     PB[bav][64:128, 256:512], rbs[64:128, :], ALU.mult)
        evac_tok2[pp] = tok_ev
        return tok_ev

    def attn_heads(n, tok_negb, tok_qk, tok_v, filler=None):
        tok_ev = None
        for c in range(10):
            if 2 <= c <= 9:
                tok_ev = rb_evac_pair(n, c - 2)
            if 1 <= c <= 8:
                sums_av_pair(n, c - 1, tok_v)
            if c <= 7:
                st_pair(n, c, tok_qk, tok_negb)
            if filler is not None:
                next(filler, None)
        return tok_ev

    def attn_wout(n, tok_ev_last):
        t0 = n * CS
        wait("tensor", tok_ev_last)
        wait("tensor", tok_woutw)
        tok_evw = None
        for oc in range(8):
            b = next_qk()
            bank_wait(b)
            tok = None
            for kc in range(KC):
                o = A_WOUT + kc * 1024 + oc * 128
                tok = mm(PB[b][:, 0:256], AR[:, o:o + 128],
                         ATT[:, kc * 256:(kc + 1) * 256],
                         kc == 0, kc == KC - 1, inc=(kc == KC - 1))
            tok_wout_done[n] = tok
            wait("vector", tok)
            wait("vector", tok_xa if n < 4 else tok_xb)
            sl = slice(oc * T + t0, oc * T + t0 + 256)
            tok_evw = vstt(RES[:, sl], PB[b][:, 0:256], BOUT[:, oc:oc + 1],
                           RES[:, sl], ALU.add, ALU.add)
            bfree[b] = tok_evw
        tok_chunk_res[n] = tok_evw

    def emit_gwsum():
        b = next_qk()
        bank_wait(b)
        wait("tensor", tok_consts)
        tok = None
        for kc in range(KC):
            tok = mm(PB[b][0:8, 0:1], GW[:, kc * 8:(kc + 1) * 8], ONESC32[:],
                     kc == 0, kc == KC - 1, inc=(kc == KC - 1))
        wait("vector", tok)
        tok_g = vcopy(rcol(R_GWSUM, 1, 8), PB[b][0:8, 0:1])
        vdrain()
        bfree[b] = tok_g

    emit_gwsum()
    for p in range(4):
        ln_stats(p, tok_xa)
        ln_apply(p)

    def ln_full(p, res_tok):
        ln_stats(p, res_tok)
        ln_apply(p)
        if p >= 8:
            ln_gate1(p)
            ln_gate_j(p, 0)
            ln_gate_j(p, 1)
            ln_gate_sg(p)

    def stats2_filler(n):
        for c in range(10):
            if c == 3 and n >= 1:
                ln_stats(8 + n - 1, tok_chunk_res[n - 1])
            yield

    for n in range(NCH):
        tok_qk, tok_v, tok_negb = attn_qkv_bound(n)
        if n >= 1 and n + 3 <= 7:
            ln_apply(n + 3)
        tok_ev_last = attn_heads(n, tok_negb, tok_qk, tok_v,
                                 filler=stats2_filler(n))
        if n >= 1:
            p = 8 + n - 1
            ln_apply(p)
            ln_gate1(p)
            ln_gate_j(p, 0)
            ln_gate_j(p, 1)
            ln_gate_sg(p)
        if n + 4 <= 7:
            ln_stats(n + 4, tok_xb)
        attn_wout(n, tok_ev_last)
    ln_stats(15, tok_chunk_res[7])
    ln_apply(15)
    ln_gate1(15)
    ln_gate_j(15, 0)
    ln_gate_j(15, 1)
    ln_gate_sg(15)

    for pp in (0, 1):
        for hp in (0, 1):
            wait("tensor", exp_tok4[pp][hp])
        wait("tensor", rtok2[pp])
        wait("tensor", evac_tok2[pp])

    if stop == "attn":
        dbg_outs = {
            "d_res": (RES[:], [128, KC * T], F32, tok_chunk_res[7]),
            "d_lnb": (LNB[:], [128, KC * T], BF16, tok_lnb_w[15]),
            "d_lcol": (LCOL, [128, 256], F32, tok_lnb_w[15]),
            "d_q7": (bfv(A_Q, 2048), [128, 2048], BF16, tok_lnb_w[15]),
            "d_k7": (bfv(A_K, 2048), [128, 2048], BF16, tok_lnb_w[15]),
            "d_v7": (bfv(A_V, 2048), [128, 2048], BF16, tok_lnb_w[15]),
            "d_att7": (bfv(A_ATT, 2048), [128, 2048], BF16, tok_lnb_w[15]),
            "d_wet": (WET, [8, 2048], BF16, tok_gating["wet"]),
            "d_sg": (SG, [1, 2048], BF16, tok_gating["sg"]),
            "d_negb": (NEGB, [128, 1], F32, tok_lnb_w[15]),
        }
        ndma = 0
        for name, (ap, shape, dt_, tok) in dbg_outs.items():
            dd = nc.dram_tensor(name, shape, dt_, kind="ExternalOutput")
            wait("sync", tok)
            wait("sync", tok_gating["sg"])
            dma("sync", dd[:], ap, "sOUT")
            ndma += 1
        wait("sync", ("sOUT", 16 * ndma))
        return _materialize(nc, ops, sems)

    dma_gu_tok = [None] * len(UNITS)
    dma_dn_tok = [None] * len(UNITS)

    def emit_unit_dma(u, pe_release_tok):
        slot = u % 2
        gu_off, gu_len, dn_off, dn_len = FFNW_OFFS[u]
        wait("gpsimd", pe_release_tok)
        dma_gu_tok[u] = dma("gpsimd", bfv(AF_GU[slot], gu_len),
                            d_ffnw[:, gu_off:gu_off + gu_len], f"sGU{slot}")
        dma_dn_tok[u] = dma("gpsimd", bfv(AF_DN[slot], dn_len),
                            d_ffnw[:, dn_off:dn_off + dn_len], f"sDN{slot}")

    wait("gpsimd", tok_qkv_done[7])
    wait("gpsimd", tok_wout_done[7])
    wait("gpsimd", tok_st_done[7])
    emit_unit_dma(0, None)
    emit_unit_dma(1, None)

    unit_release = [None] * len(UNITS)
    tok_wb = None
    tok_wb_prev_uw = None
    h_free = None
    sgb_read = [None, None]
    gu_bank_tok = [None, None, None, None]
    dn_bank_tok = [None, None, None, None]
    last_evac = None
    tok_wb_region_ok = (xscr_free[0], evac_tok2[0], evac_tok2[1], b2_chain[0])
    first_wb = [True]

    for u, (kind, idx, ic0, nic) in enumerate(UNITS):
        slot = u % 2
        gu_base = AF_GU[slot]
        dn_base = AF_DN[slot]
        build_wb = (kind == "r" and ic0 == 0) or (kind == "s" and idx == 0)
        if build_wb:
            for tt in range(4):
                b4 = 4 + tt % 4
                bank_wait(b4, (dn_bank_tok[tt % 4], tok_gating["wet"],
                               tok_gating["sg"], tok_selb, tok_onesr))
                if kind == "r":
                    tok = mm(PB[b4][:, 0:512],
                             SELB[0:8, idx * 128:idx * 128 + 128],
                             WET[0:8, tt * 512:(tt + 1) * 512], True, True,
                             inc=True)
                else:
                    tok = mm(PB[b4][:, 0:512], ONESR[:],
                             SG[0:1, tt * 512:(tt + 1) * 512], True, True,
                             inc=True)
                wait("scalar", tok)
                if first_wb[0]:
                    for tk in tok_wb_region_ok:
                        wait("scalar", tk)
                    first_wb[0] = False
                wait("scalar", tok_wb_prev_uw)
                tok_wb = scopy(WB[:, tt * 512:(tt + 1) * 512],
                               PB[b4][:, 0:512])
                dn_bank_tok[tt % 4] = tok_wb
        tok_uw_last = None
        for tt in range(4):
            wait("tensor", tok_lnb_w[8 + min(2 * tt + 1, 7)])
            htok_list = []
            silu_toks = [None] * nic
            uw_toks = [None] * nic

            def emit_h(icl):
                par = icl % 2
                wait("vector", silu_toks[icl])
                wait("vector", uw_toks[icl])
                wait("vector", h_free)
                tok_h = vtt(bfv(AF_H + icl * 512, 512), bfv(AF_SGB[par], 512),
                            bfv(AF_UW[par], 512), ALU.mult)
                sgb_read[par] = tok_h
                htok_list.append(tok_h)

            for icl in range(nic):
                par = icl % 2
                bg, bu = par, 2 + par
                bank_wait(bg, (gu_bank_tok[bg],))
                wait("tensor", dma_gu_tok[u])
                tok_g = None
                for kc in range(KC):
                    o = gu_base + kc * (2 * nic * 128) + icl * 128
                    tok_g = mm(PB[bg][:, 0:512], AR[:, o:o + 128],
                               LNB[:, kc * T + tt * 512:
                                   kc * T + (tt + 1) * 512],
                               kc == 0, kc == KC - 1, inc=(kc == KC - 1))
                bank_wait(bu, (gu_bank_tok[bu],))
                tok_u = None
                for kc in range(KC):
                    o = gu_base + kc * (2 * nic * 128) + (nic + icl) * 128
                    tok_u = mm(PB[bu][:, 0:512], AR[:, o:o + 128],
                               LNB[:, kc * T + tt * 512:
                                   kc * T + (tt + 1) * 512],
                               kc == 0, kc == KC - 1, inc=(kc == KC - 1))
                wait("scalar", tok_g)
                wait("scalar", sgb_read[par])
                sgb = bfv(AF_SGB[par], 512)
                tok_silu = act(sgb, PB[bg][:, 0:512], AF.Silu)
                silu_toks[icl] = tok_silu
                gu_bank_tok[bg] = tok_silu
                wait("vector", tok_u)
                wait("vector", tok_wb)
                if icl >= 2:
                    wait("vector", htok_list[icl - 2])
                uwb = bfv(AF_UW[par], 512)
                tok_uw = vtt(uwb, PB[bu][:, 0:512],
                             WB[:, tt * 512:(tt + 1) * 512], ALU.mult)
                gu_bank_tok[bu] = tok_uw
                uw_toks[icl] = tok_uw
                tok_uw_last = tok_uw
                if icl >= 1:
                    emit_h(icl - 1)
            emit_h(nic - 1)
            dtoks = [None] * 4
            for oh in range(2):
                for oc4 in range(4):
                    oc = oh * 4 + oc4
                    bd = 4 + oc4
                    bank_wait(bd, (dn_bank_tok[oc4], dma_dn_tok[u]))
                    tok_d = None
                    for icl in range(nic):
                        wait("tensor", htok_list[icl])
                        o = dn_base + icl * 1024 + oc * 128
                        tok_d = mm(PB[bd][:, 0:512], AR[:, o:o + 128],
                                   bfv(AF_H + icl * 512, 512),
                                   icl == 0, icl == nic - 1,
                                   inc=(icl == nic - 1))
                    dtoks[oc4] = tok_d
                for oc4 in range(4):
                    oc = oh * 4 + oc4
                    wait("vector", dtoks[oc4])
                    sl = slice(oc * T + tt * 512, oc * T + (tt + 1) * 512)
                    last_evac = vtt(RES[:, sl], PB[4 + oc4][:, 0:512],
                                    RES[:, sl], ALU.add)
                    dn_bank_tok[oc4] = last_evac
            h_free = dtoks[3]
        unit_release[u] = dtoks[3]
        tok_wb_prev_uw = tok_uw_last
        if u + 2 < len(UNITS):
            emit_unit_dma(u + 2, unit_release[u])

    wait("sync", last_evac)
    tok_out = dma("sync", d_out[:], RES[:], "sOUT")
    wait("sync", tok_out)
    return _materialize(nc, ops, sems)


def _materialize(nc, ops, sems):
    names = ("sIN", "sX", "sX2", "sWQ", "sWK", "sWV", "sWO", "sW", "sGU0",
             "sGU1", "sDN0", "sDN1", "sPO", "sPE", "sSC", "sVE", "sOUT")
    import contextlib
    with contextlib.ExitStack() as stack:
        for nm in names:
            sems[nm] = stack.enter_context(nc.semaphore(nm))
        block = stack.enter_context(nc.Block())

        @block.sync
        def _(eng):
            for fn in ops["sync"]:
                fn(eng)

        @block.gpsimd
        def _(eng):
            for fn in ops["gpsimd"]:
                fn(eng)

        @block.tensor
        def _(eng):
            for fn in ops["tensor"]:
                fn(eng)

        @block.scalar
        def _(eng):
            for fn in ops["scalar"]:
                fn(eng)

        @block.vector
        def _(eng):
            for fn in ops["vector"]:
                fn(eng)

    return nc


_NC_CACHE = {}


def _get_nc():
    if "nc" not in _NC_CACHE:
        _NC_CACHE["nc"] = build_nc()
    return _NC_CACHE["nc"]


def _chunked_fm(a):
    k, o = a.shape
    return np.ascontiguousarray(
        a.reshape(k // 128, 128, o).transpose(1, 0, 2).reshape(128, -1))


def prep_in_maps(inputs):
    g = {k: np.asarray(v, dtype=np.float32) for k, v in inputs.items()}
    x = g["L0_attn_x"]
    wqkv = g["L0_attn_Wqkv"]
    wout = g["L0_attn_Wout"]
    bout = g["L0_attn_bout"]
    wl1 = g["L0_attn_Wl1"]
    bl1 = g["L0_attn_bl1"]
    wl2 = g["L0_attn_Wl2"]
    bl2 = g["L0_attn_bl2"]
    gate_w = g["L0_ffn_gate_w"]
    egu = g["L0_ffn_experts_gate_up"]
    edn = g["L0_ffn_experts_down"]
    sgw = g["L0_ffn_shared_gate_w"]
    suw = g["L0_ffn_shared_up_w"]
    sdw = g["L0_ffn_shared_down_w"]
    segw = g["L0_ffn_shared_expert_gate_w"]

    wqkvT = wqkv.T
    atnw = np.concatenate([_chunked_fm(wqkvT[:, 0:1024]),
                           _chunked_fm(wqkvT[:, 1024:2048]),
                           _chunked_fm(wqkvT[:, 2048:3072]),
                           _chunked_fm(wout.T)], axis=1)

    ffnw = np.empty((128, FFNW_TOT), dtype=np.float32)
    for u, (kind, idx, ic0, nic) in enumerate(UNITS):
        gu_off, gu_len, dn_off, dn_len = FFNW_OFFS[u]
        if kind == "r":
            guT = egu[idx].T
            gcol = guT[:, ic0 * 128:(ic0 + nic) * 128]
            ucol = guT[:, I_MOE + ic0 * 128:I_MOE + (ic0 + nic) * 128]
            drow = edn[idx].T[ic0 * 128:(ic0 + nic) * 128, :]
        else:
            gcol = sgw.T[:, ic0 * 128:(ic0 + nic) * 128]
            ucol = suw.T[:, ic0 * 128:(ic0 + nic) * 128]
            drow = sdw.T[ic0 * 128:(ic0 + nic) * 128, :]
        blk = np.concatenate([gcol, ucol], axis=1)
        ffnw[:, gu_off:gu_off + gu_len] = _chunked_fm(blk)
        ffnw[:, dn_off:dn_off + dn_len] = np.ascontiguousarray(
            drow.reshape(nic, 128, 1024).transpose(1, 0, 2).reshape(128, -1))

    gw = _chunked_fm(gate_w.T)
    seg = np.ascontiguousarray(segw.reshape(KC, 128).T)
    boutp = np.ascontiguousarray(bout.reshape(KC, 128).T)
    sel = np.zeros((8, 1024), np.float32)
    for e in range(8):
        sel[e, e * 128:(e + 1) * 128] = 1.0
    shared = dict(
        atnw=atnw, ffnw=ffnw, gw=gw, seg=seg, boutp=boutp, sel=sel,
        idn=np.eye(128, dtype=np.float32),
        ones1=np.ones((1, 128), np.float32),
        onesc=np.ones((128, 1), np.float32),
        onesf=np.ones((128, 128), np.float32),
        wl1t=np.ascontiguousarray(wl1.T),
        bl1c=np.ascontiguousarray(bl1.reshape(64, 1)),
        wl2t=np.ascontiguousarray(wl2.T),
        bl2c=np.ascontiguousarray(bl2.reshape(16, 1)))

    in_maps = []
    for b in range(NCORE):
        m = dict(shared)
        m["xt"] = np.ascontiguousarray(
            x[b].T.reshape(KC, 128, T).transpose(1, 0, 2).reshape(128, -1))
        in_maps.append(m)
    return in_maps


def kernel(**inputs):
    in_maps = prep_in_maps(inputs)
    nc = _get_nc()
    res = run_bass_kernel_spmd(nc, in_maps, core_ids=list(range(NCORE)))
    outs = []
    for b in range(NCORE):
        o = np.asarray(res.results[b]["out"])
        y = o.reshape(128, KC, T).transpose(1, 0, 2).reshape(C, T).T
        outs.append(np.ascontiguousarray(y))
    return np.stack(outs, axis=0).astype(np.float32)


# version 17
# speedup vs baseline: 1.1109x; 1.0005x over previous
import numpy as np

import concourse.bass as bass
import concourse.mybir as mybir
from concourse.bass_utils import run_bass_kernel_spmd

F32 = mybir.dt.float32
BF16 = mybir.dt.bfloat16
AF = mybir.ActivationFunctionType
ALU = mybir.AluOpType
AX = mybir.AxisListType

T = 2048
C = 1024
KC = 8
NCH = 8
CS = 256
E = 8
I_MOE = 1408
NCORE = 8

UNITS = []
for e in range(E):
    UNITS.append(("r", e, 0, 6))
    UNITS.append(("r", e, 6, 5))
for j, (ic0, nic) in enumerate([(0, 6), (6, 5), (11, 6), (17, 5)]):
    UNITS.append(("s", j, ic0, nic))

FFNW_OFFS = []
_off = 0
for (_k, _i, _ic0, _nic) in UNITS:
    gu_len = 8 * 2 * _nic * 128
    dn_len = _nic * 1024
    FFNW_OFFS.append((_off, gu_len, _off + gu_len, dn_len))
    _off += gu_len + dn_len
FFNW_TOT = _off

A_QKV = 0
A_WOUT = 24576
A_Q = 32768
A_K = 34816
A_V = 36864
A_ATT = 38912
A_EXPT = 40960
A_XSCR = 43008
A_RB = 45056
A_T = 46080
A_LCOL = 46592
A_ROWS = 47104
A_WET = 50416
A_SG = 52464
A_T2 = 54512
A_END = 55024
AF_GU = [0, 12288]
AF_DN = [24576, 30720]
AF_H = 36864
AF_SGB = [39936, 40448]
AF_UW = [40960, 41472]
AF_WB = 41984
AS_LOGP = 24576
AS_H1 = 28672
AS_E = 29696
AS_LAM = 30720

R_MEAN = 0
R_VAR = 256
R_RSTD = 512
R_MR = 768
R_RROW = 1024
R_MISC = 1536
R_NEGB = 1600
R_SMALL = 1602
R_STM = 1608
R_E8 = 1616
R_SC8 = 1624
R_GE = 1632
R_WTM = 1640
R_GWSUM = 1648
R_BL2N = 1650
ROWS_F32 = 1656


def build_nc(stop=None):
    nc = bass.Bass("TRN2", target_bir_lowering=False, debug=False)

    d_x = nc.dram_tensor("xt", [128, KC * T], F32, kind="ExternalInput")
    d_atnw = nc.dram_tensor("atnw", [128, 32768], F32, kind="ExternalInput")
    d_ffnw = nc.dram_tensor("ffnw", [128, FFNW_TOT], F32, kind="ExternalInput")
    d_gw = nc.dram_tensor("gw", [128, 64], F32, kind="ExternalInput")
    d_seg = nc.dram_tensor("seg", [128, 8], F32, kind="ExternalInput")
    d_bout = nc.dram_tensor("boutp", [128, 8], F32, kind="ExternalInput")
    d_sel = nc.dram_tensor("sel", [8, 1024], F32, kind="ExternalInput")
    d_idn = nc.dram_tensor("idn", [128, 128], F32, kind="ExternalInput")
    d_ones1 = nc.dram_tensor("ones1", [1, 128], F32, kind="ExternalInput")
    d_onesc = nc.dram_tensor("onesc", [128, 1], F32, kind="ExternalInput")
    d_onesf = nc.dram_tensor("onesf", [128, 128], F32, kind="ExternalInput")
    d_wl1t = nc.dram_tensor("wl1t", [1, 64], F32, kind="ExternalInput")
    d_bl1 = nc.dram_tensor("bl1c", [64, 1], F32, kind="ExternalInput")
    d_wl2t = nc.dram_tensor("wl2t", [64, 16], F32, kind="ExternalInput")
    d_bl2 = nc.dram_tensor("bl2c", [16, 1], F32, kind="ExternalInput")
    d_out = nc.dram_tensor("out", [128, KC * T], F32, kind="ExternalOutput")

    RES = nc.alloc_sbuf_tensor("res", [128, KC * T], F32)
    LNB = nc.alloc_sbuf_tensor("lnb", [128, KC * T], BF16)
    AR = nc.alloc_sbuf_tensor("arena", [128, A_END], BF16)
    IDN32 = nc.alloc_sbuf_tensor("idn32", [128, 128], F32)
    ONES32 = nc.alloc_sbuf_tensor("ones32", [1, 128], F32)
    ONESC32 = nc.alloc_sbuf_tensor("onesc32", [128, 1], F32)
    ONESB = nc.alloc_sbuf_tensor("onesb", [128, 1], BF16)
    ONESF = nc.alloc_sbuf_tensor("onesf_sb", [128, 128], BF16)
    ONESR = nc.alloc_sbuf_tensor("onesr", [1, 128], BF16)
    SELB = nc.alloc_sbuf_tensor("selb", [8, 1024], BF16)
    GW = nc.alloc_sbuf_tensor("gwsb", [128, 64], F32)
    SEGB = nc.alloc_sbuf_tensor("segb", [128, 8], BF16)
    BOUT = nc.alloc_sbuf_tensor("boutsb", [128, 8], F32)
    WL1T = nc.alloc_sbuf_tensor("wl1ts", [1, 64], F32)
    BL1 = nc.alloc_sbuf_tensor("bl1s", [64, 1], F32)
    WL2T = nc.alloc_sbuf_tensor("wl2ts", [64, 16], F32)
    BL2 = nc.alloc_sbuf_tensor("bl2s", [16, 1], F32)

    PB = [nc.alloc_psum_tensor(f"pb{i}", [128, 512], F32) for i in range(8)]

    def bfv(off, n, p0=0, p1=128):
        return AR[p0:p1, off:off + n]

    def f32v(off, nf, p0=0, p1=128):
        return AR[p0:p1, off:off + 2 * nf].bitcast(F32)

    def rcol(off, n=1, p=1):
        return AR[0:p, A_ROWS + 2 * off:A_ROWS + 2 * (off + n)].bitcast(F32)

    LCOL = f32v(A_LCOL, 256)
    WET = bfv(A_WET, 2048, 0, 8)
    SG = bfv(A_SG, 2048, 0, 1)
    WB = f32v(AF_WB, 2048)
    XSCR = bfv(A_XSCR, 2048)
    AT = f32v(A_T, 256)
    AT2 = f32v(A_T2, 256)
    NEGB = rcol(R_NEGB, 1, 128)
    LOGP = f32v(AS_LOGP, 2048, 0, 1)
    H1 = f32v(AS_H1, 512, 0, 64)
    EE = f32v(AS_E, 512, 0, 16)
    LAMR = f32v(AS_LAM, 512, 0, 16)

    ENGS = ("sync", "gpsimd", "tensor", "scalar", "vector")
    ops = {e: [] for e in ENGS}
    cnt = {s: 0 for s in ("sIN", "sX", "sX2", "sWQ", "sWK", "sWV", "sWO",
                          "sW", "sGU0", "sGU1", "sDN0", "sDN1", "sPO",
                          "sPE", "sSC", "sVE", "sOUT")}
    lastw = {e: {} for e in ENGS}
    sems = {}

    def emit(eng, fn, inc=None):
        if inc is None:
            ops[eng].append(fn)
            return None
        sem, k = inc
        cnt[sem] += k
        v = cnt[sem]

        def wrapped(Eng, fn=fn, sem=sem, k=k):
            fn(Eng).then_inc(sems[sem], k)
        ops[eng].append(wrapped)
        return (sem, v)

    def wait(eng, tok):
        if not tok:
            return
        sem, v = tok
        if v > cnt[sem]:
            raise RuntimeError(f"forward wait {eng} {sem} {v} > {cnt[sem]}")
        if lastw[eng].get(sem, 0) >= v:
            return
        lastw[eng][sem] = v
        ops[eng].append(lambda Eng, s=sem, v=v: Eng.wait_ge(sems[s], v))

    def mm(bank_ap, lhsT, rhs, start, stop, inc=False):
        fn = (lambda Eng, o=bank_ap, l=lhsT, r=rhs, st=start, sp=stop:
              Eng.matmul(o, l, r, start=st, stop=sp, skip_group_check=True))
        return emit("tensor", fn, inc=("sPE", 1) if inc else None)

    def tr(out_ap, in_ap, idn_ap, inc=True):
        fn = lambda Eng, o=out_ap, i=in_ap, d=idn_ap: Eng.transpose(o, i, d)
        return emit("tensor", fn, inc=("sPE", 1) if inc else None)

    def act(out_ap, in_ap, func, bias=0.0, scale=1.0, inc=True):
        fn = (lambda Eng, o=out_ap, i=in_ap, f=func, b=bias, s=scale:
              Eng.activation(o, i, f, bias=b, scale=s))
        return emit("scalar", fn, inc=("sSC", 1) if inc else None)

    def scopy(out_ap, in_ap, inc=True):
        fn = lambda Eng, o=out_ap, i=in_ap: Eng.copy(o, i)
        return emit("scalar", fn, inc=("sSC", 1) if inc else None)

    def sdrain():
        emit("scalar", lambda Eng: Eng.drain())

    def vtt(out_ap, a_ap, b_ap, op, inc=True):
        fn = (lambda Eng, o=out_ap, a=a_ap, b=b_ap, p=op:
              Eng.tensor_tensor(o, a, b, p))
        return emit("vector", fn, inc=("sVE", 1) if inc else None)

    def vts(out_ap, a_ap, scalar, op, inc=True):
        fn = (lambda Eng, o=out_ap, a=a_ap, s=scalar, p=op:
              Eng.tensor_single_scalar(o, a, s, p))
        return emit("vector", fn, inc=("sVE", 1) if inc else None)

    def vstt(out_ap, in0, scalar, in1, op0, op1, inc=True):
        fn = (lambda Eng, o=out_ap, a=in0, s=scalar, b=in1, p0=op0, p1=op1:
              Eng.scalar_tensor_tensor(o, a, s, b, p0, p1))
        return emit("vector", fn, inc=("sVE", 1) if inc else None)

    def vred(out_ap, in_ap, op, negate=False, inc=True):
        fn = (lambda Eng, o=out_ap, i=in_ap, p=op, n=negate:
              Eng.tensor_reduce(o, i, AX.X, p, negate=n))
        return emit("vector", fn, inc=("sVE", 1) if inc else None)

    def vrecip(out_ap, in_ap, inc=True):
        fn = lambda Eng, o=out_ap, i=in_ap: Eng.reciprocal(o, i)
        return emit("vector", fn, inc=("sVE", 1) if inc else None)

    def vdrain():
        emit("vector", lambda Eng: Eng.drain())

    def vcopy(out_ap, in_ap, inc=True):
        fn = lambda Eng, o=out_ap, i=in_ap: Eng.tensor_copy(o, i)
        return emit("vector", fn, inc=("sVE", 1) if inc else None)

    def pcopy(out_ap, in_ap, inc=True):
        fn = lambda Eng, o=out_ap, i=in_ap: Eng.tensor_copy(o, i)
        return emit("gpsimd", fn, inc=("sPO", 1) if inc else None)

    def dma(q, out_ap, in_ap, sem):
        fn = lambda Eng, o=out_ap, i=in_ap: Eng.dma_start(o, i)
        return emit(q, fn, inc=(sem, 16))

    bfree = [None] * 8

    def bank_wait(i, extra=()):
        wait("tensor", bfree[i])
        for t in extra:
            wait("tensor", t)

    resh = RES[:].rearrange("p (c t) -> p c t", c=KC)
    xh = d_x[:].rearrange("p (c t) -> p c t", c=KC)
    tok_xa = dma("sync", resh[:, :, 0:1024], xh[:, :, 0:1024], "sX")
    tok_xb = dma("sync", resh[:, :, 1024:2048], xh[:, :, 1024:2048], "sX2")

    def tok_x_for(ch):
        return tok_xa if ch < 4 else tok_xb
    tok_x = tok_xb
    tok_consts = None
    for (dst, src) in ((IDN32[:], d_idn[:]), (ONES32[:], d_ones1[:]),
                       (ONESC32[:], d_onesc[:]), (GW[:], d_gw[:]),
                       (BOUT[:], d_bout[:]), (WL1T[:], d_wl1t[:]),
                       (BL1[:], d_bl1[:]), (WL2T[:], d_wl2t[:]),
                       (BL2[:], d_bl2[:])):
        tok_consts = dma("sync", dst, src, "sIN")
    tok_wq = dma("gpsimd", bfv(A_QKV, 8192), d_atnw[:, 0:8192], "sWQ")
    tok_wk = dma("gpsimd", bfv(A_QKV + 8192, 8192), d_atnw[:, 8192:16384],
                 "sWK")
    tok_wv = dma("gpsimd", bfv(A_QKV + 16384, 8192), d_atnw[:, 16384:24576],
                 "sWV")
    dma("gpsimd", ONESB[:], d_onesc[:], "sW")
    dma("gpsimd", ONESR[:], d_ones1[:], "sW")
    dma("gpsimd", SELB[:], d_sel[:], "sW")
    dma("gpsimd", ONESF[:], d_onesf[:], "sW")
    tok_smalls = dma("gpsimd", SEGB[:], d_seg[:], "sW")
    tok_onesb = tok_onesr = tok_selb = tok_segb = tok_smalls

    def iota_fn(Eng, ap=LOGP):
        return Eng.iota(ap, [[1, T]], channel_multiplier=0,
                        allow_small_or_imprecise_dtypes=True)
    tok_iota = emit("gpsimd", iota_fn, inc=("sPO", 1))

    wait("vector", tok_iota)
    tok_ip1 = vts(LOGP, LOGP, 1.0, ALU.add)
    wait("scalar", tok_ip1)
    wait("scalar", tok_consts)
    tok_logp = act(LOGP, LOGP, AF.Ln)
    BL2N = rcol(R_BL2N, 1, 16)
    wait("vector", tok_consts)
    tok_bl2n = vts(BL2N, BL2[:], -1.0, ALU.mult)
    tok_relu = None
    tok_sig = None
    tok_lam_done = None
    for tt in range(4):
        wait("tensor", tok_logp)
        wait("tensor", tok_relu)
        t1 = mm(PB[0][0:64, :], WL1T[0:1, :],
                LOGP[0:1, tt * 512:(tt + 1) * 512], True, True, inc=True)
        wait("scalar", t1)
        tok_relu = act(H1, PB[0][0:64, :], AF.Relu, bias=BL1[:])
        wait("tensor", tok_relu)
        wait("tensor", tok_sig)
        t3 = mm(PB[1][0:16, :], WL2T[:], H1, True, True, inc=True)
        wait("scalar", t3)
        wait("scalar", tok_bl2n)
        tok_sig = act(EE, PB[1][0:16, :], AF.Exp, bias=BL2N, scale=-1.0)
        wait("vector", tok_sig)
        vts(EE, EE, 1.0, ALU.add, inc=False)
        vdrain()
        t6 = vrecip(LAMR, EE)
        for q in range(4):
            t0g = tt * 512 + q * 128
            n_i, kth = t0g // 256, (t0g // 128) % 2
            wait("tensor", t6)
            wait("tensor", tok_lam_done)
            t7 = tr(PB[2][0:128, 0:16], LAMR[0:16, q * 128:(q + 1) * 128],
                    IDN32[0:16, 0:16])
            wait("scalar", t7)
            co = n_i * 32 + kth * 16
            tok_lam_done = scopy(LCOL[:, co:co + 16], PB[2][0:128, 0:16])
    bfree[2] = tok_lam_done
    wait("gpsimd", tok_lam_done)
    tok_woutw = dma("gpsimd", bfv(A_WOUT, 8192), d_atnw[:, 24576:32768], "sWO")

    xscr_free = [None]
    rows_free = [None]
    b2_chain = [None]
    tok_chunk_res = [None] * NCH
    tok_qkv_done = [None] * NCH
    tok_st_done = [None] * NCH
    tok_avsum_done = [None] * NCH
    tok_wout_done = [None] * NCH
    tok_lnb_w = [None] * 16
    exp_tok4 = [[None, None], [None, None]]
    rtok2 = [None, None]
    sum_tok4 = [[None, None], [None, None]]
    avp_tok = [[None, None], [None, None]]
    evac_tok2 = [None, None]
    tok_rb_last = [None]
    tok_gating = dict(wet=None, sg=None)
    qk_bank = [0]

    def next_qk():
        b = qk_bank[0]
        qk_bank[0] = 1 - b
        return b

    ln_ctx = {}

    def ln_stats(p, res_tok):
        gating = p >= 8
        ch = p % 8
        t0 = ch * 256
        wait("gpsimd", tok_x_for(p % 8))
        wait("gpsimd", res_tok)
        if gating:
            wait("gpsimd", tok_qkv_done[ch])
        lnb3 = LNB[:].rearrange("p (c t) -> p c t", c=KC)[:, :, t0:t0 + 256]
        res3 = RES[:].rearrange("p (c t) -> p c t", c=KC)[:, :, t0:t0 + 256]
        tok_cast = pcopy(lnb3, res3)
        wait("scalar", tok_cast)
        wait("scalar", xscr_free[0])
        xs3 = XSCR[:].rearrange("p (c t) -> p c t", c=KC)
        tok_sq = act(xs3, lnb3, AF.Square)
        bs = next_qk()
        bank_wait(bs)
        wait("tensor", tok_cast)
        wait("tensor", tok_smalls)
        tok_sum = None
        for kc in range(KC):
            tok_sum = mm(PB[bs][:, 0:256], ONESF[:],
                         LNB[:, kc * T + t0:kc * T + t0 + 256],
                         kc == 0, kc == KC - 1, inc=(kc == KC - 1))
        wait("tensor", tok_sq)
        tok_sq2 = None
        for kc in range(KC):
            tok_sq2 = mm(PB[bs][:, 256:512], ONESF[:],
                         XSCR[:, kc * 256:(kc + 1) * 256],
                         kc == 0, kc == KC - 1, inc=(kc == KC - 1))
        xscr_free[0] = tok_sq2
        MEAN = rcol(R_MEAN, 256)
        VAR = rcol(R_VAR, 256)
        RSTD = rcol(R_RSTD, 256)
        MR = rcol(R_MR, 256)
        wait("vector", tok_sum)
        wait("vector", tok_sq2)
        wait("vector", rows_free[0])
        vts(MEAN, PB[bs][0:1, 0:256], 1.0 / C, ALU.mult, inc=False)
        vdrain()
        vtt(MR, MEAN, MEAN, ALU.mult, inc=False)
        vdrain()
        vstt(VAR, PB[bs][0:1, 256:512], 1.0 / C, MR, ALU.mult,
             ALU.subtract, inc=False)
        vdrain()
        tok_var = vts(VAR, VAR, 1e-5, ALU.add)
        bfree[bs] = tok_var
        wait("scalar", tok_var)
        act(VAR, VAR, AF.Ln, inc=False)
        sdrain()
        tok_rstd = act(RSTD, VAR, AF.Exp, scale=-0.5)
        wait("vector", tok_rstd)
        tok_mr = vtt(MR, MEAN, RSTD, ALU.mult)
        vdrain()
        ln_ctx[p] = (tok_var, tok_mr)

    def ln_apply(p):
        ch = p % 8
        t0 = ch * 256
        MR = rcol(R_MR, 256)
        RSTD = rcol(R_RSTD, 256)
        tok_var, tok_mr = ln_ctx[p]
        bb = next_qk()
        bank_wait(bb, (tok_var, tok_mr))
        mm(PB[bb][:, 0:256], ONES32[:], MR, True, True)
        tok_bc = mm(PB[bb][:, 256:512], ONES32[:], RSTD, True, True, inc=True)
        wait("vector", tok_bc)
        wait("vector", b2_chain[0])
        tok_ln = None
        ats = [AT, AT2]
        ttoks = [None] * KC
        lntoks = [None] * KC
        prev = None
        for kc in range(KC):
            sl = slice(kc * T + t0, kc * T + t0 + 256)
            if kc >= 2:
                wait("vector", lntoks[kc - 2])
            ttoks[kc] = vtt(ats[kc % 2], RES[:, sl], PB[bb][:, 256:512],
                            ALU.mult)
            if prev is not None:
                wait("vector", ttoks[prev])
                slp = slice(prev * T + t0, prev * T + t0 + 256)
                tok_ln = lntoks[prev] = vtt(LNB[:, slp], ats[prev % 2],
                                            PB[bb][:, 0:256], ALU.subtract)
            prev = kc
        wait("vector", ttoks[prev])
        slp = slice(prev * T + t0, prev * T + t0 + 256)
        tok_ln = lntoks[prev] = vtt(LNB[:, slp], ats[prev % 2],
                                    PB[bb][:, 0:256], ALU.subtract)
        b2_chain[0] = tok_ln
        bfree[bb] = tok_ln
        tok_lnb_w[p] = tok_ln
        if p < 8:
            rows_free[0] = tok_bc
        else:
            ln_ctx[p] = (tok_bc, tok_ln)

    gate_ctx = {}

    def ln_gate1(p):
        ch = p % 8
        t0 = ch * 256
        MEAN = rcol(R_MEAN, 256)
        RSTD = rcol(R_RSTD, 256)
        g1 = next_qk()
        bank_wait(g1)
        tok_lg = None
        for kc in range(KC):
            tok_lg = mm(PB[g1][0:8, 0:256], GW[:, kc * 8:(kc + 1) * 8],
                        RES[:, kc * T + t0:kc * T + t0 + 256],
                        kc == 0, kc == KC - 1, inc=(kc == KC - 1))
        g2 = next_qk()
        bank_wait(g2)
        mm(PB[g2][0:8, 0:256], ONES32[0:1, 0:8], MEAN, True, True)
        tok_m8 = mm(PB[g2][0:8, 256:512], ONES32[0:1, 0:8], RSTD, True, True,
                    inc=True)
        LG1 = rcol(R_VAR, 256, 8)
        LG2 = rcol(R_MR, 256, 8)
        GWS = rcol(R_GWSUM, 1, 8)
        wait("vector", tok_m8)
        wait("vector", tok_lg)
        vts(LG1, PB[g2][0:8, 0:256], GWS, ALU.mult, inc=False)
        vdrain()
        vstt(LG2, PB[g1][0:8, 0:256], 1.0, LG1, ALU.mult, ALU.subtract,
             inc=False)
        vdrain()
        tok_l3 = vtt(LG1, LG2, PB[g2][0:8, 256:512], ALU.mult)
        vdrain()
        bfree[g2] = tok_l3
        bfree[g1] = tok_l3
        gate_ctx[p] = dict(tok_l3=tok_l3, tok_wet=None)

    def ln_gate_j(p, j):
        ch = p % 8
        t0 = ch * 256
        gc = gate_ctx[p]
        g1 = next_qk()
        bank_wait(g1)
        LG1 = rcol(R_VAR, 256, 8)
        STM = rcol(R_STM, 8, 128)
        E8 = rcol(R_E8, 8, 128)
        SC8 = rcol(R_SC8, 8, 128)
        GE8 = rcol(R_GE, 8, 128)
        WTM = rcol(R_WTM, 8, 128)
        NM = rcol(R_SMALL, 1, 128)
        SM = rcol(R_SMALL + 1, 1, 128)
        RC = rcol(R_SMALL + 2, 1, 128)
        M1 = rcol(R_SMALL + 3, 1, 128)
        M2 = rcol(R_SMALL + 4, 1, 128)
        wait("tensor", gc["tok_l3"])
        if gc["tok_wet"]:
            wait("tensor", gc["tok_wet"])
        tok_tr = tr(PB[g1][0:128, 0:8], LG1[0:8, j * 128:(j + 1) * 128],
                    IDN32[0:8, 0:8])
        wait("scalar", tok_tr)
        tok_stm = scopy(STM, PB[g1][0:128, 0:8])
        wait("vector", tok_stm)
        tok_nm = vred(NM, STM, ALU.max, negate=True)
        wait("scalar", tok_nm)
        tok_e8 = act(E8, STM, AF.Exp, bias=NM)
        wait("vector", tok_e8)
        vred(SM, E8, ALU.add, inc=False)
        vdrain()
        vrecip(RC, SM, inc=False)
        vdrain()
        vts(SC8, E8, RC, ALU.mult, inc=False)
        vdrain()
        vred(M1, SC8, ALU.max, inc=False)
        vdrain()
        vts(GE8, SC8, M1, ALU.is_ge, inc=False)
        vdrain()
        vstt(GE8, GE8, -1e30, SC8, ALU.mult, ALU.add, inc=False)
        vdrain()
        vred(M2, GE8, ALU.max, inc=False)
        vdrain()
        vts(GE8, SC8, M2, ALU.is_ge, inc=False)
        vdrain()
        tok_w = vtt(WTM, SC8, GE8, ALU.mult)
        wait("tensor", tok_w)
        tok_trb = tr(PB[g1][0:8, 0:128], WTM, IDN32[:])
        wait("scalar", tok_trb)
        gc["tok_wet"] = scopy(WET[0:8, t0 + j * 128:t0 + j * 128 + 128],
                              PB[g1][0:8, 0:128])
        bfree[g1] = gc["tok_wet"]

    def ln_gate_sg(p):
        ch = p % 8
        t0 = ch * 256
        gc = gate_ctx[p]
        g1 = next_qk()
        bank_wait(g1)
        tok_bc, tok_ln = ln_ctx[p]
        tok_gating["wet"] = gc["tok_wet"]
        wait("tensor", gc["tok_wet"])
        wait("tensor", tok_ln)
        wait("tensor", tok_segb)
        tok_sg = None
        for kc in range(KC):
            tok_sg = mm(PB[g1][0:1, 0:256], SEGB[:, kc:kc + 1],
                        LNB[:, kc * T + t0:kc * T + t0 + 256],
                        kc == 0, kc == KC - 1, inc=(kc == KC - 1))
        SGR = rcol(R_VAR, 256)
        wait("scalar", tok_sg)
        tok_sge = act(SGR, PB[g1][0:1, 0:256], AF.Exp, scale=-1.0)
        bfree[g1] = tok_sge
        wait("vector", tok_sge)
        vts(SGR, SGR, 1.0, ALU.add, inc=False)
        vdrain()
        SGR2 = rcol(R_RSTD, 256)
        vrecip(SGR2, SGR, inc=False)
        vdrain()
        tok_sgf = vcopy(SG[0:1, t0:t0 + 256], SGR2)
        tok_gating["sg"] = tok_sgf
        rows_free[0] = tok_sgf

    def attn_qkv_bound(n):
        t0 = n * CS
        wait("tensor", tok_lnb_w[n])
        QMX = rcol(R_MISC, 16)
        tok_qk_evac = None
        tok_v_evac = None
        tok_bnd_mm = None
        for which, dst_off, scale, poff in ((0, A_Q, 0.125, 0),
                                            (1, A_K, 1.0, 4)):
            ev_toks = [None] * 8
            sq_toks = [None] * 8

            def emit_sq(oc, which=which, dst_off=dst_off, ev_toks=None,
                        sq_toks=None):
                wait("scalar", ev_toks[oc])
                wait("scalar", xscr_free[0])
                sq_toks[oc] = act(XSCR[:, oc * 256:(oc + 1) * 256],
                                  bfv(dst_off, 2048)[:, oc * 256:
                                                     (oc + 1) * 256],
                                  AF.Square)

            wait("tensor", tok_wq if which == 0 else tok_wk)
            for oc in range(8):
                b = next_qk()
                bank_wait(b)
                tok = None
                for kc in range(KC):
                    o = A_QKV + which * 8192 + kc * 1024 + oc * 128
                    tok = mm(PB[b][:, 0:256], AR[:, o:o + 128],
                             LNB[:, kc * T + t0:kc * T + t0 + 256],
                             kc == 0, kc == KC - 1, inc=(kc == KC - 1))
                wait("scalar", tok)
                if n > 0:
                    wait("scalar", tok_st_done[n - 1])
                dst = bfv(dst_off, 2048)[:, oc * 256:(oc + 1) * 256]
                if scale != 1.0:
                    ev_toks[oc] = act(dst, PB[b][:, 0:256], AF.Copy,
                                      scale=scale)
                else:
                    ev_toks[oc] = scopy(dst, PB[b][:, 0:256])
                tok_qk_evac = ev_toks[oc]
                bfree[b] = ev_toks[oc]
                if oc >= 1:
                    emit_sq(oc - 1, ev_toks=ev_toks, sq_toks=sq_toks)
            emit_sq(7, ev_toks=ev_toks, sq_toks=sq_toks)
            for tt4 in range(4):
                b = next_qk()
                bank_wait(b, (sq_toks[2 * tt4], sq_toks[2 * tt4 + 1]))
                tok_bnd_mm = mm(PB[b][:, 0:512], ONESF[:],
                                XSCR[:, tt4 * 512:(tt4 + 1) * 512],
                                True, True, inc=True)
                wait("vector", tok_bnd_mm)
                tok_r = vred(QMX[0:1, poff + tt4:poff + tt4 + 1],
                             PB[b][0:1, 0:512], ALU.max)
                bfree[b] = tok_r
            xscr_free[0] = tok_bnd_mm
        wait("tensor", tok_wv)
        for th in range(2):
            for dv2 in range(2):
                b = next_qk()
                bank_wait(b)
                tok = None
                for kc in range(KC):
                    o = A_QKV + 16384 + kc * 1024 + dv2 * 512
                    tok = mm(PB[b][:, 0:512],
                             LNB[:, kc * T + t0 + th * 128:
                                 kc * T + t0 + th * 128 + 128],
                             AR[:, o:o + 512],
                             kc == 0, kc == KC - 1, inc=(kc == KC - 1))
                wait("scalar", tok)
                if n > 0:
                    wait("scalar", tok_avsum_done[n - 1])
                tok_v_evac = scopy(
                    bfv(A_V, 2048)[:, th * 1024 + dv2 * 512:
                                   th * 1024 + dv2 * 512 + 512],
                    PB[b][:, 0:512])
                bfree[b] = tok_v_evac
        tok_qkv_done[n] = tok
        vdrain()
        vred(QMX[0:1, 8:9], QMX[0:1, 0:4], ALU.max, inc=False)
        vred(QMX[0:1, 9:10], QMX[0:1, 4:8], ALU.max, inc=False)
        vdrain()
        tok_p = vtt(QMX[0:1, 10:11], QMX[0:1, 8:9], QMX[0:1, 9:10], ALU.mult)
        wait("scalar", tok_p)
        act(QMX[0:1, 11:12], QMX[0:1, 10:11], AF.Ln, inc=False)
        sdrain()
        tok_b = act(QMX[0:1, 12:13], QMX[0:1, 11:12], AF.Exp, scale=0.5)
        wait("vector", tok_b)
        tok_nb = vts(QMX[0:1, 13:14], QMX[0:1, 12:13], -1.0, ALU.mult)
        b = next_qk()
        bank_wait(b, (tok_nb,))
        tok = mm(PB[b][:, 0:1], ONES32[:], QMX[0:1, 13:14], True, True,
                 inc=True)
        wait("scalar", tok)
        for pp in (0, 1):
            for hp in (0, 1):
                wait("scalar", exp_tok4[pp][hp])
        tok_negb = scopy(NEGB, PB[b][:, 0:1])
        sdrain()
        bfree[b] = tok_negb
        return tok_qk_evac, tok_v_evac, tok_negb

    QN = bfv(A_Q, 2048)
    KN = bfv(A_K, 2048)
    VN = bfv(A_V, 2048)
    ATT = bfv(A_ATT, 2048)

    def st_pair(n, c, tok_qk, tok_negb):
        pp = c % 2
        for hp in (0, 1):
            h = 2 * c + hp
            bi = 2 + pp * 2 + hp
            bank_wait(bi, (exp_tok4[pp][hp], tok_qk))
            lh = KN[hp * 64:hp * 64 + 64, c * 256:c * 256 + 128]
            lh2 = KN[hp * 64:hp * 64 + 64, c * 256 + 128:c * 256 + 256]
            rh = QN[hp * 64:hp * 64 + 64, c * 256:(c + 1) * 256]
            mm(PB[bi][:, 0:256], lh, rh, True, True)
            tok_st = mm(PB[bi][:, 256:512], lh2, rh, True, True, inc=True)
            tok_st_done[n] = tok_st
            wait("scalar", tok_st)
            wait("scalar", tok_negb)
            if avp_tok[pp][hp]:
                wait("scalar", avp_tok[pp][hp])
            ex = bfv(A_EXPT + pp * 1024 + hp * 512, 512)
            tok_exp = None
            for kth in range(2):
                lc = LCOL[:, n * 32 + kth * 16 + h:n * 32 + kth * 16 + h + 1]
                tok_exp = act(ex[:, kth * 256:(kth + 1) * 256],
                              PB[bi][:, kth * 256:(kth + 1) * 256],
                              AF.Exp, bias=NEGB, scale=lc)
            exp_tok4[pp][hp] = tok_exp

    def sums_av_pair(n, p, tok_v):
        pp = p % 2
        bs0 = next_qk()
        bank_wait(bs0)
        for hp in (0, 1):
            ex = bfv(A_EXPT + pp * 1024 + hp * 512, 512)
            wait("tensor", exp_tok4[pp][hp])
            co = hp * 256
            mm(PB[bs0][:, co:co + 256], ONESF[:], ex[:, 0:256], True, False)
            sum_tok4[pp][hp] = mm(PB[bs0][:, co:co + 256], ONESF[:],
                                  ex[:, 256:512], False, True, inc=True)
            bav = 6 + pp
            bank_wait(bav, (evac_tok2[pp], tok_v))
            mm(PB[bav][:, co:co + 256], VN[:, p * 128:p * 128 + 128],
               ex[:, 0:256], True, False)
            tok_av = mm(PB[bav][:, co:co + 256],
                        VN[:, 1024 + p * 128:1024 + p * 128 + 128],
                        ex[:, 256:512], False, True, inc=True)
            avp_tok[pp][hp] = tok_av
        tok_avsum_done[n] = tok_av
        RR = rcol(R_RROW, 512)
        RRB = AR[0:1, A_ROWS + 2 * R_RROW:A_ROWS + 2 * R_RROW + 512]
        wait("scalar", sum_tok4[pp][1])
        wait("scalar", tok_rb_last[0])
        tok_lnr = act(RR, PB[bs0][0:1, 0:512], AF.Ln)
        bfree[bs0] = tok_lnr
        wait("scalar", tok_lnr)
        rtok2[pp] = act(RRB, RR, AF.Exp, scale=-1.0)

    def rb_evac_pair(n, p):
        pp = p % 2
        RRB = AR[0:1, A_ROWS + 2 * R_RROW:A_ROWS + 2 * R_RROW + 512]
        brb = next_qk()
        bank_wait(brb, (rtok2[pp],))
        mm(PB[brb][0:64, 0:256], ONESR[0:1, 0:64], RRB[0:1, 0:256],
           True, True)
        tok_rb = mm(PB[brb][64:128, 0:256], ONESR[0:1, 0:64],
                    RRB[0:1, 256:512], True, True, inc=True)
        tok_rb_last[0] = tok_rb
        wait("vector", tok_rb)
        wait("vector", evac_tok2[pp])
        rbs = f32v(A_RB + pp * 512, 256)
        tok_rbc = vcopy(rbs, PB[brb][:, 0:256])
        bfree[brb] = tok_rbc
        vdrain()
        wait("vector", avp_tok[pp][0])
        wait("vector", avp_tok[pp][1])
        if n > 0:
            wait("vector", tok_wout_done[n - 1])
        bav = 6 + pp
        vtt(ATT[0:64, p * 256:(p + 1) * 256], PB[bav][0:64, 0:256],
            rbs[0:64, :], ALU.mult, inc=False)
        tok_ev = vtt(ATT[64:128, p * 256:(p + 1) * 256],
                     PB[bav][64:128, 256:512], rbs[64:128, :], ALU.mult)
        evac_tok2[pp] = tok_ev
        return tok_ev

    def attn_heads(n, tok_negb, tok_qk, tok_v, filler=None):
        tok_ev = None
        for c in range(10):
            if 2 <= c <= 9:
                tok_ev = rb_evac_pair(n, c - 2)
            if 1 <= c <= 8:
                sums_av_pair(n, c - 1, tok_v)
            if c <= 7:
                st_pair(n, c, tok_qk, tok_negb)
            if filler is not None:
                next(filler, None)
        return tok_ev

    def attn_wout(n, tok_ev_last):
        t0 = n * CS
        wait("tensor", tok_ev_last)
        wait("tensor", tok_woutw)
        tok_evw = None
        for oc in range(8):
            b = next_qk()
            bank_wait(b)
            tok = None
            for kc in range(KC):
                o = A_WOUT + kc * 1024 + oc * 128
                tok = mm(PB[b][:, 0:256], AR[:, o:o + 128],
                         ATT[:, kc * 256:(kc + 1) * 256],
                         kc == 0, kc == KC - 1, inc=(kc == KC - 1))
            tok_wout_done[n] = tok
            wait("vector", tok)
            wait("vector", tok_xa if n < 4 else tok_xb)
            sl = slice(oc * T + t0, oc * T + t0 + 256)
            tok_evw = vstt(RES[:, sl], PB[b][:, 0:256], BOUT[:, oc:oc + 1],
                           RES[:, sl], ALU.add, ALU.add)
            bfree[b] = tok_evw
        tok_chunk_res[n] = tok_evw

    def emit_gwsum():
        b = next_qk()
        bank_wait(b)
        wait("tensor", tok_consts)
        tok = None
        for kc in range(KC):
            tok = mm(PB[b][0:8, 0:1], GW[:, kc * 8:(kc + 1) * 8], ONESC32[:],
                     kc == 0, kc == KC - 1, inc=(kc == KC - 1))
        wait("vector", tok)
        tok_g = vcopy(rcol(R_GWSUM, 1, 8), PB[b][0:8, 0:1])
        vdrain()
        bfree[b] = tok_g

    emit_gwsum()
    for p in range(4):
        ln_stats(p, tok_xa)
        ln_apply(p)

    def ln_full(p, res_tok):
        ln_stats(p, res_tok)
        ln_apply(p)
        if p >= 8:
            ln_gate1(p)
            ln_gate_j(p, 0)
            ln_gate_j(p, 1)
            ln_gate_sg(p)

    def stats2_filler(n):
        for c in range(10):
            if c == 3 and n >= 1:
                ln_stats(8 + n - 1, tok_chunk_res[n - 1])
            yield

    for n in range(NCH):
        tok_qk, tok_v, tok_negb = attn_qkv_bound(n)
        if n >= 1 and n + 3 <= 7:
            ln_apply(n + 3)
        tok_ev_last = attn_heads(n, tok_negb, tok_qk, tok_v,
                                 filler=stats2_filler(n))
        if n >= 1:
            p = 8 + n - 1
            ln_apply(p)
            ln_gate1(p)
            ln_gate_j(p, 0)
            ln_gate_j(p, 1)
            ln_gate_sg(p)
        if n + 4 <= 7:
            ln_stats(n + 4, tok_xb)
        attn_wout(n, tok_ev_last)
    ln_stats(15, tok_chunk_res[7])
    ln_apply(15)
    ln_gate1(15)
    ln_gate_j(15, 0)
    ln_gate_j(15, 1)
    ln_gate_sg(15)

    for pp in (0, 1):
        for hp in (0, 1):
            wait("tensor", exp_tok4[pp][hp])
        wait("tensor", rtok2[pp])
        wait("tensor", evac_tok2[pp])

    if stop == "attn":
        dbg_outs = {
            "d_res": (RES[:], [128, KC * T], F32, tok_chunk_res[7]),
            "d_lnb": (LNB[:], [128, KC * T], BF16, tok_lnb_w[15]),
            "d_lcol": (LCOL, [128, 256], F32, tok_lnb_w[15]),
            "d_q7": (bfv(A_Q, 2048), [128, 2048], BF16, tok_lnb_w[15]),
            "d_k7": (bfv(A_K, 2048), [128, 2048], BF16, tok_lnb_w[15]),
            "d_v7": (bfv(A_V, 2048), [128, 2048], BF16, tok_lnb_w[15]),
            "d_att7": (bfv(A_ATT, 2048), [128, 2048], BF16, tok_lnb_w[15]),
            "d_wet": (WET, [8, 2048], BF16, tok_gating["wet"]),
            "d_sg": (SG, [1, 2048], BF16, tok_gating["sg"]),
            "d_negb": (NEGB, [128, 1], F32, tok_lnb_w[15]),
        }
        ndma = 0
        for name, (ap, shape, dt_, tok) in dbg_outs.items():
            dd = nc.dram_tensor(name, shape, dt_, kind="ExternalOutput")
            wait("sync", tok)
            wait("sync", tok_gating["sg"])
            dma("sync", dd[:], ap, "sOUT")
            ndma += 1
        wait("sync", ("sOUT", 16 * ndma))
        return _materialize(nc, ops, sems)

    dma_gu_tok = [None] * len(UNITS)
    dma_dn_tok = [None] * len(UNITS)

    def emit_unit_dma(u, pe_release_tok):
        slot = u % 2
        gu_off, gu_len, dn_off, dn_len = FFNW_OFFS[u]
        wait("gpsimd", pe_release_tok)
        dma_gu_tok[u] = dma("gpsimd", bfv(AF_GU[slot], gu_len),
                            d_ffnw[:, gu_off:gu_off + gu_len], f"sGU{slot}")
        dma_dn_tok[u] = dma("gpsimd", bfv(AF_DN[slot], dn_len),
                            d_ffnw[:, dn_off:dn_off + dn_len], f"sDN{slot}")

    wait("gpsimd", tok_qkv_done[7])
    wait("gpsimd", tok_wout_done[7])
    wait("gpsimd", tok_st_done[7])
    emit_unit_dma(0, None)
    emit_unit_dma(1, None)

    unit_release = [None] * len(UNITS)
    tok_wb = None
    tok_wb_prev_uw = None
    h_free = None
    sgb_read = [None, None]
    gu_bank_tok = [None, None, None, None]
    dn_bank_tok = [None, None, None, None]
    last_evac = None
    tok_wb_region_ok = (xscr_free[0], evac_tok2[0], evac_tok2[1], b2_chain[0])
    first_wb = [True]

    for u, (kind, idx, ic0, nic) in enumerate(UNITS):
        slot = u % 2
        gu_base = AF_GU[slot]
        dn_base = AF_DN[slot]
        build_wb = (kind == "r" and ic0 == 0) or (kind == "s" and idx == 0)
        if build_wb:
            for tt in range(4):
                b4 = 4 + tt % 4
                bank_wait(b4, (dn_bank_tok[tt % 4], tok_gating["wet"],
                               tok_gating["sg"], tok_selb, tok_onesr))
                if kind == "r":
                    tok = mm(PB[b4][:, 0:512],
                             SELB[0:8, idx * 128:idx * 128 + 128],
                             WET[0:8, tt * 512:(tt + 1) * 512], True, True,
                             inc=True)
                else:
                    tok = mm(PB[b4][:, 0:512], ONESR[:],
                             SG[0:1, tt * 512:(tt + 1) * 512], True, True,
                             inc=True)
                wait("scalar", tok)
                if first_wb[0]:
                    for tk in tok_wb_region_ok:
                        wait("scalar", tk)
                    first_wb[0] = False
                wait("scalar", tok_wb_prev_uw)
                tok_wb = scopy(WB[:, tt * 512:(tt + 1) * 512],
                               PB[b4][:, 0:512])
                dn_bank_tok[tt % 4] = tok_wb
        tok_uw_last = None
        for tt in range(4):
            wait("tensor", tok_lnb_w[8 + min(2 * tt + 1, 7)])
            htok_list = []
            silu_toks = [None] * nic
            uw_toks = [None] * nic

            def emit_h(icl):
                par = icl % 2
                wait("vector", silu_toks[icl])
                wait("vector", uw_toks[icl])
                wait("vector", h_free)
                tok_h = vtt(bfv(AF_H + icl * 512, 512), bfv(AF_SGB[par], 512),
                            bfv(AF_UW[par], 512), ALU.mult)
                sgb_read[par] = tok_h
                htok_list.append(tok_h)

            for icl in range(nic):
                par = icl % 2
                bg, bu = par, 2 + par
                bank_wait(bg, (gu_bank_tok[bg],))
                wait("tensor", dma_gu_tok[u])
                tok_g = None
                for kc in range(KC):
                    o = gu_base + kc * (2 * nic * 128) + icl * 128
                    tok_g = mm(PB[bg][:, 0:512], AR[:, o:o + 128],
                               LNB[:, kc * T + tt * 512:
                                   kc * T + (tt + 1) * 512],
                               kc == 0, kc == KC - 1, inc=(kc == KC - 1))
                bank_wait(bu, (gu_bank_tok[bu],))
                tok_u = None
                for kc in range(KC):
                    o = gu_base + kc * (2 * nic * 128) + (nic + icl) * 128
                    tok_u = mm(PB[bu][:, 0:512], AR[:, o:o + 128],
                               LNB[:, kc * T + tt * 512:
                                   kc * T + (tt + 1) * 512],
                               kc == 0, kc == KC - 1, inc=(kc == KC - 1))
                wait("scalar", tok_g)
                wait("scalar", sgb_read[par])
                sgb = bfv(AF_SGB[par], 512)
                tok_silu = act(sgb, PB[bg][:, 0:512], AF.Silu)
                silu_toks[icl] = tok_silu
                gu_bank_tok[bg] = tok_silu
                wait("vector", tok_u)
                wait("vector", tok_wb)
                if icl >= 2:
                    wait("vector", htok_list[icl - 2])
                uwb = bfv(AF_UW[par], 512)
                tok_uw = vtt(uwb, PB[bu][:, 0:512],
                             WB[:, tt * 512:(tt + 1) * 512], ALU.mult)
                gu_bank_tok[bu] = tok_uw
                uw_toks[icl] = tok_uw
                tok_uw_last = tok_uw
                if icl >= 1:
                    emit_h(icl - 1)
            emit_h(nic - 1)
            dtoks = [None] * 4
            for oh in range(2):
                for oc4 in range(4):
                    oc = oh * 4 + oc4
                    bd = 4 + oc4
                    bank_wait(bd, (dn_bank_tok[oc4], dma_dn_tok[u]))
                    tok_d = None
                    for icl in range(nic):
                        wait("tensor", htok_list[icl])
                        o = dn_base + icl * 1024 + oc * 128
                        tok_d = mm(PB[bd][:, 0:512], AR[:, o:o + 128],
                                   bfv(AF_H + icl * 512, 512),
                                   icl == 0, icl == nic - 1,
                                   inc=(icl == nic - 1))
                    dtoks[oc4] = tok_d
                for oc4 in range(4):
                    oc = oh * 4 + oc4
                    wait("vector", dtoks[oc4])
                    sl = slice(oc * T + tt * 512, oc * T + (tt + 1) * 512)
                    last_evac = vtt(RES[:, sl], PB[4 + oc4][:, 0:512],
                                    RES[:, sl], ALU.add)
                    dn_bank_tok[oc4] = last_evac
            h_free = dtoks[3]
        unit_release[u] = dtoks[3]
        tok_wb_prev_uw = tok_uw_last
        if u + 2 < len(UNITS):
            emit_unit_dma(u + 2, unit_release[u])

    wait("sync", last_evac)
    tok_out = dma("sync", d_out[:], RES[:], "sOUT")
    wait("sync", tok_out)
    return _materialize(nc, ops, sems)


def _materialize(nc, ops, sems):
    names = ("sIN", "sX", "sX2", "sWQ", "sWK", "sWV", "sWO", "sW", "sGU0",
             "sGU1", "sDN0", "sDN1", "sPO", "sPE", "sSC", "sVE", "sOUT")
    import contextlib
    with contextlib.ExitStack() as stack:
        for nm in names:
            sems[nm] = stack.enter_context(nc.semaphore(nm))
        block = stack.enter_context(nc.Block())

        @block.sync
        def _(eng):
            for fn in ops["sync"]:
                fn(eng)

        @block.gpsimd
        def _(eng):
            for fn in ops["gpsimd"]:
                fn(eng)

        @block.tensor
        def _(eng):
            for fn in ops["tensor"]:
                fn(eng)

        @block.scalar
        def _(eng):
            for fn in ops["scalar"]:
                fn(eng)

        @block.vector
        def _(eng):
            for fn in ops["vector"]:
                fn(eng)

    return nc


_NC_CACHE = {}


def _get_nc():
    if "nc" not in _NC_CACHE:
        _NC_CACHE["nc"] = build_nc()
    return _NC_CACHE["nc"]


def _chunked_fm(a):
    k, o = a.shape
    return np.ascontiguousarray(
        a.reshape(k // 128, 128, o).transpose(1, 0, 2).reshape(128, -1))


def prep_in_maps(inputs):
    g = {k: np.asarray(v, dtype=np.float32) for k, v in inputs.items()}
    x = g["L0_attn_x"]
    wqkv = g["L0_attn_Wqkv"]
    wout = g["L0_attn_Wout"]
    bout = g["L0_attn_bout"]
    wl1 = g["L0_attn_Wl1"]
    bl1 = g["L0_attn_bl1"]
    wl2 = g["L0_attn_Wl2"]
    bl2 = g["L0_attn_bl2"]
    gate_w = g["L0_ffn_gate_w"]
    egu = g["L0_ffn_experts_gate_up"]
    edn = g["L0_ffn_experts_down"]
    sgw = g["L0_ffn_shared_gate_w"]
    suw = g["L0_ffn_shared_up_w"]
    sdw = g["L0_ffn_shared_down_w"]
    segw = g["L0_ffn_shared_expert_gate_w"]

    wqkvT = wqkv.T
    atnw = np.concatenate([_chunked_fm(wqkvT[:, 0:1024]),
                           _chunked_fm(wqkvT[:, 1024:2048]),
                           _chunked_fm(wqkvT[:, 2048:3072]),
                           _chunked_fm(wout.T)], axis=1)

    ffnw = np.empty((128, FFNW_TOT), dtype=np.float32)
    for u, (kind, idx, ic0, nic) in enumerate(UNITS):
        gu_off, gu_len, dn_off, dn_len = FFNW_OFFS[u]
        if kind == "r":
            guT = egu[idx].T
            gcol = guT[:, ic0 * 128:(ic0 + nic) * 128]
            ucol = guT[:, I_MOE + ic0 * 128:I_MOE + (ic0 + nic) * 128]
            drow = edn[idx].T[ic0 * 128:(ic0 + nic) * 128, :]
        else:
            gcol = sgw.T[:, ic0 * 128:(ic0 + nic) * 128]
            ucol = suw.T[:, ic0 * 128:(ic0 + nic) * 128]
            drow = sdw.T[ic0 * 128:(ic0 + nic) * 128, :]
        blk = np.concatenate([gcol, ucol], axis=1)
        ffnw[:, gu_off:gu_off + gu_len] = _chunked_fm(blk)
        ffnw[:, dn_off:dn_off + dn_len] = np.ascontiguousarray(
            drow.reshape(nic, 128, 1024).transpose(1, 0, 2).reshape(128, -1))

    gw = _chunked_fm(gate_w.T)
    seg = np.ascontiguousarray(segw.reshape(KC, 128).T)
    boutp = np.ascontiguousarray(bout.reshape(KC, 128).T)
    sel = np.zeros((8, 1024), np.float32)
    for e in range(8):
        sel[e, e * 128:(e + 1) * 128] = 1.0
    shared = dict(
        atnw=atnw, ffnw=ffnw, gw=gw, seg=seg, boutp=boutp, sel=sel,
        idn=np.eye(128, dtype=np.float32),
        ones1=np.ones((1, 128), np.float32),
        onesc=np.ones((128, 1), np.float32),
        onesf=np.ones((128, 128), np.float32),
        wl1t=np.ascontiguousarray(wl1.T),
        bl1c=np.ascontiguousarray(bl1.reshape(64, 1)),
        wl2t=np.ascontiguousarray(wl2.T),
        bl2c=np.ascontiguousarray(bl2.reshape(16, 1)))

    in_maps = []
    for b in range(NCORE):
        m = dict(shared)
        m["xt"] = np.ascontiguousarray(
            x[b].T.reshape(KC, 128, T).transpose(1, 0, 2).reshape(128, -1))
        in_maps.append(m)
    return in_maps


def kernel(**inputs):
    in_maps = prep_in_maps(inputs)
    nc = _get_nc()
    res = run_bass_kernel_spmd(nc, in_maps, core_ids=list(range(NCORE)))
    outs = []
    for b in range(NCORE):
        o = np.asarray(res.results[b]["out"])
        y = o.reshape(128, KC, T).transpose(1, 0, 2).reshape(C, T).T
        outs.append(np.ascontiguousarray(y))
    return np.stack(outs, axis=0).astype(np.float32)
